# version 06
# speedup vs baseline: 1.0184x; 1.0065x over previous
import numpy as np
import ml_dtypes

import concourse.bass as bass
import concourse.mybir as mybir
from concourse.bass_core_rust import IndirectOffsetOnAxis
from concourse.bass_utils import run_bass_kernel_spmd

F32 = mybir.dt.float32
BF16 = mybir.dt.bfloat16
I32 = mybir.dt.int32
AF = mybir.ActivationFunctionType
ALU = mybir.AluOpType
AX = mybir.AxisListType
bfdt = ml_dtypes.bfloat16

NCORE = 8
N = 65536
NT = 8192
D = 1024
EXP = 8
FF = 4096
KSEL = 8192
GAMMA = 0.99
EPS = 1e-5
KCAP = 34
SLOTC = 48
NSLOT = EXP * SLOTC
CO_CH = 1040
CO_ROWS = 8 * CO_CH
ZROW = 8200
NBISECT = 30
NGRP = 16
RG = [list(range(NCORE))]


class Planner:
    def __init__(self, nc):
        self.nc = nc
        self.ops = []
        self.cnt = {}

    def tok(self, sem):
        if sem not in self.cnt:
            hits = [k for k in self.cnt if k.startswith(sem + "@")]
            if len(hits) == 1:
                sem = hits[0]
            elif len(hits) > 1:
                raise KeyError(f"ambiguous sem {sem}: {hits}")
        return (sem, self.cnt.get(sem, 0))

    def op(self, eng, fn, waits=(), sem=None, dma=False, n=None):
        w = [t for t in waits if t is not None and t[1] > 0]
        incs = []
        token = None
        if sem is not None:
            sem = sem + "@" + eng
            amt = n if n is not None else (16 if dma else 1)
            self.cnt[sem] = self.cnt.get(sem, 0) + amt
            incs.append((sem, amt))
            token = (sem, self.cnt[sem])
        self.ops.append((eng, fn, w, incs))
        return token

    def gate_engines(self, token):
        pend = {e: True for e in ("sync", "vector", "scalar", "tensor", "gpsimd")}
        self._gates = getattr(self, "_gates", [])
        self._gates.append((len(self.ops), token, pend))

    def finalize_gates(self):
        for start, token, pend in getattr(self, "_gates", []):
            for idx in range(start, len(self.ops)):
                eng, fn, w, incs = self.ops[idx]
                if pend.get(eng):
                    w.append(token)
                    pend[eng] = False

    def replay(self, block, handles):
        def runner(myname, engine):
            for eng, fn, waits, incs in self.ops:
                if eng != myname:
                    continue
                for (sname, v) in waits:
                    if v > 0:
                        engine.wait_ge(handles[sname], v)
                inst = fn(engine)
                for (sname, a) in incs:
                    inst.then_inc(handles[sname], a)

        @block.sync
        def _(e):
            runner("sync", e)

        @block.vector
        def _(e):
            runner("vector", e)

        @block.scalar
        def _(e):
            runner("scalar", e)

        @block.tensor
        def _(e):
            runner("tensor", e)

        @block.gpsimd
        def _(e):
            runner("gpsimd", e)


def v3e(ap):
    return ap.rearrange("p (a e) -> p a e", e=8)


def emit_mod(P, s, dst, src, m, iscr, waits, sem):
    inv = 1.0 / float(m)
    tk = P.op("vector", lambda e: e.tensor_scalar(dst, src, inv, 0.5 * inv - 0.5, op0=ALU.mult, op1=ALU.add),
              waits=waits, sem=sem)
    tk = P.op("vector", lambda e: e.tensor_copy(iscr, dst), waits=[tk], sem=sem)
    tk = P.op("vector", lambda e: e.tensor_copy(dst, iscr), waits=[tk], sem=sem)
    tk = P.op("vector", lambda e: e.scalar_tensor_tensor(dst, dst, -float(m), src, op0=ALU.mult, op1=ALU.add),
              waits=[tk], sem=sem)
    return tk


def emit_ffn(P, s, Xin, Xout):
    t = "F_"
    SD, SC, SCC, SMM, SEV = t + "d", t + "c", t + "cc", t + "mm", t + "ev"

    tk_cg = P.op("sync", lambda e: e.dma_start(s["gid"][:, :], s["c_gid"][:, :]), sem=SD, dma=True)
    tk_cz = P.op("sync", lambda e: e.dma_start(s["zrow"][:, :], s["c_zrow"][:, :]), sem=SD, dma=True)
    tk_cp = P.op("sync", lambda e: e.dma_start(s["piog"][:, :], s["c_piog"][:, :]), sem=SD, dma=True)
    s["_tmp"]["crel"] = (tk_cg, tk_cz, tk_cp)

    ln_chunk_toks = []
    probs_done = []
    for i in range(64):
        b = i % 2
        co = 8 * b
        c = s["cols"]
        tk_x = P.op("sync", lambda e, i=i, b=b: e.dma_start(s["xbuf"][b][:, :], Xin[i * 128:(i + 1) * 128, :]),
                    waits=[P.tok(t + f"lnfree{b}")], sem=t + f"ldx{b}", dma=True)
        tk1 = P.op("vector", lambda e, b=b, co=co: e.reduce_sum(c[:, co:co + 1], s["xbuf"][b][:, :], axis=AX.X),
                   waits=[tk_x], sem=SC)
        tk2 = P.op("scalar", lambda e, b=b, co=co: e.activation(s["lnscr"][:, :], s["xbuf"][b][:, :], AF.Square,
                                                                accum_out=c[:, co + 1:co + 2]),
                   waits=[tk_x], sem=SC)
        tk = P.op("vector", lambda e, co=co: e.tensor_scalar(c[:, co + 2:co + 3], c[:, co:co + 1], 1.0 / D, None, op0=ALU.mult),
                  waits=[tk1], sem=SC)
        tk = P.op("vector", lambda e, co=co: e.tensor_scalar(c[:, co + 3:co + 4], c[:, co + 1:co + 2], 1.0 / D, None, op0=ALU.mult),
                  waits=[tk2, tk], sem=SC)
        tk = P.op("vector", lambda e, co=co: e.tensor_tensor(c[:, co + 4:co + 5], c[:, co + 2:co + 3], c[:, co + 2:co + 3], op=ALU.mult),
                  waits=[tk], sem=SC)
        tk = P.op("vector", lambda e, co=co: e.tensor_tensor(c[:, co + 4:co + 5], c[:, co + 3:co + 4], c[:, co + 4:co + 5], op=ALU.subtract),
                  waits=[tk], sem=SC)
        tk = P.op("vector", lambda e, co=co: e.tensor_scalar(c[:, co + 4:co + 5], c[:, co + 4:co + 5], EPS, None, op0=ALU.add),
                  waits=[tk], sem=SC)
        tk = P.op("scalar", lambda e, co=co: e.activation(c[:, co + 4:co + 5], c[:, co + 4:co + 5], AF.Sqrt),
                  waits=[tk], sem=SC)
        tk = P.op("vector", lambda e, co=co: e.reciprocal(c[:, co + 4:co + 5], c[:, co + 4:co + 5]),
                  waits=[tk], sem=SC)
        tk = P.op("vector", lambda e, co=co: e.tensor_tensor(c[:, co + 5:co + 6], c[:, co + 2:co + 3], c[:, co + 4:co + 5], op=ALU.mult),
                  waits=[tk], sem=SC)
        tk = P.op("vector", lambda e, co=co: e.tensor_scalar(c[:, co + 5:co + 6], c[:, co + 5:co + 6], -1.0, None, op0=ALU.mult),
                  waits=[tk], sem=SC)
        tk_ln = P.op("scalar", lambda e, b=b, co=co: e.activation(s["lnbuf"][b][:, :], s["xbuf"][b][:, :], AF.Identity,
                                                                  bias=c[:, co + 5:co + 6], scale=c[:, co + 4:co + 5]),
                     waits=[tk, P.tok(t + "lnout")], sem=t + f"lnfree{b}")
        P.op("scalar", lambda e, i=i, b=b: e.dma_start(s["myLN"][i * 128:(i + 1) * 128, :], s["lnbuf"][b][:, :]),
             waits=[tk_ln], sem=t + "lnout", dma=True)
        if i % 8 == 7:
            ln_chunk_toks.append(P.tok(t + "lnout"))

    lnall_toks = []
    for kch in range(8):
        lnall_toks.append(P.op("gpsimd", lambda e, kch=kch: e.collective_compute(
            "AllGather", ALU.bypass, replica_groups=RG,
            ins=[s["myLN"][kch * 1024:(kch + 1) * 1024, :].opt()],
            outs=[s["LNall"][kch * 8192:(kch + 1) * 8192, :].opt()]),
            waits=[ln_chunk_toks[kch]], sem=SCC))

    base_lnout = P.tok(t + "lnout")[1] - 64 * 16
    prev_logit = None
    prev_ltm = None
    for g in range(NGRP):
        wln = [(t + "lnout@scalar", base_lnout + 16 * 4 * (g + 1))]
        mm_toks = []
        for kd in range(8):
            b = kd % 2
            tk_l = P.op("sync", lambda e, g=g, kd=kd, b=b: e.dma_start(
                s["lnfm"][b][:, :], s["myLN"][g * 512:(g + 1) * 512, kd * 128:(kd + 1) * 128], transpose=True),
                waits=wln + [P.tok(t + f"rfree{b}")], sem=t + f"lfm{b}", dma=True)
            mm_toks.append(P.op("tensor", lambda e, kd=kd, b=b: e.matmul(
                s["pE"][0:8, :], s["wsb"][:, kd * 8:(kd + 1) * 8], s["lnfm"][b][:, :],
                start=(kd == 0), stop=(kd == 7)),
                waits=[tk_l] + ([prev_logit] if kd == 0 and prev_logit else []), sem=t + f"rfree{b}"))
        tk = P.op("scalar", lambda e: e.activation(s["logit_bf"][0:8, :], s["pE"][0:8, :], AF.Identity,
                                                   bias=s["bscol"][0:8, 0:1]),
                  waits=mm_toks[-2:], sem=SEV)
        prev_logit = tk
        tr_toks = []
        for a4 in range(4):
            tr_toks.append(P.op("tensor", lambda e, a4=a4: e.transpose(
                s["pT"][:, a4 * 8:(a4 + 1) * 8], s["logit_bf"][0:8, a4 * 128:(a4 + 1) * 128], s["identbf"][0:8, 0:8]),
                waits=[tk] + ([prev_ltm] if a4 == 0 and prev_ltm else []), sem=SMM))
        tk = P.op("vector", lambda e: e.tensor_copy(s["ltm"][:, 0:32], s["pT"][:, 0:32]), waits=tr_toks, sem=SEV)
        prev_ltm = tk
        tk = P.op("vector", lambda e: e.reduce_max(s["red1"][:, 0:4], v3e(s["ltm"][:, 0:32]), axis=AX.X),
                  waits=[tk], sem=SC)
        tk = P.op("vector", lambda e: e.tensor_tensor(
            v3e(s["ltm"][:, 32:64]), v3e(s["ltm"][:, 0:32]),
            s["red1"][:, 0:4].rearrange("p (a one) -> p a one", one=1).to_broadcast([128, 4, 8]), op=ALU.subtract),
            waits=[tk], sem=SC)
        tk = P.op("scalar", lambda e: e.activation(s["ltm"][:, 32:64], s["ltm"][:, 32:64], AF.Exp), waits=[tk], sem=SC)
        tk = P.op("vector", lambda e: e.reduce_sum(s["red1"][:, 4:8], v3e(s["ltm"][:, 32:64]), axis=AX.X),
                  waits=[tk], sem=SC)
        tk = P.op("vector", lambda e: e.reciprocal(s["red1"][:, 4:8], s["red1"][:, 4:8]), waits=[tk], sem=SC)
        probs_done.append(P.op("vector", lambda e, g=g: e.tensor_tensor(
            v3e(s["probs_tm"][:, g * 32:(g + 1) * 32]), v3e(s["ltm"][:, 32:64]),
            s["red1"][:, 4:8].rearrange("p (a one) -> p a one", one=1).to_broadcast([128, 4, 8]), op=ALU.mult),
            waits=[tk], sem=t + "probs"))

    tk_pw = P.op("scalar", lambda e: e.dma_start(
        s["probsMine"][:, :].rearrange("(a p) e -> p a e", p=128), v3e(s["probs_tm"][:, :])),
        waits=probs_done, sem=SD, dma=True)

    tk = P.op("gpsimd", lambda e: e.collective_compute(
        "AllGather", ALU.bypass, replica_groups=RG,
        ins=[s["probsMine"][:, :].opt()], outs=[s["probsAll"][:, :].opt()]),
        waits=[tk_pw], sem=SCC)
    tk_pa = P.op("sync", lambda e: e.dma_start(
        v3e(s["pAll"][:, :]), s["probsAll"][:, :].rearrange("(a p) e -> p a e", p=128)),
        waits=[tk], sem=SD, dma=True)
    tk = P.op("vector", lambda e: e.memset(s["plane"][:, :], 0.0), waits=[tk_pa], sem=SC)
    for j in range(EXP):
        tk2 = P.op("vector", lambda e, j=j: e.tensor_scalar(
            s["cmpA"][:, :], v3e(s["pAll"][:, :])[:, :, j:j + 1].rearrange("p a 1 -> p a"),
            s["ehot"][:, j:j + 1], None, op0=ALU.mult), waits=[tk_pa] if j == 0 else [tk], sem=SC)
        tk = P.op("vector", lambda e: e.tensor_tensor(s["plane"][:, :], s["plane"][:, :], s["cmpA"][:, :],
                                                      op=ALU.add), waits=[tk, tk2], sem=SC)
    tk_plane = tk

    bis = s["bis"]
    lo, hi, th, pred = bis[:, 0:1], bis[:, 1:2], bis[:, 2:3], bis[:, 3:4]
    tk1 = P.op("vector", lambda e: e.memset(lo, 0.0), waits=[tk_plane], sem=SC)
    tk2 = P.op("vector", lambda e: e.memset(hi, 1.0), sem=SC)
    tk = P.op("vector", lambda e: e.memset(th, 0.5), sem=SC)
    for it in range(NBISECT):
        tk = P.op("vector", lambda e: e.tensor_scalar(s["cmpA"][:, :], s["plane"][:, :], th[:, 0:1], None, op0=ALU.is_gt),
                  waits=[tk, tk1, tk2] if it == 0 else [tk], sem=SC)
        tk = P.op("vector", lambda e: e.reduce_sum(s["cols"][:, 16:17], s["cmpA"][:, :], axis=AX.X), waits=[tk], sem=SC)
        tk = P.op("tensor", lambda e: e.matmul(s["pF"][:, 0:1], s["onesff"][:, :], s["cols"][:, 16:17],
                                               start=True, stop=True), waits=[tk], sem=SMM)
        tk = P.op("vector", lambda e: e.tensor_scalar(pred, s["pF"][:, 0:1], KSEL - 0.5, None, op0=ALU.is_gt),
                  waits=[tk], sem=SC)
        d1, d2 = bis[:, 4:5], bis[:, 5:6]
        tka = P.op("vector", lambda e: e.tensor_tensor(d1, th, lo, op=ALU.subtract), waits=[tk], sem=SC)
        tka = P.op("vector", lambda e: e.tensor_tensor(d1, d1, pred, op=ALU.mult), waits=[tka], sem=SC)
        tkb = P.op("vector", lambda e: e.tensor_tensor(d2, hi, th, op=ALU.subtract), waits=[tk], sem=SC)
        tkb = P.op("vector", lambda e: e.tensor_tensor(d2, d2, pred, op=ALU.mult), waits=[tkb], sem=SC)
        tk1 = P.op("vector", lambda e: e.tensor_tensor(lo, lo, d1, op=ALU.add), waits=[tka], sem=SC)
        tk2 = P.op("vector", lambda e: e.tensor_tensor(hi, th, d2, op=ALU.add), waits=[tkb, tk1], sem=SC)
        tk = P.op("vector", lambda e: e.tensor_tensor(th, lo, hi, op=ALU.add), waits=[tk1, tk2], sem=SC)
        tk = P.op("vector", lambda e: e.tensor_scalar(th, th, 0.5, None, op0=ALU.mult), waits=[tk], sem=SC)
    tk_lo = tk1
    tk = P.op("scalar", lambda e: e.dma_start(s["thMine"][:, :], lo[0:1, 0:1]), waits=[tk, tk_lo], sem=SD, dma=True)
    tk = P.op("gpsimd", lambda e: e.collective_compute(
        "AllGather", ALU.bypass, replica_groups=RG,
        ins=[s["thMine"][:, :].opt()], outs=[s["thAll"][:, :].opt()]),
        waits=[tk], sem=SCC)
    tk = P.op("sync", lambda e: e.dma_start(s["throw"][0:1, 0:8], s["thAll"][:, :].rearrange("e one -> (one e)").rearrange("(o e) -> o e", o=1)),
              waits=[tk], sem=SD, dma=True)
    tk = P.op("tensor", lambda e: e.matmul(s["pF"][:, 8:16], s["onesrowf"][0:1, :], s["throw"][0:1, 0:8],
                                           start=True, stop=True), waits=[tk], sem=SMM)
    tk_thc = P.op("vector", lambda e: e.tensor_copy(s["thcols"][:, :], s["pF"][:, 8:16]), waits=[tk], sem=SC)

    sD_toks = []
    tk_free = None
    prev_sD = None
    for ee in range(EXP):
        tk = P.op("vector", lambda e, ee=ee: e.tensor_scalar(
            s["selbf"][:, :], v3e(s["pAll"][:, :])[:, :, ee:ee + 1].rearrange("p a 1 -> p a"),
            s["thcols"][:, ee:ee + 1], None, op0=ALU.is_gt),
            waits=[tk_thc, tk_pa] + ([tk_free] if tk_free else []), sem=SC)
        tk_cnt = P.op("tensor", lambda e: e.matmul(s["pE"][0:1, 0:512], s["onescolb"][:, 0:1], s["selbf"][:, :],
                                                   start=True, stop=True), waits=[tk], sem=SMM)
        tk = P.op("vector", lambda e: e.tensor_copy(s["cntrow"][0:1, :], s["pE"][0:1, 0:512]), waits=[tk_cnt], sem=SC)
        tk_free = tk
        tk = P.op("vector", lambda e: e.tensor_tensor_scan(s["srow"][0:1, 1:513], s["cntrow"][0:1, :],
                                                           s["zrow"][0:1, 0:512], 0.0, op0=ALU.add, op1=ALU.add),
                  waits=[tk, s["_tmp"]["crel"][1]] + ([prev_sD] if prev_sD else []), sem=SC)
        tkz = P.op("vector", lambda e: e.memset(s["srow"][0:1, 0:1], 0.0), waits=[tk], sem=SC)
        prev_sD = P.op("scalar", lambda e, ee=ee: e.dma_start(s["Srows"][ee:ee + 1, 0:640], s["srow"][0:1, 0:640]),
                       waits=[tk, tkz], sem=SD, dma=True)
        sD_toks.append(prev_sD)

    tk_sel = P.op("vector", lambda e: e.tensor_scalar(s["selbf"][:, :], s["plane"][:, :], lo[:, 0:1], None, op0=ALU.is_gt),
                  waits=[tk_lo, tk_free], sem=SC)
    tk_r = P.op("tensor", lambda e: e.matmul(s["pG"][:, 0:512], s["lstrictb"][:, :], s["selbf"][:, :],
                                             start=True, stop=True), waits=[tk_sel], sem=SMM)
    tk_rv = P.op("vector", lambda e: e.tensor_copy(s["rmine"][:, :], s["pG"][:, 0:512]), waits=[tk_r], sem=SC)
    tk_cnt = P.op("tensor", lambda e: e.matmul(s["pE"][0:1, 0:512], s["onescolb"][:, 0:1], s["selbf"][:, :],
                                               start=True, stop=True), waits=[tk_sel, tk_rv], sem=SMM)
    tk_cv = P.op("vector", lambda e: e.tensor_copy(s["cntrow"][0:1, :], s["pE"][0:1, 0:512]),
                 waits=[tk_cnt, prev_sD], sem=SC)
    tk = P.op("vector", lambda e: e.tensor_tensor_scan(s["srow"][0:1, 1:513], s["cntrow"][0:1, :],
                                                       s["zrow"][0:1, 0:512], 0.0, op0=ALU.add, op1=ALU.add),
              waits=[tk_cv], sem=SC)
    tkz = P.op("vector", lambda e: e.memset(s["srow"][0:1, 0:1], 0.0), waits=[tk], sem=SC)
    tk_sown = P.op("vector", lambda e: e.tensor_copy(s["sown"][0:1, 0:513], s["srow"][0:1, 0:513]),
                   waits=[tk, tkz], sem=SC)

    tk_gs = P.op("vector", lambda e: e.tensor_tensor(s["gidsel"][:, :], s["gid"][:, :], s["selbf"][:, :], op=ALU.mult),
                 waits=[tk_sel, s["_tmp"]["crel"][0]], sem=SC)
    last_pg = None
    pg_dma = []
    for k in range(KCAP):
        tk = P.op("vector", lambda e, k=k: e.tensor_scalar(s["cmpA"][:, :], s["rmine"][:, :], float(k), None, op0=ALU.is_equal),
                  waits=[tk_rv, tk_gs] if k == 0 else [last_pg], sem=SC)
        tk = P.op("vector", lambda e: e.tensor_tensor(s["cmpB"][:, :], s["cmpA"][:, :], s["gidsel"][:, :], op=ALU.mult),
                  waits=[tk], sem=SC)
        tk = P.op("tensor", lambda e: e.matmul(s["pE"][0:1, 0:512], s["onescolf"][:, 0:1], s["cmpB"][:, :],
                                               start=True, stop=True), waits=[tk], sem=SMM)
        tk = P.op("vector", lambda e, k=k: e.tensor_copy(s["srow"][0:1, 0:512], s["pE"][0:1, 0:512]),
                  waits=[tk, tk_sown] + pg_dma[-1:], sem=SC)
        last_pg = tk
        pg_dma.append(P.op("scalar", lambda e, k=k: e.dma_start(
            s["pagesKB"][k * 512:(k + 1) * 512, :].rearrange("r one -> (one r)").rearrange("(o r) -> o r", o=1),
            s["srow"][0:1, 0:512]),
            waits=[tk], sem=SD, dma=True))

    w0 = [tk_sown, tk_cv, last_pg]
    tk_ne = P.op("vector", lambda e: e.tensor_scalar(s["row1"][0:1, :], s["cntrow"][0:1, :], 0.0, None, op0=ALU.is_gt),
                 waits=w0, sem=SC)
    tk = P.op("vector", lambda e: e.tensor_scalar(s["row2"][0:1, :], s["cntrow"][0:1, :], -1.0, float(KCAP),
                                                  op0=ALU.mult, op1=ALU.add), waits=w0, sem=SC)
    tk = P.op("vector", lambda e: e.tensor_tensor_scan(s["row3"][0:1, :], s["row2"][0:1, :], s["zrow"][0:1, :], 0.0,
                                                       op0=ALU.add, op1=ALU.add), waits=[tk], sem=SC)
    tkA = P.op("vector", lambda e: e.tensor_copy(s["row2"][0:1, 1:512], s["row3"][0:1, 0:511]), waits=[tk], sem=SC)
    tkB = P.op("vector", lambda e: e.memset(s["row2"][0:1, 0:1], 0.0), waits=[tk], sem=SC)
    tk_V = P.op("vector", lambda e: e.tensor_copy(s["row2"][0:1, 0:1], s["row2"][0:1, 0:1]), waits=[tkA, tkB], sem=SC)
    tk_d0 = P.op("vector", lambda e: e.tensor_scalar(s["row3"][0:1, :], s["row1"][0:1, :], -1.0, 1.0,
                                                     op0=ALU.mult, op1=ALU.add), waits=[tk_ne, tkA], sem=SC)
    tk_d1 = P.op("vector", lambda e: e.tensor_tensor(s["row4"][0:1, :], s["row2"][0:1, :], s["row1"][0:1, :],
                                                     op=ALU.mult), waits=[tk_ne, tk_V], sem=SC)
    tk = P.op("vector", lambda e: e.tensor_tensor_scan(s["row5"][0:1, :], s["row3"][0:1, :], s["row4"][0:1, :], 0.0,
                                                       op0=ALU.mult, op1=ALU.add), waits=[tk_d0, tk_d1], sem=SC)
    tkA = P.op("vector", lambda e: e.tensor_copy(s["row3"][0:1, 1:512], s["row5"][0:1, 0:511]), waits=[tk], sem=SC)
    tkB = P.op("vector", lambda e: e.memset(s["row3"][0:1, 0:1], 0.0), waits=[tk, tkA], sem=SC)
    tk_J = P.op("vector", lambda e: e.tensor_tensor(s["row4"][0:1, :], s["row2"][0:1, :], s["row3"][0:1, :],
                                                    op=ALU.subtract), waits=[tkA, tkB, tk_V, tk_d1], sem=SC)
    tkd0 = P.op("scalar", lambda e: e.dma_start(s["rowD"][0:1, :], s["row4"][0:1, :]), waits=[tk_J], sem=SD, dma=True)
    tkd1 = P.op("scalar", lambda e: e.dma_start(s["rowD"][1:2, :], s["sown"][0:1, 0:512]),
                waits=[tk_sown, tkd0], sem=SD, dma=True)
    tkd2 = P.op("scalar", lambda e: e.dma_start(s["rowD"][2:3, :], s["row1"][0:1, :]),
                waits=[tk_ne, tkd1], sem=SD, dma=True)
    tk_jt = P.op("sync", lambda e: e.dma_start(s["jtile"][:, 0:4], s["rowD"][0:1, :].rearrange("one (p x) -> p (x one)", p=128)),
                 waits=[tkd0], sem=SD, dma=True)
    tk_st = P.op("sync", lambda e: e.dma_start(s["stile"][:, 0:4], s["rowD"][1:2, :].rearrange("one (p x) -> p (x one)", p=128)),
                 waits=[tkd1], sem=SD, dma=True)
    tk_mt = P.op("sync", lambda e: e.dma_start(s["mtile"][:, 0:4], s["rowD"][2:3, :].rearrange("one (p x) -> p (x one)", p=128)),
                 waits=[tkd2], sem=SD, dma=True)
    tk = P.op("vector", lambda e: e.tensor_scalar(s["postile"][:, 0:4], s["stile"][:, 0:4], 1.0 / 64.0, 0.5 / 64.0 - 0.5,
                                                  op0=ALU.mult, op1=ALU.add), waits=[tk_st], sem=SC)
    tk = P.op("vector", lambda e: e.tensor_copy(s["positile"][:, 0:4], s["postile"][:, 0:4]), waits=[tk], sem=SC)
    tk = P.op("vector", lambda e: e.tensor_copy(s["postile"][:, 0:4], s["positile"][:, 0:4]), waits=[tk], sem=SC)
    tk = P.op("vector", lambda e: e.tensor_tensor(s["postile"][:, 0:4], s["postile"][:, 0:4], s["stile"][:, 0:4],
                                                  op=ALU.add), waits=[tk], sem=SC)
    tk = P.op("vector", lambda e: e.tensor_scalar(s["postile"][:, 0:4], s["postile"][:, 0:4], -8319.0, None, op0=ALU.add),
              waits=[tk], sem=SC)
    tk = P.op("vector", lambda e: e.tensor_tensor(s["postile"][:, 0:4], s["postile"][:, 0:4], s["mtile"][:, 0:4],
                                                  op=ALU.mult), waits=[tk, tk_mt], sem=SC)
    tk = P.op("vector", lambda e: e.tensor_scalar(s["postile"][:, 0:4], s["postile"][:, 0:4], 8319.0, None, op0=ALU.add),
              waits=[tk], sem=SC)
    tk_pi32 = P.op("vector", lambda e: e.tensor_copy(s["positile"][:, 0:4], s["postile"][:, 0:4]),
                   waits=[tk], sem=SC)
    tk = P.op("vector", lambda e: e.memset(s["ztile"][:, :], 0.0), waits=[tk_pi32], sem=SC)
    tk_za = P.op("scalar", lambda e: e.dma_start(s["piAdjD"][:, :].rearrange("(p g) one -> p g", p=128),
                                                 s["ztile"][:, 0:65]), waits=[tk], sem=SD, dma=True)
    sc_toks = []
    for x in range(4):
        sc_toks.append(P.op("gpsimd", lambda e, x=x: e.indirect_dma_start(
            s["piAdjD"][:, :], IndirectOffsetOnAxis(s["positile"][:, x:x + 1], axis=0),
            s["jtile"][:, x:x + 1], None),
            waits=[tk_pi32, tk_za, tk_jt] + sc_toks[-1:], sem=SD, dma=True))
    tk_lA = P.op("sync", lambda e: e.dma_start(s["atile"][:, 0:65],
                                               s["piAdjD"][:, :].rearrange("(p g) one -> p g", p=128)),
                 waits=sc_toks, sem=SD, dma=True)
    tk = P.op("vector", lambda e: e.tensor_tensor_scan(s["A65"][:, 0:65], s["atile"][:, 0:65], s["ztile"][:, 0:65], 0.0,
                                                       op0=ALU.add, op1=ALU.add), waits=[tk_lA], sem=SC)
    tk_tot = P.op("vector", lambda e: e.tensor_copy(s["cols"][:, 17:18], s["A65"][:, 64:65]), waits=[tk], sem=SC)
    tk_pf = P.op("tensor", lambda e: e.matmul(s["pF"][:, 24:25], s["lstrictf"][:, :], s["cols"][:, 17:18],
                                              start=True, stop=True), waits=[tk_tot], sem=SMM)
    tk_off = P.op("vector", lambda e: e.tensor_copy(s["cols"][:, 18:19], s["pF"][:, 24:25]), waits=[tk_pf], sem=SC)
    tk = P.op("vector", lambda e: e.tensor_scalar(s["A65"][:, 0:64], s["A65"][:, 0:64], s["cols"][:, 18:19], None, op0=ALU.add),
              waits=[tk_off, tk_tot], sem=SC)
    tk = P.op("vector", lambda e: e.tensor_tensor(s["A65"][:, 0:64], s["A65"][:, 0:64], s["piog"][:, :], op=ALU.add),
              waits=[tk, s["_tmp"]["crel"][2]], sem=SC)
    tk = P.op("vector", lambda e: e.tensor_scalar(s["A65"][:, 0:64], s["A65"][:, 0:64], float(512 * KCAP - 1), None, op0=ALU.min),
              waits=[tk], sem=SC)
    tk = P.op("vector", lambda e: e.tensor_scalar(s["A65"][:, 0:64], s["A65"][:, 0:64], 0.0, None, op0=ALU.max),
              waits=[tk], sem=SC)
    tk = emit_mod(P, s, s["B65"][:, :], s["A65"][:, 0:64], KCAP, s["pii"][:, :], [tk], SC)
    tk = P.op("vector", lambda e: e.tensor_tensor(s["C65"][:, :], s["A65"][:, 0:64], s["B65"][:, :], op=ALU.subtract),
              waits=[tk], sem=SC)
    tk = P.op("vector", lambda e: e.tensor_scalar(s["C65"][:, :], s["C65"][:, :], 1.0 / KCAP, None, op0=ALU.mult),
              waits=[tk], sem=SC)
    tk = P.op("vector", lambda e: e.tensor_scalar(s["B65"][:, :], s["B65"][:, :], 512.0, None, op0=ALU.mult),
              waits=[tk], sem=SC)
    tk = P.op("vector", lambda e: e.tensor_tensor(s["C65"][:, :], s["C65"][:, :], s["B65"][:, :], op=ALU.add),
              waits=[tk], sem=SC)
    tk_pii = P.op("vector", lambda e: e.tensor_copy(s["pii"][:, :], s["C65"][:, :]), waits=[tk], sem=SC)

    idv_toks = []
    for g in range(64):
        idv_toks.append(P.op("gpsimd", lambda e, g=g: e.indirect_dma_start(
            s["idvf"][:, g:g + 1], None, s["pagesKB"][:, :], IndirectOffsetOnAxis(s["pii"][:, g:g + 1], axis=0)),
            waits=([tk_pii] + pg_dma) if g == 0 else [], sem=SD, dma=True))
    tk = emit_mod(P, s, s["w64a"][:, :], s["idvf"][:, :], 8192, s["pii"][:, :], idv_toks, SC)
    tk = P.op("vector", lambda e: e.tensor_tensor(s["w64b"][:, :], s["idvf"][:, :], s["w64a"][:, :], op=ALU.subtract),
              waits=[tk], sem=SC)
    tk = P.op("vector", lambda e: e.tensor_scalar(s["w64b"][:, :], s["w64b"][:, :], 1.0 / 8192.0, None, op0=ALU.mult),
              waits=[tk], sem=SC)
    tk = emit_mod(P, s, s["w64c"][:, :], s["w64a"][:, :], 1024, s["pii"][:, :], [tk], SC)
    tk = P.op("vector", lambda e: e.tensor_tensor(s["w64a"][:, :], s["w64a"][:, :], s["w64c"][:, :], op=ALU.subtract),
              waits=[tk], sem=SC)
    tk = P.op("vector", lambda e: e.tensor_scalar(s["w64a"][:, :], s["w64a"][:, :], 8.0, None, op0=ALU.mult),
              waits=[tk], sem=SC)
    tk = P.op("vector", lambda e: e.tensor_scalar(s["w64b"][:, :], s["w64b"][:, :], 1024.0, None, op0=ALU.mult),
              waits=[tk], sem=SC)
    tk = P.op("vector", lambda e: e.tensor_tensor(s["w64a"][:, :], s["w64a"][:, :], s["w64b"][:, :], op=ALU.add),
              waits=[tk], sem=SC)
    tk = P.op("vector", lambda e: e.tensor_tensor(s["w64a"][:, :], s["w64a"][:, :], s["w64c"][:, :], op=ALU.add),
              waits=[tk], sem=SC)
    tk = P.op("vector", lambda e: e.tensor_scalar(s["w64a"][:, :], s["w64a"][:, :], float(N - 1), None, op0=ALU.min),
              waits=[tk], sem=SC)
    tk = P.op("vector", lambda e: e.tensor_scalar(s["w64a"][:, :], s["w64a"][:, :], 0.0, None, op0=ALU.max),
              waits=[tk], sem=SC)
    tk_rows = P.op("vector", lambda e: e.tensor_copy(s["rowsi"][:, :], s["w64a"][:, :]), waits=[tk], sem=SC)

    xg_toks = []
    for g in range(64):
        b = g % 2
        tka = P.op("gpsimd", lambda e, g=g, b=b: e.indirect_dma_start(
            s["gbuf"][b][:, :], None, s["LNall"][:, :], IndirectOffsetOnAxis(s["rowsi"][:, g:g + 1], axis=0)),
            waits=([tk_rows] + lnall_toks if g <= 1 else []) + ([xg_toks[-2]] if len(xg_toks) >= 2 else []),
            sem=t + "xg" + str(b), dma=True)
        xg_toks.append(P.op("gpsimd", lambda e, g=g, b=b: e.dma_start(
            s["CX"][:, :].rearrange("(p g) d -> p g d", p=128)[:, g:g + 1, :].rearrange("p 1 d -> p d"),
            s["gbuf"][b][:, :]), waits=[tka], sem=t + "xg" + str(b), dma=True))
    all_xg = [P.tok(t + "xg0"), P.tok(t + "xg1")]

    co_group_done = []
    prev_out2_last = None
    prev_evac = dict(s["_tmp"].get("Fevac", {}))
    for g in range(NGRP):
        b = g % 2
        ld_toks = []
        for kd in range(8):
            ld_toks.append(P.op("sync", lambda e, g=g, kd=kd, b=b: e.dma_start(
                s["xfm"][b][:, kd * 512:(kd + 1) * 512],
                s["CX"][g * 512:(g + 1) * 512, kd * 128:(kd + 1) * 128], transpose=True),
                waits=all_xg + [P.tok(t + f"xffree{b}")], sem=t + f"xfl{b}", dma=True))
        gelu_toks = []
        for fc in range(32):
            pb = s["pA"] if fc % 2 == 0 else s["pB"]
            mmw = list(ld_toks) if fc < 2 else [gelu_toks[fc - 2]]
            if fc == 0 and prev_out2_last is not None:
                mmw.append(prev_out2_last)
            tkm = None
            for kd in range(8):
                tkm = P.op("tensor", lambda e, fc=fc, kd=kd, b=b, pb=pb: e.matmul(
                    pb[:, :], s["w1sb"][:, kd * 4096 + fc * 128: kd * 4096 + (fc + 1) * 128],
                    s["xfm"][b][:, kd * 512:(kd + 1) * 512], start=(kd == 0), stop=(kd == 7)),
                    waits=mmw if kd == 0 else [], sem=(SMM if kd == 7 else None))
            gelu_toks.append(P.op("scalar", lambda e, fc=fc, pb=pb: e.activation(
                s["hsb"][:, fc * 512:(fc + 1) * 512], pb[:, :], AF.Gelu_apprx_tanh,
                bias=s["b1sb"][:, fc:fc + 1]), waits=[tkm], sem=SEV))
        P.op("vector", lambda e: e.tensor_copy(s["dmy"][0:1, 0:1], s["dmy"][0:1, 0:1]),
             waits=[gelu_toks[-1]], sem=t + f"xffree{b}")
        out2_w = []
        for t4 in range(4):
            for dh in range(2):
                j = t4 * 2 + dh
                pc = s["pC"] if j % 2 == 0 else s["pD"]
                mmw = [gelu_toks[-1]]
                if (j % 2) in prev_evac:
                    mmw.append(prev_evac[j % 2])
                tkm = None
                for fc in range(32):
                    tkm = P.op("tensor", lambda e, fc=fc, t4=t4, dh=dh, pc=pc: e.matmul(
                        pc[:, :], s["hsb"][:, fc * 512 + t4 * 128: fc * 512 + (t4 + 1) * 128],
                        s["w2sb"][:, fc * 1024 + dh * 512: fc * 1024 + (dh + 1) * 512],
                        start=(fc == 0), stop=False), waits=mmw if fc == 0 else [], sem=None)
                tkm = P.op("tensor", lambda e, dh=dh, pc=pc: e.matmul(
                    pc[:, :], s["onesrowb"][0:1, :], s["b2sb"][0:1, dh * 512:(dh + 1) * 512],
                    start=False, stop=True), sem=SMM)
                cowtok = s["_tmp"].get("cowtok")
                tke = P.op("vector", lambda e, dh=dh, pc=pc: e.tensor_copy(
                    s["obufb"][:, dh * 512:(dh + 1) * 512], pc[:, :]),
                    waits=[tkm] + ([cowtok] if cowtok else []), sem=SEV)
                prev_evac[j % 2] = tke
                out2_w.append(tke)
                if dh == 1:
                    P.op("scalar", lambda e, g=g, t4=t4: e.dma_start(
                        s["compact"][g * 512 + t4 * 128: g * 512 + (t4 + 1) * 128, :], s["obufb"][:, :]),
                        waits=out2_w[-2:], sem=t + "cow", dma=True)
                    s["_tmp"]["cowtok"] = P.tok(t + "cow")
        prev_out2_last = out2_w[-1]
        co_group_done.append(P.tok(t + "cow"))
    s["_tmp"]["Fevac"] = prev_evac

    tk = P.op("vector", lambda e: e.memset(s["zbig"][:, :], 0.0), waits=[co_group_done[-1]], sem=SC)
    tk_zs = P.op("scalar", lambda e: e.dma_start(s["compact"][8192:CO_ROWS, :], s["zbig"][:, :]),
                 waits=[tk], sem=t + "cow", dma=True)

    ag_toks = []
    for kch in range(8):
        need_groups = min(NGRP, (CO_CH * (kch + 1) + 511) // 512)
        w = [co_group_done[need_groups - 1]]
        if kch == 7:
            w.append(tk_zs)
        ag_toks.append(P.op("gpsimd", lambda e, kch=kch: e.collective_compute(
            "AllGather", ALU.bypass, replica_groups=RG,
            ins=[s["compact"][kch * CO_CH:(kch + 1) * CO_CH, :].opt()],
            outs=[s["allOut"][kch * 8 * CO_CH:(kch + 1) * 8 * CO_CH, :].opt()]),
            waits=w, sem=SCC))

    hrel = []
    for cn, sn in [("c_emap", "emap"), ("c_eoff3", "eoff3"), ("c_zrowcol", "zrowcol"),
                   ("c_iotaslot", "iotaslot"), ("c_keslot3", "keslot3")]:
        hrel.append(P.op("scalar", lambda e, cn=cn, sn=sn: e.dma_start(s[sn][:, :], s[cn][:, :]),
                         waits=[co_group_done[-1]] if not hrel else [], sem=SD, dma=True))

    def ld_smine(e):
        pid = P.nc.partition_id([mybir.EngineType.SP])
        return e.dma_start(s["smine"][0:8, 0:64], s["Srows"][0:8, bass.ts(pid, 64)])

    def ld_smine2(e):
        pid = P.nc.partition_id([mybir.EngineType.SP])
        return e.dma_start(s["smine"][0:8, 64:128], s["Srows"][0:8, bass.ts(pid + 1, 64)])

    tk_sm = P.op("sync", ld_smine, waits=sD_toks + [co_group_done[-1]], sem=SD, dma=True)
    tk_sm2 = P.op("sync", ld_smine2, waits=[tk_sm], sem=SD, dma=True)

    tk_se = P.op("vector", lambda e: e.tensor_copy(
        s["sEven"][0:8, 0:33],
        s["smine"][0:8, 0:128].rearrange("e (b two) -> e b two", two=2)[:, 0:33, 0:1].rearrange("e b 1 -> e b")),
        waits=[tk_sm, tk_sm2], sem=SC)
    tk_cb = P.op("vector", lambda e: e.tensor_tensor(s["cEven"][0:8, 0:32], s["sEven"][0:8, 1:33],
                                                     s["sEven"][0:8, 0:32], op=ALU.subtract),
                 waits=[tk_se], sem=SC)
    sv_toks = []
    for gc in range(3):
        sv_toks.append(P.op("tensor", lambda e, gc=gc: e.matmul(
            s["pE"][:, gc * 32:(gc + 1) * 32], s["emap"][0:8, gc * 128:(gc + 1) * 128], s["sEven"][0:8, 0:32],
            start=True, stop=True), waits=[tk_se] + hrel, sem=SMM))
        sv_toks.append(P.op("tensor", lambda e, gc=gc: e.matmul(
            s["pF"][:, 128 + gc * 32:128 + (gc + 1) * 32], s["emap"][0:8, gc * 128:(gc + 1) * 128],
            s["cEven"][0:8, 0:32], start=True, stop=True), waits=[tk_cb], sem=SMM))
    tk_sv = P.op("vector", lambda e: e.tensor_copy(s["svals"][:, 0:96], s["pE"][:, 0:96]), waits=sv_toks, sem=SC)
    tk_cvv = P.op("vector", lambda e: e.tensor_copy(s["cvals"][:, 0:96], s["pF"][:, 128:224]), waits=sv_toks, sem=SC)
    kebc = s["keslot3"][:, 0:3].rearrange("p (g one) -> p g one", one=1).to_broadcast([128, 3, 32])
    v96 = lambda ap: ap.rearrange("p (g b) -> p g b", b=32)
    tk = P.op("vector", lambda e: e.tensor_tensor(v96(s["iwork"][:, 0:96]), v96(s["svals"][:, 0:96]), kebc, op=ALU.add),
              waits=[tk_sv] + hrel, sem=SC)
    tkp = P.op("vector", lambda e: e.tensor_tensor(v96(s["ipred"][:, 0:96]), kebc, v96(s["cvals"][:, 0:96]), op=ALU.is_lt),
               waits=[tk_cvv], sem=SC)
    tk = P.op("vector", lambda e: e.tensor_scalar(s["iwork2"][:, 0:96], s["iwork"][:, 0:96], float(ZROW), None, op0=ALU.min),
              waits=[tk], sem=SC)
    tk = P.op("vector", lambda e: e.tensor_tensor(s["iwork"][:, 0:96], s["iwork2"][:, 0:96], s["zrowcol"][:, 0:96],
                                                  op=ALU.subtract), waits=[tk, tkp], sem=SC)
    tk = P.op("vector", lambda e: e.tensor_tensor(s["iwork"][:, 0:96], s["iwork"][:, 0:96], s["ipred"][:, 0:96],
                                                  op=ALU.mult), waits=[tk], sem=SC)
    tk = P.op("vector", lambda e: e.tensor_tensor(s["iwork"][:, 0:96], s["iwork"][:, 0:96], s["zrowcol"][:, 0:96],
                                                  op=ALU.add), waits=[tk], sem=SC)
    tk = emit_mod(P, s, s["iwork2"][:, 0:96], s["iwork"][:, 0:96], CO_CH, s["idxslot"][:, 0:96], [tk], SC)
    tk = P.op("vector", lambda e: e.tensor_tensor(s["iwork3"][:, 0:96], s["iwork"][:, 0:96], s["iwork2"][:, 0:96],
                                                  op=ALU.subtract), waits=[tk], sem=SC)
    tk = P.op("vector", lambda e: e.tensor_scalar(s["iwork3"][:, 0:96], s["iwork3"][:, 0:96], 8.0, None, op0=ALU.mult),
              waits=[tk], sem=SC)
    tk = P.op("vector", lambda e: e.tensor_tensor(s["iwork2"][:, 0:96], s["iwork2"][:, 0:96], s["eoff3"][:, 0:96],
                                                  op=ALU.add), waits=[tk], sem=SC)
    tk = P.op("vector", lambda e: e.tensor_tensor(s["iwork"][:, 0:96], s["iwork3"][:, 0:96], s["iwork2"][:, 0:96],
                                                  op=ALU.add), waits=[tk], sem=SC)
    tk_idx = P.op("vector", lambda e: e.tensor_copy(s["idxslot"][:, 0:96], s["iwork"][:, 0:96]), waits=[tk], sem=SC)

    for blk in range(32):
        db = blk % 2
        g_toks = []
        for gc in range(3):
            g_toks.append(P.op("gpsimd", lambda e, blk=blk, gc=gc, db=db: e.indirect_dma_start(
                s["rslot"][db][:, gc * 1024:(gc + 1) * 1024], None, s["allOut"][:, :],
                IndirectOffsetOnAxis(s["idxslot"][:, gc * 32 + blk:gc * 32 + blk + 1], axis=0)),
                waits=([tk_idx] + ag_toks if blk <= 1 and gc == 0 else []) +
                      [P.tok(t + "rsfree" + str(db))],
                sem=t + "rg" + str(db), dma=True))
        emit_combine_block(P, s, t, blk, db, g_toks, Xin, Xout, tk_thc,
                           [tk_idx] if blk < 2 else [])

    return P.tok(t + "xow")


def emit_combine_block(P, s, t, blk, db, g_toks, Xin, Xout, tk_thc, w0):
    SD, SC, SMM, SEV = t + "d", t + "c", t + "mm", t + "ev"
    pt_toks = []
    half_done = []
    tk_sc8 = None
    for half in range(2):
        a = 2 * blk + half
        ho = half * 8
        tk = P.op("vector", lambda e, a=a, ho=ho: e.tensor_tensor(
            s["selc"][:, ho:ho + 8], s["probs_tm"][:, a * 8:(a + 1) * 8], s["thcols"][:, 0:8], op=ALU.is_gt),
            waits=[tk_thc, P.tok(t + "pfree")] + w0, sem=SC)
        tk_selbf = P.op("vector", lambda e, ho=ho: e.tensor_copy(s["selcb"][:, ho:ho + 8], s["selc"][:, ho:ho + 8]),
                        waits=[tk], sem=SC)
        tk_r = P.op("tensor", lambda e, ho=ho, half=half: e.matmul(
            s["pG"][:, ho:ho + 8], s["lstrictb"][:, :], s["selcb"][:, ho:ho + 8],
            start=True, stop=(half == 0)), waits=[tk_selbf], sem=SMM)
        if half == 0:
            tk_c = P.op("tensor", lambda e: e.matmul(s["pG"][0:1, 16:24], s["onescolb"][:, 0:1], s["selcb"][:, 0:8],
                                                     start=True, stop=True), waits=[tk_selbf], sem=SMM)
            tk_sc8 = P.op("vector", lambda e: e.tensor_copy(s["sc8"][0:1, 0:8], s["pG"][0:1, 16:24]),
                          waits=[tk_c], sem=SC)
        else:
            tk_r = P.op("tensor", lambda e: e.matmul(s["pG"][:, 8:16], s["onesrowb"][0:1, :], s["sc8"][0:1, 0:8],
                                                     start=False, stop=True), waits=[tk_sc8], sem=SMM)
        tk_rv = P.op("vector", lambda e, ho=ho: e.tensor_copy(s["slotof"][:, ho:ho + 8], s["pG"][:, ho:ho + 8]),
                     waits=[tk_r], sem=SC)
        tk = P.op("vector", lambda e, ho=ho: e.tensor_tensor(s["slotof"][:, ho:ho + 8], s["slotof"][:, ho:ho + 8],
                                                             s["selc"][:, ho:ho + 8], op=ALU.mult),
                  waits=[tk_rv], sem=SC)
        tk2 = P.op("vector", lambda e, ho=ho: e.tensor_scalar(s["sel4k"][:, ho:ho + 8], s["selc"][:, ho:ho + 8],
                                                              -4096.0, 4096.0, op0=ALU.mult, op1=ALU.add),
                   waits=[tk], sem=SC)
        tk = P.op("vector", lambda e, ho=ho: e.tensor_tensor(s["slotof"][:, ho:ho + 8], s["slotof"][:, ho:ho + 8],
                                                             s["sel4k"][:, ho:ho + 8], op=ALU.add),
                  waits=[tk2], sem=SC)
        pe_toks = []
        for ee in range(EXP):
            pe_toks.append(P.op("vector", lambda e, ee=ee, a=a, ho=ho: e.tensor_scalar(
                s["Ptile"][:, ee * SLOTC:(ee + 1) * SLOTC], s["iotaslot"][:, :], s["slotof"][:, ho + ee:ho + ee + 1],
                s["probs_tm"][:, a * 8 + ee:a * 8 + ee + 1], op0=ALU.is_equal, op1=ALU.mult),
                waits=[tk] + (pt_toks[-1:] if pe_toks == [] and pt_toks else []), sem=SC))
        for sc in range(3):
            tkt = P.op("tensor", lambda e, sc=sc: e.transpose(
                s["pT"][:, 0:128], s["Ptile"][:, sc * 128:(sc + 1) * 128], s["identbf"][:, :]),
                waits=pe_toks + (pt_toks[-1:] if pt_toks else []), sem=SMM)
            pt_toks.append(P.op("vector", lambda e, half=half, sc=sc: e.tensor_copy(
                s["PT"][half][:, sc * 128:(sc + 1) * 128], s["pT"][:, 0:128]), waits=[tkt], sem=SEV))
        half_done.append(pt_toks[-1])
    P.op("vector", lambda e: e.tensor_copy(s["dmy"][0:1, 0:1], s["dmy"][0:1, 0:1]),
         waits=half_done, sem=t + "pfree")

    tk_x1 = P.op("sync", lambda e, blk=blk, db=db: e.dma_start(s["xcomb"][db][:, :], Xin[blk * 256:blk * 256 + 128, :]),
                 waits=[P.tok(t + "xcfree" + str(db))] + w0, sem=t + f"xc{db}", dma=True)
    tk_x2 = P.op("sync", lambda e, blk=blk, db=db: e.dma_start(s["xcomb2"][db][:, :], Xin[blk * 256 + 128:(blk + 1) * 256, :]),
                 waits=list(w0), sem=t + f"xc{db}", dma=True)
    ev_toks = []
    for half in range(2):
        for dh in range(2):
            j = half * 2 + dh
            pc = s["pC"] if j % 2 == 0 else s["pD"]
            mmw = g_toks + [half_done[half]]
            wk = "Hpsum" + str(j % 2)
            if wk in s["_tmp"]:
                mmw.append(s["_tmp"][wk])
            tkm = None
            for sc in range(3):
                tkm = P.op("tensor", lambda e, half=half, dh=dh, sc=sc, pc=pc, db=db: e.matmul(
                    pc[:, :], s["PT"][half][:, sc * 128:(sc + 1) * 128],
                    s["rslot"][db][:, sc * 1024 + dh * 512: sc * 1024 + (dh + 1) * 512],
                    start=(sc == 0), stop=(sc == 2)),
                    waits=mmw if sc == 0 else [], sem=(SMM if sc == 2 else None))
            xb = s["xcomb"] if half == 0 else s["xcomb2"]
            tke = P.op("vector", lambda e, dh=dh, pc=pc, db=db, xb=xb: e.scalar_tensor_tensor(
                s["ocomb"][:, dh * 512:(dh + 1) * 512], pc[:, :], 0.5, xb[db][:, dh * 512:(dh + 1) * 512],
                op0=ALU.mult, op1=ALU.add),
                waits=[tkm, tk_x1 if half == 0 else tk_x2, P.tok(t + "xow")], sem=SEV)
            s["_tmp"][wk] = tke
            ev_toks.append(tke)
            if dh == 1:
                P.op("scalar", lambda e, blk=blk, half=half: e.dma_start(
                    Xout[blk * 256 + half * 128: blk * 256 + (half + 1) * 128, :], s["ocomb"][:, :]),
                    waits=ev_toks[-2:], sem=t + "xow", dma=True)
    P.op("vector", lambda e: e.tensor_copy(s["dmy"][0:1, 0:1], s["dmy"][0:1, 0:1]),
         waits=[s["_tmp"]["Hpsum0"], s["_tmp"]["Hpsum1"]], sem=t + "rsfree" + str(db))
    P.op("vector", lambda e: e.tensor_copy(s["dmy"][0:1, 0:1], s["dmy"][0:1, 0:1]),
         waits=ev_toks[-2:], sem=t + "xcfree" + str(db))


CH = 128
NCHUNK = 4096 // CH
GC = GAMMA ** CH


def emit_att(P, s, f1_done, dr):
    t = "T_"
    SC, SMM, SEV = t + "c", t + "mm", t + "ev"
    X1 = dr["X1"]
    X2 = dr["X2"]
    myLN = s["myLN"]

    wl = []
    wl.append(P.op("gpsimd", lambda e: e.dma_start(s["wqsb"][:, :], s["wqt"][:, :]), waits=[f1_done], sem=t + "wl", dma=True))
    wl.append(P.op("gpsimd", lambda e: e.dma_start(s["wksb"][:, :], s["wkt"][:, :]), sem=t + "wl", dma=True))
    wl.append(P.op("gpsimd", lambda e: e.dma_start(s["wvsb"][:, :], s["wvt"][:, :]), sem=t + "wl", dma=True))
    wl.append(P.op("gpsimd", lambda e: e.dma_start(s["wosb"][:, :], s["wot"][:, :]), sem=t + "wl", dma=True))

    c = s["cols"]
    for i in range(64):
        b = i % 2
        co = 8 * b
        tk_x = P.op("sync", lambda e, i=i, b=b: e.dma_start(s["xbuf"][b][:, :], X1[i * 128:(i + 1) * 128, :]),
                    waits=[f1_done] + [P.tok(t + f"lnfree{b}")], sem=t + f"ldx{b}", dma=True)
        tk1 = P.op("vector", lambda e, b=b, co=co: e.reduce_sum(c[:, co:co + 1], s["xbuf"][b][:, :], axis=AX.X),
                   waits=[tk_x], sem=SC)
        tk2 = P.op("scalar", lambda e, b=b, co=co: e.activation(s["lnscr"][:, :], s["xbuf"][b][:, :], AF.Square,
                                                                accum_out=c[:, co + 1:co + 2]),
                   waits=[tk_x], sem=SC)
        tk = P.op("vector", lambda e, co=co: e.tensor_scalar(c[:, co + 2:co + 3], c[:, co:co + 1], 1.0 / D, None, op0=ALU.mult),
                  waits=[tk1], sem=SC)
        tk = P.op("vector", lambda e, co=co: e.tensor_scalar(c[:, co + 3:co + 4], c[:, co + 1:co + 2], 1.0 / D, None, op0=ALU.mult),
                  waits=[tk2, tk], sem=SC)
        tk = P.op("vector", lambda e, co=co: e.tensor_tensor(c[:, co + 4:co + 5], c[:, co + 2:co + 3], c[:, co + 2:co + 3], op=ALU.mult),
                  waits=[tk], sem=SC)
        tk = P.op("vector", lambda e, co=co: e.tensor_tensor(c[:, co + 4:co + 5], c[:, co + 3:co + 4], c[:, co + 4:co + 5], op=ALU.subtract),
                  waits=[tk], sem=SC)
        tk = P.op("vector", lambda e, co=co: e.tensor_scalar(c[:, co + 4:co + 5], c[:, co + 4:co + 5], EPS, None, op0=ALU.add),
                  waits=[tk], sem=SC)
        tk = P.op("scalar", lambda e, co=co: e.activation(c[:, co + 4:co + 5], c[:, co + 4:co + 5], AF.Sqrt),
                  waits=[tk], sem=SC)
        tk = P.op("vector", lambda e, co=co: e.reciprocal(c[:, co + 4:co + 5], c[:, co + 4:co + 5]),
                  waits=[tk], sem=SC)
        tk = P.op("vector", lambda e, co=co: e.tensor_tensor(c[:, co + 5:co + 6], c[:, co + 2:co + 3], c[:, co + 4:co + 5], op=ALU.mult),
                  waits=[tk], sem=SC)
        tk = P.op("vector", lambda e, co=co: e.tensor_scalar(c[:, co + 5:co + 6], c[:, co + 5:co + 6], -1.0, None, op0=ALU.mult),
                  waits=[tk], sem=SC)
        tk_ln = P.op("scalar", lambda e, b=b, co=co: e.activation(s["lnbuf"][b][:, :], s["xbuf"][b][:, :], AF.Identity,
                                                                  bias=c[:, co + 5:co + 6], scale=c[:, co + 4:co + 5]),
                     waits=[tk, P.tok(t + "ln")], sem=t + f"lnfree{b}")
        P.op("scalar", lambda e, i=i, b=b: e.dma_start(myLN[i * 128:(i + 1) * 128, :], s["lnbuf"][b][:, :]),
             waits=[tk_ln], sem=t + "ln", dma=True)
    base_ln = P.tok(t + "ln")[1] - 64 * 16

    prev_evac = {}
    prev_sp = None
    wsbs = {0: "wqsb", 1: "wksb", 2: "wvsb"}
    fmds = {0: "QfmD", 1: "KfmD", 2: "VfmD"}
    for g in range(NGRP):
        pb = g % 2
        lnp = s["lnpA"] if pb == 0 else s["lnpB"]
        wln = [(t + "ln@scalar", base_ln + 16 * 4 * (g + 1))]
        ld_toks = []
        for dc in range(8):
            ld_toks.append(P.op("sync", lambda e, g=g, dc=dc, lnp=lnp: e.dma_start(
                lnp[:, dc * 512:(dc + 1) * 512],
                myLN[g * 512:(g + 1) * 512, dc * 128:(dc + 1) * 128], transpose=True),
                waits=wln + wl + ([P.tok(t + f"prel{pb}")] if g >= 2 else []), sem=t + f"pl{pb}", dma=True))
        mm_last = []
        for j in range(12):
            pj, hdc = j // 4, j % 4
            psn = "pF" if j % 2 == 0 else "pG"
            spb = s["spA"] if j % 2 == 0 else s["spB"]
            mmw = list(ld_toks)
            if (j % 2) in prev_evac:
                mmw.append(prev_evac[j % 2])
            tkm = None
            for dc in range(8):
                tkm = P.op("tensor", lambda e, pj=pj, hdc=hdc, dc=dc, psn=psn, lnp=lnp: e.matmul(
                    s[psn][:, :], s[wsbs[pj]][:, dc * 512 + hdc * 128: dc * 512 + (hdc + 1) * 128],
                    lnp[:, dc * 512:(dc + 1) * 512], start=(dc == 0), stop=(dc == 7)),
                    waits=mmw if dc == 0 else [], sem=(SMM if dc == 7 else None))
            mm_last.append(tkm)
            tke = P.op("vector", lambda e, psn=psn, spb=spb: e.tensor_copy(spb[:, :], s[psn][:, :]),
                       waits=[tkm] + ([prev_sp] if prev_sp else []), sem=SEV)
            prev_evac[j % 2] = tke
            P.op("scalar", lambda e, pj=pj, hdc=hdc, g=g, spb=spb: e.dma_start(
                s[fmds[pj]][hdc * 128:(hdc + 1) * 128, g * 512:(g + 1) * 512], spb[:, :]),
                waits=[tke], sem=t + "sp", dma=True)
            prev_sp = P.tok(t + "sp")
        P.op("vector", lambda e: e.tensor_copy(s["dmy"][0:1, 2:3], s["dmy"][0:1, 0:1]),
             waits=mm_last[-2:], sem=t + f"prel{pb}")
    proj_done = P.tok(t + "sp")
    ml = []
    for cn, sn in [("c_mA4", "mA4"), ("c_mA24", "mA24"), ("c_mB2p4", "mB2p4"), ("c_mX124", "mX124"),
                   ("c_mB2T4", "mB2T4"), ("c_mD", "mD"), ("c_mqd", "mqd"), ("c_mblk", "mblk"),
                   ("c_kgcol", "kgcol")]:
        ml.append(P.op("sync", lambda e, cn=cn, sn=sn: e.dma_start(s[sn][:, :], s[cn][:, :]),
                       waits=[proj_done, P.tok(t + "prel0"), P.tok(t + "prel1")] if not ml else [],
                       sem=t + "ml", dma=True))

    pA, pB, pC, pD, pE, pF2, pG2 = s["pA"], s["pB"], s["pC"], s["pD"], s["pE"], s["pF"], s["pG"]
    v4b = lambda ap: ap.rearrange("p (s i) -> p s i", i=CH)
    m1b = lambda ap: ap.rearrange("p (one i) -> p one i", one=1).to_broadcast([128, 4, CH])
    m1b32 = lambda ap: ap.rearrange("p (one i) -> p one i", one=1).to_broadcast([32, 4, CH])
    T = s["_tmp"]
    for k in ["aQK1", "aQK2", "aP12", "aO", "aPE", "aPF", "aPG", "aOF"]:
        T.pop(k, None)
    ST = {}
    for gp in range(4):
        for sl in (0, 1):
            g8 = 2 * gp + sl
            tkz = P.op("vector", lambda e, g8=g8: e.memset(s["stm"][0:32, g8 * 384:(g8 + 1) * 384], 0.0),
                       waits=[proj_done] + ([T["aSb" + str(sl)]] if ("aSb" + str(sl)) in T else []), sem=SC)
            tkz2 = P.op("vector", lambda e, sl=sl: e.memset(s[f"Sb4{sl}"][0:32, :], 0.0), waits=[tkz], sem=SC)
            tkz2 = P.op("vector", lambda e, sl=sl: e.memset(s[f"Cb4{sl}"][0:32, :], 0.0), waits=[tkz2], sem=SC)
            ST[sl] = P.op("vector", lambda e, sl=sl: e.memset(s[f"Gnb4{sl}"][0:32, :], 0.0), waits=[tkz2], sem=SC)
        for ck in range(NCHUNK):
          for sl in (0, 1):
            g8 = 2 * gp + sl
            seq, hg = g8 // 4, g8 % 4
            db = sl
            tk_st0 = ST[sl]
            tok0 = seq * 4096 + ck * CH
            Sb4_, Cb4_, Gnb4_ = s[f"Sb4{sl}"], s[f"Cb4{sl}"], s[f"Gnb4{sl}"]
            lw = ([proj_done] + ml if gp == 0 and ck == 0 else []) + \
                 ([P.tok(t + f"lr{db}")] if ck >= 1 else [])
            l_toks = []
            for st4 in range(4):
                r0 = 128 * hg + 32 * st4
                l_toks.append(P.op("sync", lambda e, st4=st4, r0=r0, tok0=tok0, db=db: e.dma_start(
                    s[f"qT4{db}"][0:32, st4 * CH:(st4 + 1) * CH], s["QfmD"][r0:r0 + 32, tok0:tok0 + CH]),
                    waits=lw if st4 == 0 else [], sem=t + f"ld{db}", dma=True))
                l_toks.append(P.op("sync", lambda e, st4=st4, r0=r0, tok0=tok0, db=db: e.dma_start(
                    s[f"kT4{db}"][0:32, st4 * CH:(st4 + 1) * CH], s["KfmD"][r0:r0 + 32, tok0:tok0 + CH]),
                    sem=t + f"ld{db}", dma=True))
                l_toks.append(P.op("sync", lambda e, st4=st4, r0=r0, tok0=tok0, db=db: e.dma_start(
                    s[f"q4tm{db}"][:, st4 * 32:(st4 + 1) * 32], s["QfmD"][r0:r0 + 32, tok0:tok0 + CH], transpose=True),
                    sem=t + f"ld{db}", dma=True))
                l_toks.append(P.op("sync", lambda e, st4=st4, r0=r0, tok0=tok0, db=db: e.dma_start(
                    s[f"k4tm{db}"][:, st4 * 32:(st4 + 1) * 32], s["KfmD"][r0:r0 + 32, tok0:tok0 + CH], transpose=True),
                    sem=t + f"ld{db}", dma=True))
                l_toks.append(P.op("sync", lambda e, st4=st4, r0=r0, tok0=tok0, db=db: e.dma_start(
                    s[f"v4tm{db}"][:, st4 * 32:(st4 + 1) * 32], s["VfmD"][r0:r0 + 32, tok0:tok0 + CH], transpose=True),
                    sem=t + f"ld{db}", dma=True))
            ld_tok = l_toks[-1]
            qT4, kT4 = s[f"qT4{db}"], s[f"kT4{db}"]
            q4tm, k4tm, v4tm = s[f"q4tm{db}"], s[f"k4tm{db}"], s[f"v4tm{db}"]

            qk1 = []
            for st4 in range(4):
                qk1.append(P.op("tensor", lambda e, st4=st4, qT4=qT4, kT4=kT4: e.matmul(
                    pA[:, st4 * CH:(st4 + 1) * CH], kT4[0:32, st4 * CH:(st4 + 1) * CH],
                    qT4[0:32, st4 * CH:(st4 + 1) * CH], start=True, stop=True),
                    waits=([ld_tok] if st4 == 0 else []) + ([T["aQK1"]] if st4 == 0 and "aQK1" in T else []) +
                          ([T["aBp"]] if st4 == 0 and "aBp" in T else []),
                    sem=SMM))
            qk2 = []
            for st4 in range(4):
                qk2.append(P.op("tensor", lambda e, st4=st4, qT4=qT4, kT4=kT4: e.matmul(
                    pB[:, st4 * CH:(st4 + 1) * CH], qT4[0:32, st4 * CH:(st4 + 1) * CH],
                    kT4[0:32, st4 * CH:(st4 + 1) * CH], start=True, stop=True),
                    waits=([T["aQK2"]] if st4 == 0 and "aQK2" in T else []), sem=SMM))
            qpp = []
            for st4 in range(4):
                qpp.append(P.op("tensor", lambda e, st4=st4, qT4=qT4, Sb4_=Sb4_: e.matmul(
                    pE[0:32, st4 * CH:(st4 + 1) * CH], Sb4_[0:32, st4 * 32:(st4 + 1) * 32],
                    qT4[0:32, st4 * CH:(st4 + 1) * CH], start=True, stop=True),
                    waits=([tk_st0] if st4 == 0 else []) + ([T["aPE"]] if st4 == 0 and "aPE" in T else []),
                    sem=SMM))

            e_bp = P.op("scalar", lambda e: e.activation(s["Bp4"][:, :], pA[:, :], AF.Copy), waits=qk1, sem=SEV)
            T["aBp"] = e_bp
            tk = None
            for nm, msk in [("Ap4", "mA4"), ("A2n4", "mA24"), ("B2p4", "mB2p4"), ("X12_4", "mX124")]:
                tk = P.op("vector", lambda e, nm=nm, msk=msk: e.tensor_tensor(
                    s[nm][:, :], pA[:, :], s[msk][:, :], op=ALU.mult),
                    waits=qk1 + [e_bp] if tk is None else [], sem=SC)
            e_x12 = tk
            T["aQK1"] = e_x12
            e_b2t = P.op("vector", lambda e: e.tensor_tensor(
                s["B2T4"][:, :], pB[:, :], s["mB2T4"][:, :], op=ALU.mult),
                waits=qk2 + [e_x12], sem=SC)
            T["aQK2"] = e_b2t
            e_qpp = P.op("vector", lambda e: e.tensor_scalar(s["QppT4"][0:32, :], pE[0:32, :], GAMMA, None, op0=ALU.mult),
                         waits=qpp + [e_b2t], sem=SC)

            p12 = []
            for st4 in range(4):
                sl = slice(st4 * CH, (st4 + 1) * CH)
                w0 = [e_bp, e_x12, e_qpp] + ([T["aP12"]] if "aP12" in T else []) if st4 == 0 else []
                P.op("tensor", lambda e, sl=sl: e.matmul(pC[:, sl], s["Bp4"][:, sl], s["Ap4"][:, sl],
                                                         start=True, stop=False), waits=w0, sem=None)
                P.op("tensor", lambda e, sl=sl, qT4=qT4: e.matmul(pC[:, sl], qT4[0:32, sl], s["QppT4"][0:32, sl],
                                                                  start=False, stop=False), sem=None)
                p12.append(P.op("tensor", lambda e, sl=sl: e.matmul(pC[:, sl], s["B2p4"][:, sl], s["A2n4"][:, sl],
                                                                    start=False, stop=True), sem=SMM))
            tk = None
            for st4 in range(4):
                sl = slice(st4 * CH, (st4 + 1) * CH)
                tk = P.op("vector", lambda e, sl=sl: e.tensor_tensor(
                    s["DT4"][:, sl], pC[:, sl], s["mD"][:, :], op=ALU.mult),
                    waits=p12 if tk is None else [], sem=SC)
            e_dt = tk
            T["aP12"] = e_dt

            u_mm = []
            for st4 in range(4):
                u_mm.append(P.op("tensor", lambda e, st4=st4, kT4=kT4, Cb4_=Cb4_: e.matmul(
                    pF2[:, 32 * st4:32 * (st4 + 1)], kT4[0:32, st4 * CH:(st4 + 1) * CH],
                    Cb4_[0:32, st4 * 32:(st4 + 1) * 32], start=True, stop=True),
                    waits=([tk_st0] + ([T["aPF"]] if "aPF" in T else []) if st4 == 0 else []), sem=SMM))
            e_u = P.op("scalar", lambda e: e.activation(s["U4"][:, 0:128], pF2[:, 0:128], AF.Copy),
                       waits=u_mm, sem=SEV)
            z_mm = []
            for st4 in range(4):
                z_mm.append(P.op("tensor", lambda e, st4=st4, Sb4_=Sb4_, Cb4_=Cb4_: e.matmul(
                    pG2[0:32, 32 * st4:32 * (st4 + 1)], Sb4_[0:32, st4 * 32:(st4 + 1) * 32],
                    Cb4_[0:32, st4 * 32:(st4 + 1) * 32], start=True, stop=True),
                    waits=([tk_st0] + ([T["aPG"]] if "aPG" in T else []) if st4 == 0 else []), sem=SMM))
            e_z = P.op("scalar", lambda e: e.activation(s["Z4"][0:32, 0:128], pG2[0:32, 0:128], AF.Copy),
                       waits=z_mm, sem=SEV)
            e_qd1 = P.op("vector", lambda e, qT4=qT4: e.tensor_tensor(
                v4b(s["Qd1T4"][0:32, :]), v4b(qT4[0:32, :]), m1b32(s["mqd"][0:32, 0:CH]), op=ALU.mult),
                waits=[ld_tok] + ([T["aO"]] if "aO" in T else []), sem=SC)
            e_qd2 = P.op("vector", lambda e, qT4=qT4: e.tensor_tensor(
                v4b(s["Qd2T4"][0:32, :]), v4b(qT4[0:32, :]), m1b32(s["mqd"][0:32, CH:2 * CH]), op=ALU.mult),
                waits=[e_qd1], sem=SC)
            T["aQd"] = e_qd2

            o_mm = []
            w0 = [e_dt, e_u, e_z, e_qd2] + ([T["aO"]] if "aO" in T else [])
            P.op("tensor", lambda e, v4tm=v4tm: e.matmul(pD[:, :], v4tm[:, :], s["DT4"][:, :],
                                                         start=True, stop=False), waits=w0, sem=None)
            P.op("tensor", lambda e: e.matmul(pD[:, :], s["U4"][:, 0:128], s["X12_4"][:, :],
                                              start=False, stop=False), sem=None)
            P.op("tensor", lambda e: e.matmul(pD[:, :], s["Z4"][0:32, 0:128], s["Qd2T4"][0:32, :],
                                              start=False, stop=False), sem=None)
            o_mm.append(P.op("tensor", lambda e, Gnb4_=Gnb4_: e.matmul(pD[:, :], Gnb4_[0:32, :], s["Qd1T4"][0:32, :],
                                                                       start=False, stop=True), sem=SMM))
            e_om = P.op("vector", lambda e: e.tensor_tensor(s["Om4"][:, :], pD[:, :], s["mblk"][:, :], op=ALU.mult),
                        waits=o_mm + ([T["aOF"]] if "aOF" in T else []), sem=SC)
            T["aO"] = e_om
            def _fold(e):
                with P.nc.allow_low_precision(reason="band-masked sum: 3 of 4 addends are exactly zero"):
                    return e.reduce_sum(s["ofm"][:, :], s["Om4"][:, :].rearrange("p (s i) -> p i s", s=4), axis=AX.X)
            e_of = P.op("vector", _fold, waits=[e_om] + ([T["aOFD"]] if "aOFD" in T else []), sem=SC)
            T["aOF"] = e_of
            T["aOFD"] = P.op("scalar", lambda e, hg=hg, tok0=tok0: e.dma_start(
                s["OFmD"][128 * hg:128 * (hg + 1), tok0:tok0 + CH], s["ofm"][:, :]),
                waits=[e_of], sem=t + "of", dma=True)

            e_kg = P.op("vector", lambda e, k4tm=k4tm: e.tensor_tensor(
                s["Kg4"][:, :], k4tm[:, :], s["kgcol"][:, 0:1].to_broadcast([128, 128]), op=ALU.mult),
                waits=[ld_tok] + (list(T["aKgR"]) if "aKgR" in T else []), sem=SC)
            e_qg = P.op("vector", lambda e, q4tm=q4tm: e.tensor_tensor(
                s["Qg4"][:, :], q4tm[:, :], s["kgcol"][:, 0:1].to_broadcast([128, 128]), op=ALU.mult),
                waits=[e_kg], sem=SC)
            mm_s = P.op("tensor", lambda e, k4tm=k4tm: e.matmul(pE[:, 0:128], s["Kg4"][:, :], k4tm[:, :],
                                                                start=True, stop=True), waits=[e_qpp, e_kg], sem=SMM)
            mm_c = P.op("tensor", lambda e, v4tm=v4tm: e.matmul(pE[:, 128:256], s["Qg4"][:, :], v4tm[:, :],
                                                                start=True, stop=True), waits=[e_qg], sem=SMM)
            mm_kk = P.op("tensor", lambda e, k4tm=k4tm: e.matmul(pE[:, 256:384], k4tm[:, :], k4tm[:, :],
                                                                 start=True, stop=True), sem=SMM)
            T["aKgR"] = (mm_s, mm_c)
            kkev = []
            for st4 in range(4):
                kkev.append(P.op("vector", lambda e, st4=st4: e.tensor_scalar(
                    s["KTKg4"][0:32, st4 * 32:(st4 + 1) * 32], pE[32 * st4:32 * (st4 + 1), 256 + 32 * st4:256 + 32 * (st4 + 1)],
                    GC, None, op0=ALU.mult), waits=[mm_kk] if st4 == 0 else [], sem=SC))
            b2v_mm = []
            for st4 in range(4):
                b2v_mm.append(P.op("tensor", lambda e, st4=st4, v4tm=v4tm: e.matmul(
                    pG2[:, 128 + 32 * st4:128 + 32 * (st4 + 1)], s["B2T4"][:, st4 * CH:(st4 + 1) * CH],
                    v4tm[:, st4 * 32:(st4 + 1) * 32], start=True, stop=True),
                    waits=[e_z, e_b2t] if st4 == 0 else [], sem=SMM))
            e_b2v = P.op("scalar", lambda e: e.activation(s["B2V4"][:, :], pG2[:, 128:256], AF.Copy),
                         waits=b2v_mm, sem=SEV)
            mm_t2 = P.op("tensor", lambda e, k4tm=k4tm: e.matmul(pF2[:, 128:256], k4tm[:, :], s["B2V4"][:, :],
                                                                 start=True, stop=True), waits=[e_u, e_b2v], sem=SMM)
            t3_mm = []
            for st4 in range(4):
                t3_mm.append(P.op("tensor", lambda e, st4=st4, Cb4_=Cb4_: e.matmul(
                    pF2[0:32, 256 + 32 * st4:256 + 32 * (st4 + 1)], s["KTKg4"][0:32, st4 * 32:(st4 + 1) * 32],
                    Cb4_[0:32, st4 * 32:(st4 + 1) * 32], start=True, stop=True),
                    waits=kkev if st4 == 0 else [], sem=SMM))
            stw = []
            for st4 in range(4):
                mo = (g8 * 4 + st4) * 96
                stw.append(P.op("vector", lambda e, st4=st4, mo=mo: e.scalar_tensor_tensor(
                    s["stm"][0:32, mo:mo + 32], s["stm"][0:32, mo:mo + 32], GC,
                    pE[32 * st4:32 * (st4 + 1), 32 * st4:32 * (st4 + 1)], op0=ALU.mult, op1=ALU.add),
                    waits=[mm_kk] if st4 == 0 else [], sem=SC))
                stw.append(P.op("vector", lambda e, st4=st4, mo=mo: e.scalar_tensor_tensor(
                    s["stm"][0:32, mo + 32:mo + 64], s["stm"][0:32, mo + 32:mo + 64], GC,
                    pE[32 * st4:32 * (st4 + 1), 128 + 32 * st4:128 + 32 * (st4 + 1)], op0=ALU.mult, op1=ALU.add),
                    waits=[], sem=SC))
                stw.append(P.op("vector", lambda e, st4=st4, mo=mo: e.scalar_tensor_tensor(
                    s["stm"][0:32, mo + 64:mo + 96], s["stm"][0:32, mo + 64:mo + 96], GC,
                    pF2[32 * st4:32 * (st4 + 1), 128 + 32 * st4:128 + 32 * (st4 + 1)], op0=ALU.mult, op1=ALU.subtract),
                    waits=[t3_mm[3]] if st4 == 0 else [], sem=SC))
                stw.append(P.op("vector", lambda e, st4=st4, mo=mo: e.tensor_tensor(
                    s["stm"][0:32, mo + 64:mo + 96], s["stm"][0:32, mo + 64:mo + 96],
                    pF2[0:32, 256 + 32 * st4:256 + 32 * (st4 + 1)], op=ALU.subtract),
                    waits=[t3_mm[st4]], sem=SC))
            T["aPE"] = stw[-3]
            T["aPF"] = stw[-1]
            T["aPG"] = e_b2v
            cpw = stw + [e_om]
            for st4 in range(4):
                mo = (g8 * 4 + st4) * 96
                tkc = P.op("vector", lambda e, st4=st4, mo=mo, Sb4_=Sb4_: e.tensor_copy(
                    Sb4_[0:32, st4 * 32:(st4 + 1) * 32], s["stm"][0:32, mo:mo + 32]),
                    waits=cpw if st4 == 0 else [], sem=SC)
                tkc = P.op("vector", lambda e, st4=st4, mo=mo, Cb4_=Cb4_: e.tensor_copy(
                    Cb4_[0:32, st4 * 32:(st4 + 1) * 32], s["stm"][0:32, mo + 32:mo + 64]), sem=SC)
                tkc = P.op("vector", lambda e, st4=st4, mo=mo, Gnb4_=Gnb4_: e.tensor_copy(
                    Gnb4_[0:32, st4 * 32:(st4 + 1) * 32], s["stm"][0:32, mo + 64:mo + 96]), sem=SC)
            ST[sl] = tkc
            T["aSb" + str(sl)] = tkc
            P.op("vector", lambda e: e.tensor_copy(s["dmy"][0:1, 3:4], s["dmy"][0:1, 0:1]),
                 waits=[e_om, stw[-1], e_qd2, e_qg], sem=t + f"lr{db}")

    scan_done = P.tok(t + "of")

    prev_ev = {}
    for i in range(64):
        db = i % 2
        tk_x = P.op("sync", lambda e, i=i, db=db: e.dma_start(s["xbuf"][db][:, :], X1[i * 128:(i + 1) * 128, :]),
                    waits=[scan_done] + ([P.tok(t + f"cr{db}")] if i >= 2 else []), sem=t + f"xl{db}", dma=True)
        o_toks = []
        for hgg in range(4):
            o_toks.append(P.op("sync", lambda e, i=i, hgg=hgg, db=db: e.dma_start(
                s[f"of4{db}"][:, hgg * 128:(hgg + 1) * 128], s["OFmD"][128 * hgg:128 * (hgg + 1), i * 128:(i + 1) * 128]),
                sem=t + f"xl{db}", dma=True))
        ev2 = []
        for dh in range(2):
            psn = "pF" if dh == 0 else "pG"
            tkm = None
            for hgg in range(4):
                w0 = []
                if hgg == 0:
                    w0 = o_toks + wl + ([prev_ev[dh]] if dh in prev_ev else [])
                    if i == 0:
                        w0 += [T[k] for k in ("aPF", "aPG") if k in T]
                tkm = P.op("tensor", lambda e, hgg=hgg, dh=dh, psn=psn, db=db: e.matmul(
                    s[psn][:, :], s[f"of4{db}"][:, hgg * 128:(hgg + 1) * 128],
                    s["wosb"][:, hgg * 1024 + dh * 512: hgg * 1024 + (dh + 1) * 512],
                    start=(hgg == 0), stop=(hgg == 3)),
                    waits=w0, sem=(SMM if hgg == 3 else None))
            tke = P.op("vector", lambda e, dh=dh, psn=psn, db=db: e.tensor_tensor(
                s["ocomb"][:, dh * 512:(dh + 1) * 512], s[psn][:, :], s["xbuf"][db][:, dh * 512:(dh + 1) * 512],
                op=ALU.add), waits=[tkm, tk_x, P.tok(t + "xw")], sem=SEV)
            prev_ev[dh] = tke
            ev2.append(tke)
        P.op("scalar", lambda e, i=i: e.dma_start(X2[i * 128:(i + 1) * 128, :], s["ocomb"][:, :]),
             waits=ev2, sem=t + "xw", dma=True)
        P.op("vector", lambda e: e.tensor_copy(s["dmy"][0:1, 3:4], s["dmy"][0:1, 0:1]),
             waits=ev2, sem=t + f"cr{db}")
    return P.tok(t + "xw")


def build(nc):
    import contextlib
    stack = contextlib.ExitStack()

    x0 = nc.declare_dram_parameter("x0", [NT, D], F32, isOutput=False)
    w1t = nc.declare_dram_parameter("w1t", [128, 32768], F32, isOutput=False)
    w2t = nc.declare_dram_parameter("w2t", [128, 32768], F32, isOutput=False)
    b1t = nc.declare_dram_parameter("b1t", [128, 32], F32, isOutput=False)
    b2r = nc.declare_dram_parameter("b2r", [1, D], F32, isOutput=False)
    wst = nc.declare_dram_parameter("wst", [128, 64], F32, isOutput=False)
    bsI = nc.declare_dram_parameter("bsI", [8, 1], F32, isOutput=False)
    ehotI = nc.declare_dram_parameter("ehotI", [128, 8], F32, isOutput=False)
    xout = nc.declare_dram_parameter("xout", [NT, D], F32, isOutput=True)
    wqt = nc.declare_dram_parameter("wqt", [128, 4096], F32, isOutput=False)
    wkt = nc.declare_dram_parameter("wkt", [128, 4096], F32, isOutput=False)
    wvt = nc.declare_dram_parameter("wvt", [128, 4096], F32, isOutput=False)
    wot = nc.declare_dram_parameter("wot", [128, 4096], F32, isOutput=False)
    cdefs = [
        ("c_mA4", [128, 512], F32), ("c_mA24", [128, 512], F32), ("c_mB2p4", [128, 512], F32),
        ("c_mX124", [128, 512], F32), ("c_mB2T4", [128, 512], F32), ("c_mD", [128, 128], F32),
        ("c_mqd", [32, 256], BF16), ("c_mblk", [128, 512], F32), ("c_kgcol", [128, 1], BF16),
        ("c_lstrictb", [128, 128], BF16), ("c_lstrictf", [128, 128], F32),
        ("c_identbf", [128, 128], BF16), ("c_onescolb", [128, 1], BF16),
        ("c_onescolf", [128, 1], F32), ("c_onesrowb", [1, 128], BF16), ("c_onesrowf", [1, 128], F32),
        ("c_onesff", [128, 128], F32),
        ("c_iotaslot", [128, SLOTC], F32), ("c_gid", [128, 512], F32),
        ("c_piog", [128, 64], F32),
        ("c_keslot3", [128, 3], F32), ("c_eoff3", [128, 96], F32), ("c_zrowcol", [128, 96], F32),
        ("c_emap", [8, 384], F32), ("c_zrow", [1, 512], F32),
    ]
    consts = {}
    for name, shape, dt in cdefs:
        consts[name] = nc.declare_dram_parameter(name, shape, dt, isOutput=False)

    dr = {}
    dr["X1"] = nc.dram_tensor("X1", [NT, D], F32)
    dr["X2"] = nc.dram_tensor("X2", [NT, D], F32)
    dr["myLN"] = nc.dram_tensor("myLN", [NT, D], BF16)
    dr["LNall"] = nc.dram_tensor("LNall", [N, D], BF16)
    dr["probsMine"] = nc.dram_tensor("probsMine", [NT, 8], F32)
    dr["probsAll"] = nc.dram_tensor("probsAll", [N, 8], F32)
    dr["thMine"] = nc.dram_tensor("thMine", [1, 1], F32)
    dr["thAll"] = nc.dram_tensor("thAll", [8, 1], F32)
    dr["Srows"] = nc.dram_tensor("Srows", [8, 640], F32)
    dr["pagesKB"] = nc.dram_tensor("pagesKB", [KCAP * 512, 1], F32)
    dr["piAdjD"] = nc.dram_tensor("piAdjD", [128 * 65, 1], F32)
    dr["rowD"] = nc.dram_tensor("rowD", [3, 512], F32)
    dr["CX"] = nc.dram_tensor("CX", [NT, D], BF16)
    dr["compact"] = nc.dram_tensor("compact", [CO_ROWS, D], BF16)
    dr["allOut"] = nc.dram_tensor("allOut", [8 * CO_ROWS, D], BF16)
    dr["QfmD"] = nc.dram_tensor("QfmD", [512, NT], BF16)
    dr["KfmD"] = nc.dram_tensor("KfmD", [512, NT], BF16)
    dr["VfmD"] = nc.dram_tensor("VfmD", [512, NT], BF16)
    dr["OFmD"] = nc.dram_tensor("OFmD", [512, NT], BF16)

    sb = {}

    def S(name, shape, dt):
        sb[name] = stack.enter_context(nc.sbuf_tensor(name, shape, dt))

    S("w1sb", [128, 32768], BF16)
    S("w2sb", [128, 32768], BF16)
    hsb_off = nc.sbuf_base
    S("hsb", [128, 16384], BF16)
    A = lambda name, shape, dt, off: sb.__setitem__(name, nc.alloc_sbuf_tensor_at(name, shape, dt, offset=off))
    A("pAll", [128, 4096], F32, hsb_off)
    A("Ptile", [128, NSLOT], BF16, hsb_off + 0)
    A("PT0", [128, NSLOT], BF16, hsb_off + 768)
    A("PT1", [128, NSLOT], BF16, hsb_off + 1536)
    A("selc", [128, 16], F32, hsb_off + 2304)
    A("selcb", [128, 16], BF16, hsb_off + 2368)
    A("slotof", [128, 16], F32, hsb_off + 2400)
    A("sel4k", [128, 16], F32, hsb_off + 2464)
    A("sc8", [1, 8], BF16, hsb_off + 2528)
    A("iwork2", [128, 96], F32, hsb_off + 2560)
    A("iwork3", [128, 96], F32, hsb_off + 2944)
    A("ipred", [128, 96], F32, hsb_off + 3328)
    A("idxslot", [128, 96], I32, hsb_off + 3712)
    A("smine", [8, 128], F32, hsb_off + 4096)
    A("sEven", [8, 34], F32, hsb_off + 4608)
    A("cEven", [8, 32], F32, hsb_off + 4768)
    A("emap", [8, 384], F32, hsb_off + 4896)
    A("eoff3", [128, 96], F32, hsb_off + 6432)
    A("zrowcol", [128, 96], F32, hsb_off + 6816)
    A("iotaslot", [128, SLOTC], F32, hsb_off + 7200)
    A("keslot3", [128, 3], F32, hsb_off + 7392)
    A("gbuf0", [128, 1024], BF16, hsb_off + 8192)
    A("gbuf1", [128, 1024], BF16, hsb_off + 10240)
    A("xcomb21", [128, 1024], F32, hsb_off + 8192)
    A("cmpA", [128, 512], F32, hsb_off + 16384)
    A("cmpB", [128, 512], F32, hsb_off + 18432)
    A("rmine", [128, 512], F32, hsb_off + 20480)
    A("selbf", [128, 512], BF16, hsb_off + 22528)
    A("row1", [1, 512], F32, hsb_off + 23552)
    A("row2", [1, 512], F32, hsb_off + 25600)
    A("row3", [1, 512], F32, hsb_off + 27648)
    A("row4", [1, 512], F32, hsb_off + 29696)
    xfm_off = nc.sbuf_base
    S("xfm0", [128, 4096], BF16)
    A("srow", [1, 640], F32, xfm_off)
    A("cntrow", [1, 512], F32, xfm_off + 2560)
    A("sown", [1, 513], F32, xfm_off + 4608)
    A("svals", [128, 96], F32, xfm_off + 6688)
    A("cvals", [128, 96], F32, xfm_off + 7072)
    A("iwork", [128, 96], F32, xfm_off + 7456)
    rslot_off = nc.sbuf_base
    S("rslot0", [128, 3072], BF16)
    S("rslot1", [128, 3072], BF16)
    A("gid", [128, 512], F32, rslot_off)
    A("zrow", [1, 512], F32, rslot_off + 2048)
    A("row5", [1, 512], F32, rslot_off + 4096)
    A("ztile", [128, 65], F32, rslot_off + 6144)
    A("atile", [128, 65], F32, rslot_off + 6432)
    A("A65", [128, 65], F32, rslot_off + 6720)
    A("B65", [128, 64], F32, rslot_off + 7008)
    A("C65", [128, 64], F32, rslot_off + 7264)
    A("pii", [128, 64], I32, rslot_off + 7520)
    A("idvf", [128, 64], F32, rslot_off + 7776)
    A("w64a", [128, 64], F32, rslot_off + 8032)
    A("w64b", [128, 64], F32, rslot_off + 8288)
    A("w64c", [128, 64], F32, rslot_off + 8544)
    A("rowsi", [128, 64], I32, rslot_off + 8800)
    A("piog", [128, 64], F32, rslot_off + 9056)
    A("logit_bf", [8, 512], BF16, rslot_off + 9312)
    A("lnfm0", [128, 512], BF16, rslot_off + 10336)
    A("ltm", [128, 64], F32, rslot_off + 11360)
    A("red1", [128, 8], F32, rslot_off + 11616)
    xbuf_off = nc.sbuf_base
    S("xbuf0", [128, 1024], F32)
    S("xbuf1", [128, 1024], F32)
    A("xfm1", [128, 4096], BF16, xbuf_off)
    A("lnfm1", [128, 512], BF16, hsb_off + 14336)
    A("lnpA", [128, 4096], BF16, hsb_off + 0)
    A("lnpB", [128, 4096], BF16, rslot_off + 0)
    A("wqsb", [128, 4096], BF16, hsb_off + 8192)
    A("wksb", [128, 4096], BF16, hsb_off + 16384)
    A("wvsb", [128, 4096], BF16, hsb_off + 24576)
    A("wosb", [128, 4096], BF16, xfm_off)
    A("Bp4", [128, 512], BF16, hsb_off + 0)
    A("Ap4", [128, 512], BF16, hsb_off + 1024)
    A("A2n4", [128, 512], BF16, hsb_off + 2048)
    A("B2p4", [128, 512], BF16, hsb_off + 3072)
    A("X12_4", [128, 512], BF16, hsb_off + 4096)
    A("B2T4", [128, 512], BF16, hsb_off + 5120)
    A("QppT4", [32, 512], BF16, hsb_off + 6144)
    A("DT4", [128, 512], BF16, hsb_off + 7168)
    A("U4", [128, 128], BF16, hsb_off + 8192)
    A("Z4", [32, 128], BF16, hsb_off + 8448)
    A("Qd1T4", [32, 512], BF16, hsb_off + 8704)
    A("Qd2T4", [32, 512], BF16, hsb_off + 9728)
    A("Kg4", [128, 128], BF16, hsb_off + 10752)
    A("Qg4", [128, 128], BF16, hsb_off + 11008)
    A("B2V4", [128, 128], BF16, hsb_off + 11264)
    A("KTKg4", [32, 128], BF16, hsb_off + 11520)
    A("Sb40", [32, 128], BF16, hsb_off + 11776)
    A("Cb40", [32, 128], BF16, hsb_off + 12032)
    A("Gnb40", [32, 128], BF16, hsb_off + 12288)
    A("Sb41", [32, 128], BF16, hsb_off + 28672)
    A("Cb41", [32, 128], BF16, hsb_off + 28928)
    A("Gnb41", [32, 128], BF16, hsb_off + 29184)
    A("Om4", [128, 512], BF16, hsb_off + 12544)
    A("ofm", [128, 128], BF16, hsb_off + 13568)
    A("qT40", [32, 512], BF16, hsb_off + 13824)
    A("qT41", [32, 512], BF16, hsb_off + 14848)
    A("kT40", [32, 512], BF16, hsb_off + 15872)
    A("kT41", [32, 512], BF16, hsb_off + 16896)
    A("q4tm0", [128, 128], BF16, hsb_off + 17920)
    A("q4tm1", [128, 128], BF16, hsb_off + 18176)
    A("k4tm0", [128, 128], BF16, hsb_off + 18432)
    A("k4tm1", [128, 128], BF16, hsb_off + 18688)
    A("v4tm0", [128, 128], BF16, hsb_off + 18944)
    A("v4tm1", [128, 128], BF16, hsb_off + 19200)
    A("of40", [128, 512], BF16, hsb_off + 0)
    A("of41", [128, 512], BF16, hsb_off + 1024)
    A("stm", [32, 3072], F32, rslot_off + 0)
    A("mA4", [128, 512], F32, hsb_off + 22528)
    A("mA24", [128, 512], F32, hsb_off + 24576)
    A("mB2p4", [128, 512], F32, hsb_off + 26624)
    A("mX124", [128, 512], F32, hsb_off + 29440)
    A("mqd", [32, 256], BF16, hsb_off + 19456)
    A("kgcol", [128, 1], BF16, hsb_off + 19968)
    A("mblk", [128, 512], F32, hsb_off + 20480)
    S("lnbuf0", [128, 1024], BF16)
    lnscr_off = nc.sbuf_base
    S("lnscr", [128, 1024], F32)
    ocomb_off = nc.sbuf_base
    S("ocomb", [128, 1024], F32)
    A("obufb", [128, 1024], BF16, ocomb_off)
    A("spA", [128, 512], BF16, ocomb_off + 0)
    A("spB", [128, 512], BF16, ocomb_off + 1024)
    A("mB2T4", [128, 512], F32, lnscr_off + 0)
    A("mD", [128, 128], F32, lnscr_off + 2048)
    S("probs_tm", [128, 512], F32)
    S("cols", [128, 24], F32)
    S("bis", [128, 8], F32)
    S("jtile", [128, 4], F32)
    S("postile", [128, 4], F32)
    S("positile", [128, 4], I32)
    S("stile", [128, 4], F32)
    S("mtile", [128, 4], F32)
    S("dmy", [1, 4], F32)
    S("lstrictb", [128, 128], BF16)
    S("lstrictf", [128, 128], F32)
    S("identbf", [128, 128], BF16)
    S("onescolb", [128, 1], BF16)
    S("onescolf", [128, 1], F32)
    S("onesrowb", [1, 128], BF16)
    S("onesrowf", [1, 128], F32)
    S("onesff", [128, 128], F32)
    S("b2sb", [1, 1024], BF16)
    S("wsb", [128, 64], BF16)
    S("b1sb", [128, 32], F32)
    S("bscol", [8, 1], F32)
    S("thcols", [128, 8], F32)
    S("throw", [1, 8], F32)
    S("ehot", [128, 8], F32)

    ps = {}
    for nm in ["pA", "pB", "pC", "pD", "pE", "pF", "pG"]:
        ps[nm] = stack.enter_context(nc.psum_tensor(nm, [128, 512], F32))
    ps["pT"] = stack.enter_context(nc.psum_tensor("pT", [128, 256], BF16))

    s = {}
    s.update(sb)
    s.update(ps)
    s.update({k: v for k, v in dr.items()})
    s["_tmp"] = {}
    s["xbuf"] = [sb["xbuf0"], sb["xbuf1"]]
    s["xcomb"] = s["xbuf"]
    s["plane"] = sb["lnscr"][:, 0:512]
    s["gidsel"] = sb["ocomb"][:, 0:512]
    s["obuf"] = sb["ocomb"]
    s["zbig"] = sb["lnbuf0"]
    s["lnbuf"] = [sb["lnbuf0"], sb["lnbuf0"]]
    s["gbuf"] = [sb["gbuf0"], sb["gbuf1"]]
    s["xfm"] = [sb["xfm0"], sb["xfm1"]]
    s["lnfm"] = [sb["lnfm0"], sb["lnfm1"]]
    s["rslot"] = [sb["rslot0"], sb["rslot1"]]
    s["xcomb2"] = [sb["lnscr"], sb["xcomb21"]]
    s["PT"] = [sb["PT0"], sb["PT1"]]

    P = Planner(nc)

    P.op("gpsimd", lambda e: e.dma_start(s["w1sb"][:, :], w1t[:, :]), sem="initg", dma=True)
    P.op("gpsimd", lambda e: e.dma_start(s["w2sb"][:, :], w2t[:, :]), sem="initg", dma=True)
    P.op("gpsimd", lambda e: e.dma_start(s["wsb"][:, :], wst[:, :]), sem="initg", dma=True)
    P.op("gpsimd", lambda e: e.dma_start(s["b2sb"][:, :], b2r[:, :]), sem="initg", dma=True)
    P.op("sync", lambda e: e.dma_start(s["b1sb"][:, :], b1t[:, :]), sem="init", dma=True)
    P.op("sync", lambda e: e.dma_start(s["ehot"][:, :], ehotI[:, :]), sem="init", dma=True)
    P.op("sync", lambda e: e.dma_start(s["bscol"][:, :], bsI[:, :]), sem="init", dma=True)
    cmap = [("c_lstrictb", "lstrictb"), ("c_lstrictf", "lstrictf"), ("c_identbf", "identbf"),
            ("c_onescolb", "onescolb"), ("c_onescolf", "onescolf"), ("c_onesrowb", "onesrowb"),
            ("c_onesrowf", "onesrowf"), ("c_onesff", "onesff")]
    for cname, sbname in cmap:
        P.op("sync", lambda e, cname=cname, sbname=sbname: e.dma_start(s[sbname][:, :], consts[cname][:, :]),
             sem="init", dma=True)
    for cn in ["c_gid", "c_zrow", "c_piog", "c_iotaslot", "c_keslot3", "c_eoff3", "c_zrowcol", "c_emap",
               "c_mA4", "c_mA24", "c_mB2p4", "c_mX124", "c_mB2T4", "c_mD", "c_mqd", "c_mblk", "c_kgcol"]:
        s[cn] = consts[cn]
    s["wqt"], s["wkt"], s["wvt"], s["wot"] = wqt, wkt, wvt, wot
    P.gate_engines(P.tok("init"))
    P.gate_engines(P.tok("initg"))

    emit_ffn(P, s, x0.ap(), dr["X1"].ap())
    f1_done = P.tok("F_xow")

    att_done = emit_att(P, s, f1_done, dr)
    P.gate_engines(att_done)

    emit_ffn(P, s, dr["X2"].ap(), xout.ap())

    P.op("vector", lambda e: e.tensor_copy(s["dmy"][0:1, 1:2], s["dmy"][0:1, 0:1]),
         waits=[P.tok("F_xow")], sem="fin")
    P.finalize_gates()
    return P, stack


def host_constants():
    c = {}
    jj = np.arange(CH, dtype=np.float64)
    J, I = np.meshgrid(jj, jj, indexing="ij")
    le = (J <= I).astype(np.float64)
    lt = (I < J).astype(np.float64)
    g = float(GAMMA)
    c["c_mA4"] = np.tile((g ** (-J) * le).astype(np.float32), (1, 4))
    c["c_mA24"] = np.tile((-(g ** (-I)) * le).astype(np.float32), (1, 4))
    c["c_mB2p4"] = np.tile(lt.astype(np.float32), (1, 4))
    c["c_mX124"] = np.tile(((g ** (2 * I + 1 - J) - g ** (I + 1)) * le).astype(np.float32), (1, 4))
    U2, J2 = np.meshgrid(jj, jj, indexing="ij")
    c["c_mB2T4"] = np.tile((g ** (CH - 1 - U2) * (U2 < J2)).astype(np.float32), (1, 4))
    c["c_mD"] = (g ** (2 * I - J) * le).astype(np.float32)
    mqd = np.zeros((32, 2 * CH), np.float64)
    mqd[:, 0:CH] = g ** (jj + 1)
    mqd[:, CH:2 * CH] = g ** (2 * jj + 2)
    c["c_mqd"] = mqd.astype(np.float32).astype(bfdt)
    pp = np.arange(128)[:, None] // 32
    cc2 = np.arange(512)[None, :] // 128
    c["c_mblk"] = (pp == cc2).astype(np.float32)
    c["c_kgcol"] = (g ** (CH - 1 - jj)).reshape(128, 1).astype(np.float32).astype(bfdt)
    c["c_lstrictb"] = np.triu(np.ones((128, 128), np.float32), 1).astype(bfdt)
    c["c_lstrictf"] = np.triu(np.ones((128, 128), np.float32), 1)
    c["c_identbf"] = np.eye(128, dtype=np.float32).astype(bfdt)
    c["c_onescolb"] = np.ones((128, 1), np.float32).astype(bfdt)
    c["c_onescolf"] = np.ones((128, 1), np.float32)
    c["c_onesrowb"] = np.ones((1, 128), np.float32).astype(bfdt)
    c["c_onesrowf"] = np.ones((1, 128), np.float32)
    c["c_onesff"] = np.ones((128, 128), np.float32)
    c["c_iotaslot"] = np.broadcast_to(np.arange(SLOTC, dtype=np.float32), (128, SLOTC)).copy()
    p = np.arange(128, dtype=np.float32)[:, None]
    a = np.arange(512, dtype=np.float32)[None, :]
    c["c_gid"] = (a * 128 + p).astype(np.float32)
    g64 = np.arange(64, dtype=np.float32)[None, :]
    c["c_piog"] = (p * 64 + g64).astype(np.float32)
    sl = np.arange(128)[:, None] + np.arange(3)[None, :] * 128
    c["c_keslot3"] = (sl % SLOTC).astype(np.float32)
    es = (sl // SLOTC).astype(np.float32)
    c["c_eoff3"] = np.repeat((es * CO_CH)[:, :, None], 32, axis=2).reshape(128, 96).astype(np.float32)
    c["c_zrowcol"] = np.full((128, 96), float(ZROW), np.float32)
    em = np.zeros((8, 384), np.float32)
    for sgl in range(384):
        em[sgl // SLOTC, sgl] = 1.0
    c["c_emap"] = em
    c["c_zrow"] = np.zeros((1, 512), np.float32)
    return c


_PROG = {}


def get_prog():
    if "nc" not in _PROG:
        nc = bass.Bass("TRN2", target_bir_lowering=False, num_devices=NCORE)
        P, stack = build(nc)
        import contextlib
        semstack = contextlib.ExitStack()
        handles = {}
        for name in P.cnt.keys():
            handles[name] = semstack.enter_context(nc.semaphore(name))
        with nc.Block() as block:
            P.replay(block, handles)
        semstack.close()
        stack.close()
        _PROG["nc"] = nc
    return _PROG["nc"]


def kernel(**inputs):
    x = np.ascontiguousarray(np.asarray(inputs["L0_attn_x"], np.float32).reshape(N, D))
    W1 = np.asarray(inputs["L0_ffn_W1"], np.float32)
    W2 = np.asarray(inputs["L0_ffn_W2"], np.float32)
    b1 = np.asarray(inputs["L0_ffn_b1"], np.float32)
    b2 = np.asarray(inputs["L0_ffn_b2"], np.float32)
    Ws = np.ascontiguousarray(np.asarray(inputs["L0_ffn_Ws"], np.float32))
    bs = np.asarray(inputs["L0_ffn_bs"], np.float32)

    Wq = np.asarray(inputs["L0_attn_Wq"], np.float32)
    Wk = np.asarray(inputs["L0_attn_Wk"], np.float32)
    Wv = np.asarray(inputs["L0_attn_Wv"], np.float32)
    Wo = np.asarray(inputs["L0_attn_Wo"], np.float32)

    nc = get_prog()
    consts = host_constants()
    wst = Ws.reshape(8, 128, 8).transpose(1, 0, 2).reshape(128, 64)
    wqt = np.ascontiguousarray(Wq.reshape(8, 128, 512).transpose(1, 0, 2).reshape(128, 4096))
    wkt = np.ascontiguousarray(Wk.reshape(8, 128, 512).transpose(1, 0, 2).reshape(128, 4096))
    wvt = np.ascontiguousarray(Wv.reshape(8, 128, 512).transpose(1, 0, 2).reshape(128, 4096))
    wot_ = np.ascontiguousarray(Wo.reshape(4, 128, 1024).transpose(1, 0, 2).reshape(128, 4096))

    in_maps = []
    for c in range(NCORE):
        m = {
            "x0": np.ascontiguousarray(x[c * NT:(c + 1) * NT]),
            "w1t": np.ascontiguousarray(W1[c].reshape(8, 128, FF).transpose(1, 0, 2).reshape(128, 8 * FF)),
            "w2t": np.ascontiguousarray(W2[c].reshape(32, 128, D).transpose(1, 0, 2).reshape(128, 32 * D)),
            "b1t": np.ascontiguousarray(b1[c].reshape(32, 128).T),
            "b2r": np.ascontiguousarray(b2[c].reshape(1, D)),
            "wst": np.ascontiguousarray(wst),
            "bsI": np.ascontiguousarray(bs.reshape(8, 1)),
            "ehotI": np.broadcast_to(np.eye(8, dtype=np.float32)[c], (128, 8)).copy(),
            "wqt": wqt, "wkt": wkt, "wvt": wvt, "wot": wot_,
        }
        m.update(consts)
        in_maps.append(m)

    res = run_bass_kernel_spmd(nc, in_maps, core_ids=list(range(NCORE)))
    out = np.concatenate([res.results[c]["xout"] for c in range(NCORE)], axis=0)
    return out.reshape(B_SHAPE)


B_SHAPE = (16, 4096, 1024)


# version 07
# speedup vs baseline: 1.0290x; 1.0104x over previous
import numpy as np
import ml_dtypes

import concourse.bass as bass
import concourse.mybir as mybir
from concourse.bass_core_rust import IndirectOffsetOnAxis
from concourse.bass_utils import run_bass_kernel_spmd

F32 = mybir.dt.float32
BF16 = mybir.dt.bfloat16
I32 = mybir.dt.int32
AF = mybir.ActivationFunctionType
ALU = mybir.AluOpType
AX = mybir.AxisListType
bfdt = ml_dtypes.bfloat16

NCORE = 8
N = 65536
NT = 8192
D = 1024
EXP = 8
FF = 4096
KSEL = 8192
GAMMA = 0.99
EPS = 1e-5
KCAP = 34
SLOTC = 48
NSLOT = EXP * SLOTC
CO_CH = 1040
CO_ROWS = 8 * CO_CH
ZROW = 8200
NBISECT = 30
NGRP = 16
RG = [list(range(NCORE))]


class Planner:
    def __init__(self, nc):
        self.nc = nc
        self.ops = []
        self.cnt = {}

    def tok(self, sem):
        if sem not in self.cnt:
            hits = [k for k in self.cnt if k.startswith(sem + "@")]
            if len(hits) == 1:
                sem = hits[0]
            elif len(hits) > 1:
                raise KeyError(f"ambiguous sem {sem}: {hits}")
        return (sem, self.cnt.get(sem, 0))

    def op(self, eng, fn, waits=(), sem=None, dma=False, n=None):
        w = [t for t in waits if t is not None and t[1] > 0]
        incs = []
        token = None
        if sem is not None:
            sem = sem + "@" + eng
            amt = n if n is not None else (16 if dma else 1)
            self.cnt[sem] = self.cnt.get(sem, 0) + amt
            incs.append((sem, amt))
            token = (sem, self.cnt[sem])
        self.ops.append((eng, fn, w, incs))
        return token

    def gate_engines(self, token):
        pend = {e: True for e in ("sync", "vector", "scalar", "tensor", "gpsimd")}
        self._gates = getattr(self, "_gates", [])
        self._gates.append((len(self.ops), token, pend))

    def finalize_gates(self):
        for start, token, pend in getattr(self, "_gates", []):
            for idx in range(start, len(self.ops)):
                eng, fn, w, incs = self.ops[idx]
                if pend.get(eng):
                    w.append(token)
                    pend[eng] = False

    def replay(self, block, handles):
        def runner(myname, engine):
            for eng, fn, waits, incs in self.ops:
                if eng != myname:
                    continue
                for (sname, v) in waits:
                    if v > 0:
                        engine.wait_ge(handles[sname], v)
                inst = fn(engine)
                for (sname, a) in incs:
                    inst.then_inc(handles[sname], a)

        @block.sync
        def _(e):
            runner("sync", e)

        @block.vector
        def _(e):
            runner("vector", e)

        @block.scalar
        def _(e):
            runner("scalar", e)

        @block.tensor
        def _(e):
            runner("tensor", e)

        @block.gpsimd
        def _(e):
            runner("gpsimd", e)


def v3e(ap):
    return ap.rearrange("p (a e) -> p a e", e=8)


def emit_mod(P, s, dst, src, m, iscr, waits, sem):
    inv = 1.0 / float(m)
    tk = P.op("vector", lambda e: e.tensor_scalar(dst, src, inv, 0.5 * inv - 0.5, op0=ALU.mult, op1=ALU.add),
              waits=waits, sem=sem)
    tk = P.op("vector", lambda e: e.tensor_copy(iscr, dst), waits=[tk], sem=sem)
    tk = P.op("vector", lambda e: e.tensor_copy(dst, iscr), waits=[tk], sem=sem)
    tk = P.op("vector", lambda e: e.scalar_tensor_tensor(dst, dst, -float(m), src, op0=ALU.mult, op1=ALU.add),
              waits=[tk], sem=sem)
    return tk


def emit_ffn(P, s, Xin, Xout):
    t = "F_"
    SD, SC, SCC, SMM, SEV = t + "d", t + "c", t + "cc", t + "mm", t + "ev"

    tk_cg = P.op("sync", lambda e: e.dma_start(s["gid"][:, :], s["c_gid"][:, :]), sem=SD, dma=True)
    tk_cz = P.op("sync", lambda e: e.dma_start(s["zrow"][:, :], s["c_zrow"][:, :]), sem=SD, dma=True)
    tk_cp = P.op("sync", lambda e: e.dma_start(s["piog"][:, :], s["c_piog"][:, :]), sem=SD, dma=True)
    s["_tmp"]["crel"] = (tk_cg, tk_cz, tk_cp)

    ln_chunk_toks = []
    probs_done = []
    for i in range(64):
        b = i % 2
        co = 8 * b
        c = s["cols"]
        tk_x = P.op("sync", lambda e, i=i, b=b: e.dma_start(s["xbuf"][b][:, :], Xin[i * 128:(i + 1) * 128, :]),
                    waits=[P.tok(t + f"lnfree{b}")], sem=t + f"ldx{b}", dma=True)
        tk1 = P.op("vector", lambda e, b=b, co=co: e.reduce_sum(c[:, co:co + 1], s["xbuf"][b][:, :], axis=AX.X),
                   waits=[tk_x], sem=SC)
        tk2 = P.op("scalar", lambda e, b=b, co=co: e.activation(s["lnscr"][:, :], s["xbuf"][b][:, :], AF.Square,
                                                                accum_out=c[:, co + 1:co + 2]),
                   waits=[tk_x], sem=SC)
        tk = P.op("vector", lambda e, co=co: e.tensor_scalar(c[:, co + 2:co + 3], c[:, co:co + 1], 1.0 / D, None, op0=ALU.mult),
                  waits=[tk1], sem=SC)
        tk = P.op("vector", lambda e, co=co: e.tensor_scalar(c[:, co + 3:co + 4], c[:, co + 1:co + 2], 1.0 / D, None, op0=ALU.mult),
                  waits=[tk2, tk], sem=SC)
        tk = P.op("vector", lambda e, co=co: e.tensor_tensor(c[:, co + 4:co + 5], c[:, co + 2:co + 3], c[:, co + 2:co + 3], op=ALU.mult),
                  waits=[tk], sem=SC)
        tk = P.op("vector", lambda e, co=co: e.tensor_tensor(c[:, co + 4:co + 5], c[:, co + 3:co + 4], c[:, co + 4:co + 5], op=ALU.subtract),
                  waits=[tk], sem=SC)
        tk = P.op("vector", lambda e, co=co: e.tensor_scalar(c[:, co + 4:co + 5], c[:, co + 4:co + 5], EPS, None, op0=ALU.add),
                  waits=[tk], sem=SC)
        tk = P.op("scalar", lambda e, co=co: e.activation(c[:, co + 4:co + 5], c[:, co + 4:co + 5], AF.Sqrt),
                  waits=[tk], sem=SC)
        tk = P.op("vector", lambda e, co=co: e.reciprocal(c[:, co + 4:co + 5], c[:, co + 4:co + 5]),
                  waits=[tk], sem=SC)
        tk = P.op("vector", lambda e, co=co: e.tensor_tensor(c[:, co + 5:co + 6], c[:, co + 2:co + 3], c[:, co + 4:co + 5], op=ALU.mult),
                  waits=[tk], sem=SC)
        tk = P.op("vector", lambda e, co=co: e.tensor_scalar(c[:, co + 5:co + 6], c[:, co + 5:co + 6], -1.0, None, op0=ALU.mult),
                  waits=[tk], sem=SC)
        tk_ln = P.op("scalar", lambda e, b=b, co=co: e.activation(s["lnbuf"][b][:, :], s["xbuf"][b][:, :], AF.Identity,
                                                                  bias=c[:, co + 5:co + 6], scale=c[:, co + 4:co + 5]),
                     waits=[tk, P.tok(t + "lnout")], sem=t + f"lnfree{b}")
        P.op("scalar", lambda e, i=i, b=b: e.dma_start(s["myLN"][i * 128:(i + 1) * 128, :], s["lnbuf"][b][:, :]),
             waits=[tk_ln], sem=t + "lnout", dma=True)
        if i % 8 == 7:
            ln_chunk_toks.append(P.tok(t + "lnout"))

    lnall_toks = []
    for kch in range(8):
        lnall_toks.append(P.op("gpsimd", lambda e, kch=kch: e.collective_compute(
            "AllGather", ALU.bypass, replica_groups=RG,
            ins=[s["myLN"][kch * 1024:(kch + 1) * 1024, :].opt()],
            outs=[s["LNall"][kch * 8192:(kch + 1) * 8192, :].opt()]),
            waits=[ln_chunk_toks[kch]], sem=SCC))

    base_lnout = P.tok(t + "lnout")[1] - 64 * 16
    prev_logit = None
    prev_ltm = None
    for g in range(NGRP):
        wln = [(t + "lnout@scalar", base_lnout + 16 * 4 * (g + 1))]
        mm_toks = []
        for kd in range(8):
            b = kd % 2
            tk_l = P.op("sync", lambda e, g=g, kd=kd, b=b: e.dma_start(
                s["lnfm"][b][:, :], s["myLN"][g * 512:(g + 1) * 512, kd * 128:(kd + 1) * 128], transpose=True),
                waits=wln + [P.tok(t + f"rfree{b}")], sem=t + f"lfm{b}", dma=True)
            mm_toks.append(P.op("tensor", lambda e, kd=kd, b=b: e.matmul(
                s["pE"][0:8, :], s["wsb"][:, kd * 8:(kd + 1) * 8], s["lnfm"][b][:, :],
                start=(kd == 0), stop=(kd == 7)),
                waits=[tk_l] + ([prev_logit] if kd == 0 and prev_logit else []), sem=t + f"rfree{b}"))
        tk = P.op("scalar", lambda e: e.activation(s["logit_bf"][0:8, :], s["pE"][0:8, :], AF.Identity,
                                                   bias=s["bscol"][0:8, 0:1]),
                  waits=mm_toks[-2:], sem=SEV)
        prev_logit = tk
        tr_toks = []
        for a4 in range(4):
            tr_toks.append(P.op("tensor", lambda e, a4=a4: e.transpose(
                s["pT"][:, a4 * 8:(a4 + 1) * 8], s["logit_bf"][0:8, a4 * 128:(a4 + 1) * 128], s["identbf"][0:8, 0:8]),
                waits=[tk] + ([prev_ltm] if a4 == 0 and prev_ltm else []), sem=SMM))
        tk = P.op("vector", lambda e: e.tensor_copy(s["ltm"][:, 0:32], s["pT"][:, 0:32]), waits=tr_toks, sem=SEV)
        prev_ltm = tk
        tk = P.op("vector", lambda e: e.reduce_max(s["red1"][:, 0:4], v3e(s["ltm"][:, 0:32]), axis=AX.X),
                  waits=[tk], sem=SC)
        tk = P.op("vector", lambda e: e.tensor_tensor(
            v3e(s["ltm"][:, 32:64]), v3e(s["ltm"][:, 0:32]),
            s["red1"][:, 0:4].rearrange("p (a one) -> p a one", one=1).to_broadcast([128, 4, 8]), op=ALU.subtract),
            waits=[tk], sem=SC)
        tk = P.op("scalar", lambda e: e.activation(s["ltm"][:, 32:64], s["ltm"][:, 32:64], AF.Exp), waits=[tk], sem=SC)
        tk = P.op("vector", lambda e: e.reduce_sum(s["red1"][:, 4:8], v3e(s["ltm"][:, 32:64]), axis=AX.X),
                  waits=[tk], sem=SC)
        tk = P.op("vector", lambda e: e.reciprocal(s["red1"][:, 4:8], s["red1"][:, 4:8]), waits=[tk], sem=SC)
        probs_done.append(P.op("vector", lambda e, g=g: e.tensor_tensor(
            v3e(s["probs_tm"][:, g * 32:(g + 1) * 32]), v3e(s["ltm"][:, 32:64]),
            s["red1"][:, 4:8].rearrange("p (a one) -> p a one", one=1).to_broadcast([128, 4, 8]), op=ALU.mult),
            waits=[tk], sem=t + "probs"))

    tk_pw = P.op("scalar", lambda e: e.dma_start(
        s["probsMine"][:, :].rearrange("(a p) e -> p a e", p=128), v3e(s["probs_tm"][:, :])),
        waits=probs_done, sem=SD, dma=True)

    tk = P.op("gpsimd", lambda e: e.collective_compute(
        "AllGather", ALU.bypass, replica_groups=RG,
        ins=[s["probsMine"][:, :].opt()], outs=[s["probsAll"][:, :].opt()]),
        waits=[tk_pw], sem=SCC)
    tk_pa = P.op("sync", lambda e: e.dma_start(
        v3e(s["pAll"][:, :]), s["probsAll"][:, :].rearrange("(a p) e -> p a e", p=128)),
        waits=[tk], sem=SD, dma=True)
    tk = P.op("vector", lambda e: e.memset(s["plane"][:, :], 0.0), waits=[tk_pa], sem=SC)
    for j in range(EXP):
        tk2 = P.op("vector", lambda e, j=j: e.tensor_scalar(
            s["cmpA"][:, :], v3e(s["pAll"][:, :])[:, :, j:j + 1].rearrange("p a 1 -> p a"),
            s["ehot"][:, j:j + 1], None, op0=ALU.mult), waits=[tk_pa] if j == 0 else [tk], sem=SC)
        tk = P.op("vector", lambda e: e.tensor_tensor(s["plane"][:, :], s["plane"][:, :], s["cmpA"][:, :],
                                                      op=ALU.add), waits=[tk, tk2], sem=SC)
    tk_plane = tk

    bis = s["bis"]
    lo, hi, th, pred = bis[:, 0:1], bis[:, 1:2], bis[:, 2:3], bis[:, 3:4]
    tk1 = P.op("vector", lambda e: e.memset(lo, 0.0), waits=[tk_plane], sem=SC)
    tk2 = P.op("vector", lambda e: e.memset(hi, 1.0), sem=SC)
    tk = P.op("vector", lambda e: e.memset(th, 0.5), sem=SC)
    for it in range(NBISECT):
        tk = P.op("vector", lambda e: e.tensor_scalar(s["cmpA"][:, :], s["plane"][:, :], th[:, 0:1], None, op0=ALU.is_gt),
                  waits=[tk, tk1, tk2] if it == 0 else [tk], sem=SC)
        tk = P.op("vector", lambda e: e.reduce_sum(s["cols"][:, 16:17], s["cmpA"][:, :], axis=AX.X), waits=[tk], sem=SC)
        tk = P.op("tensor", lambda e: e.matmul(s["pF"][:, 0:1], s["onesff"][:, :], s["cols"][:, 16:17],
                                               start=True, stop=True), waits=[tk], sem=SMM)
        tk = P.op("vector", lambda e: e.tensor_scalar(pred, s["pF"][:, 0:1], KSEL - 0.5, None, op0=ALU.is_gt),
                  waits=[tk], sem=SC)
        d1, d2 = bis[:, 4:5], bis[:, 5:6]
        tka = P.op("vector", lambda e: e.tensor_tensor(d1, th, lo, op=ALU.subtract), waits=[tk], sem=SC)
        tka = P.op("vector", lambda e: e.tensor_tensor(d1, d1, pred, op=ALU.mult), waits=[tka], sem=SC)
        tkb = P.op("vector", lambda e: e.tensor_tensor(d2, hi, th, op=ALU.subtract), waits=[tk], sem=SC)
        tkb = P.op("vector", lambda e: e.tensor_tensor(d2, d2, pred, op=ALU.mult), waits=[tkb], sem=SC)
        tk1 = P.op("vector", lambda e: e.tensor_tensor(lo, lo, d1, op=ALU.add), waits=[tka], sem=SC)
        tk2 = P.op("vector", lambda e: e.tensor_tensor(hi, th, d2, op=ALU.add), waits=[tkb, tk1], sem=SC)
        tk = P.op("vector", lambda e: e.tensor_tensor(th, lo, hi, op=ALU.add), waits=[tk1, tk2], sem=SC)
        tk = P.op("vector", lambda e: e.tensor_scalar(th, th, 0.5, None, op0=ALU.mult), waits=[tk], sem=SC)
    tk_lo = tk1
    tk = P.op("scalar", lambda e: e.dma_start(s["thMine"][:, :], lo[0:1, 0:1]), waits=[tk, tk_lo], sem=SD, dma=True)
    tk = P.op("gpsimd", lambda e: e.collective_compute(
        "AllGather", ALU.bypass, replica_groups=RG,
        ins=[s["thMine"][:, :].opt()], outs=[s["thAll"][:, :].opt()]),
        waits=[tk], sem=SCC)
    tk = P.op("sync", lambda e: e.dma_start(s["throw"][0:1, 0:8], s["thAll"][:, :].rearrange("e one -> (one e)").rearrange("(o e) -> o e", o=1)),
              waits=[tk], sem=SD, dma=True)
    tk = P.op("tensor", lambda e: e.matmul(s["pF"][:, 8:16], s["onesrowf"][0:1, :], s["throw"][0:1, 0:8],
                                           start=True, stop=True), waits=[tk], sem=SMM)
    tk_thc = P.op("vector", lambda e: e.tensor_copy(s["thcols"][:, :], s["pF"][:, 8:16]), waits=[tk], sem=SC)

    sD_toks = []
    tk_free = None
    prev_sD = None
    for ee in range(EXP):
        tk = P.op("vector", lambda e, ee=ee: e.tensor_scalar(
            s["selbf"][:, :], v3e(s["pAll"][:, :])[:, :, ee:ee + 1].rearrange("p a 1 -> p a"),
            s["thcols"][:, ee:ee + 1], None, op0=ALU.is_gt),
            waits=[tk_thc, tk_pa] + ([tk_free] if tk_free else []), sem=SC)
        tk_cnt = P.op("tensor", lambda e: e.matmul(s["pE"][0:1, 0:512], s["onescolb"][:, 0:1], s["selbf"][:, :],
                                                   start=True, stop=True), waits=[tk], sem=SMM)
        tk = P.op("vector", lambda e: e.tensor_copy(s["cntrow"][0:1, :], s["pE"][0:1, 0:512]), waits=[tk_cnt], sem=SC)
        tk_free = tk
        tk = P.op("vector", lambda e: e.tensor_tensor_scan(s["srow"][0:1, 1:513], s["cntrow"][0:1, :],
                                                           s["zrow"][0:1, 0:512], 0.0, op0=ALU.add, op1=ALU.add),
                  waits=[tk, s["_tmp"]["crel"][1]] + ([prev_sD] if prev_sD else []), sem=SC)
        tkz = P.op("vector", lambda e: e.memset(s["srow"][0:1, 0:1], 0.0), waits=[tk], sem=SC)
        prev_sD = P.op("scalar", lambda e, ee=ee: e.dma_start(s["Srows"][ee:ee + 1, 0:640], s["srow"][0:1, 0:640]),
                       waits=[tk, tkz], sem=SD, dma=True)
        sD_toks.append(prev_sD)

    tk_sel = P.op("vector", lambda e: e.tensor_scalar(s["selbf"][:, :], s["plane"][:, :], lo[:, 0:1], None, op0=ALU.is_gt),
                  waits=[tk_lo, tk_free], sem=SC)
    tk_r = P.op("tensor", lambda e: e.matmul(s["pG"][:, 0:512], s["lstrictb"][:, :], s["selbf"][:, :],
                                             start=True, stop=True), waits=[tk_sel], sem=SMM)
    tk_rv = P.op("vector", lambda e: e.tensor_copy(s["rmine"][:, :], s["pG"][:, 0:512]), waits=[tk_r], sem=SC)
    tk_cnt = P.op("tensor", lambda e: e.matmul(s["pE"][0:1, 0:512], s["onescolb"][:, 0:1], s["selbf"][:, :],
                                               start=True, stop=True), waits=[tk_sel, tk_rv], sem=SMM)
    tk_cv = P.op("vector", lambda e: e.tensor_copy(s["cntrow"][0:1, :], s["pE"][0:1, 0:512]),
                 waits=[tk_cnt, prev_sD], sem=SC)
    tk = P.op("vector", lambda e: e.tensor_tensor_scan(s["srow"][0:1, 1:513], s["cntrow"][0:1, :],
                                                       s["zrow"][0:1, 0:512], 0.0, op0=ALU.add, op1=ALU.add),
              waits=[tk_cv], sem=SC)
    tkz = P.op("vector", lambda e: e.memset(s["srow"][0:1, 0:1], 0.0), waits=[tk], sem=SC)
    tk_sown = P.op("vector", lambda e: e.tensor_copy(s["sown"][0:1, 0:513], s["srow"][0:1, 0:513]),
                   waits=[tk, tkz], sem=SC)

    tk_gs = P.op("vector", lambda e: e.tensor_tensor(s["gidsel"][:, :], s["gid"][:, :], s["selbf"][:, :], op=ALU.mult),
                 waits=[tk_sel, s["_tmp"]["crel"][0]], sem=SC)
    last_pg = None
    pg_dma = []
    for k in range(KCAP):
        tk = P.op("vector", lambda e, k=k: e.tensor_scalar(s["cmpA"][:, :], s["rmine"][:, :], float(k), None, op0=ALU.is_equal),
                  waits=[tk_rv, tk_gs] if k == 0 else [last_pg], sem=SC)
        tk = P.op("vector", lambda e: e.tensor_tensor(s["cmpB"][:, :], s["cmpA"][:, :], s["gidsel"][:, :], op=ALU.mult),
                  waits=[tk], sem=SC)
        tk = P.op("tensor", lambda e: e.matmul(s["pE"][0:1, 0:512], s["onescolf"][:, 0:1], s["cmpB"][:, :],
                                               start=True, stop=True), waits=[tk], sem=SMM)
        tk = P.op("vector", lambda e, k=k: e.tensor_copy(s["srow"][0:1, 0:512], s["pE"][0:1, 0:512]),
                  waits=[tk, tk_sown] + pg_dma[-1:], sem=SC)
        last_pg = tk
        pg_dma.append(P.op("scalar", lambda e, k=k: e.dma_start(
            s["pagesKB"][k * 512:(k + 1) * 512, :].rearrange("r one -> (one r)").rearrange("(o r) -> o r", o=1),
            s["srow"][0:1, 0:512]),
            waits=[tk], sem=SD, dma=True))

    w0 = [tk_sown, tk_cv, last_pg]
    tk_ne = P.op("vector", lambda e: e.tensor_scalar(s["row1"][0:1, :], s["cntrow"][0:1, :], 0.0, None, op0=ALU.is_gt),
                 waits=w0, sem=SC)
    tk = P.op("vector", lambda e: e.tensor_scalar(s["row2"][0:1, :], s["cntrow"][0:1, :], -1.0, float(KCAP),
                                                  op0=ALU.mult, op1=ALU.add), waits=w0, sem=SC)
    tk = P.op("vector", lambda e: e.tensor_tensor_scan(s["row3"][0:1, :], s["row2"][0:1, :], s["zrow"][0:1, :], 0.0,
                                                       op0=ALU.add, op1=ALU.add), waits=[tk], sem=SC)
    tkA = P.op("vector", lambda e: e.tensor_copy(s["row2"][0:1, 1:512], s["row3"][0:1, 0:511]), waits=[tk], sem=SC)
    tkB = P.op("vector", lambda e: e.memset(s["row2"][0:1, 0:1], 0.0), waits=[tk], sem=SC)
    tk_V = P.op("vector", lambda e: e.tensor_copy(s["row2"][0:1, 0:1], s["row2"][0:1, 0:1]), waits=[tkA, tkB], sem=SC)
    tk_d0 = P.op("vector", lambda e: e.tensor_scalar(s["row3"][0:1, :], s["row1"][0:1, :], -1.0, 1.0,
                                                     op0=ALU.mult, op1=ALU.add), waits=[tk_ne, tkA], sem=SC)
    tk_d1 = P.op("vector", lambda e: e.tensor_tensor(s["row4"][0:1, :], s["row2"][0:1, :], s["row1"][0:1, :],
                                                     op=ALU.mult), waits=[tk_ne, tk_V], sem=SC)
    tk = P.op("vector", lambda e: e.tensor_tensor_scan(s["row5"][0:1, :], s["row3"][0:1, :], s["row4"][0:1, :], 0.0,
                                                       op0=ALU.mult, op1=ALU.add), waits=[tk_d0, tk_d1], sem=SC)
    tkA = P.op("vector", lambda e: e.tensor_copy(s["row3"][0:1, 1:512], s["row5"][0:1, 0:511]), waits=[tk], sem=SC)
    tkB = P.op("vector", lambda e: e.memset(s["row3"][0:1, 0:1], 0.0), waits=[tk, tkA], sem=SC)
    tk_J = P.op("vector", lambda e: e.tensor_tensor(s["row4"][0:1, :], s["row2"][0:1, :], s["row3"][0:1, :],
                                                    op=ALU.subtract), waits=[tkA, tkB, tk_V, tk_d1], sem=SC)
    tkd0 = P.op("scalar", lambda e: e.dma_start(s["rowD"][0:1, :], s["row4"][0:1, :]), waits=[tk_J], sem=SD, dma=True)
    tkd1 = P.op("scalar", lambda e: e.dma_start(s["rowD"][1:2, :], s["sown"][0:1, 0:512]),
                waits=[tk_sown, tkd0], sem=SD, dma=True)
    tkd2 = P.op("scalar", lambda e: e.dma_start(s["rowD"][2:3, :], s["row1"][0:1, :]),
                waits=[tk_ne, tkd1], sem=SD, dma=True)
    tk_jt = P.op("sync", lambda e: e.dma_start(s["jtile"][:, 0:4], s["rowD"][0:1, :].rearrange("one (p x) -> p (x one)", p=128)),
                 waits=[tkd0], sem=SD, dma=True)
    tk_st = P.op("sync", lambda e: e.dma_start(s["stile"][:, 0:4], s["rowD"][1:2, :].rearrange("one (p x) -> p (x one)", p=128)),
                 waits=[tkd1], sem=SD, dma=True)
    tk_mt = P.op("sync", lambda e: e.dma_start(s["mtile"][:, 0:4], s["rowD"][2:3, :].rearrange("one (p x) -> p (x one)", p=128)),
                 waits=[tkd2], sem=SD, dma=True)
    tk = P.op("vector", lambda e: e.tensor_scalar(s["postile"][:, 0:4], s["stile"][:, 0:4], 1.0 / 64.0, 0.5 / 64.0 - 0.5,
                                                  op0=ALU.mult, op1=ALU.add), waits=[tk_st], sem=SC)
    tk = P.op("vector", lambda e: e.tensor_copy(s["positile"][:, 0:4], s["postile"][:, 0:4]), waits=[tk], sem=SC)
    tk = P.op("vector", lambda e: e.tensor_copy(s["postile"][:, 0:4], s["positile"][:, 0:4]), waits=[tk], sem=SC)
    tk = P.op("vector", lambda e: e.tensor_tensor(s["postile"][:, 0:4], s["postile"][:, 0:4], s["stile"][:, 0:4],
                                                  op=ALU.add), waits=[tk], sem=SC)
    tk = P.op("vector", lambda e: e.tensor_scalar(s["postile"][:, 0:4], s["postile"][:, 0:4], -8319.0, None, op0=ALU.add),
              waits=[tk], sem=SC)
    tk = P.op("vector", lambda e: e.tensor_tensor(s["postile"][:, 0:4], s["postile"][:, 0:4], s["mtile"][:, 0:4],
                                                  op=ALU.mult), waits=[tk, tk_mt], sem=SC)
    tk = P.op("vector", lambda e: e.tensor_scalar(s["postile"][:, 0:4], s["postile"][:, 0:4], 8319.0, None, op0=ALU.add),
              waits=[tk], sem=SC)
    tk_pi32 = P.op("vector", lambda e: e.tensor_copy(s["positile"][:, 0:4], s["postile"][:, 0:4]),
                   waits=[tk], sem=SC)
    tk = P.op("vector", lambda e: e.memset(s["ztile"][:, :], 0.0), waits=[tk_pi32], sem=SC)
    tk_za = P.op("scalar", lambda e: e.dma_start(s["piAdjD"][:, :].rearrange("(p g) one -> p g", p=128),
                                                 s["ztile"][:, 0:65]), waits=[tk], sem=SD, dma=True)
    sc_toks = []
    for x in range(4):
        sc_toks.append(P.op("gpsimd", lambda e, x=x: e.indirect_dma_start(
            s["piAdjD"][:, :], IndirectOffsetOnAxis(s["positile"][:, x:x + 1], axis=0),
            s["jtile"][:, x:x + 1], None),
            waits=[tk_pi32, tk_za, tk_jt] + sc_toks[-1:], sem=SD, dma=True))
    tk_lA = P.op("sync", lambda e: e.dma_start(s["atile"][:, 0:65],
                                               s["piAdjD"][:, :].rearrange("(p g) one -> p g", p=128)),
                 waits=sc_toks, sem=SD, dma=True)
    tk = P.op("vector", lambda e: e.tensor_tensor_scan(s["A65"][:, 0:65], s["atile"][:, 0:65], s["ztile"][:, 0:65], 0.0,
                                                       op0=ALU.add, op1=ALU.add), waits=[tk_lA], sem=SC)
    tk_tot = P.op("vector", lambda e: e.tensor_copy(s["cols"][:, 17:18], s["A65"][:, 64:65]), waits=[tk], sem=SC)
    tk_pf = P.op("tensor", lambda e: e.matmul(s["pF"][:, 24:25], s["lstrictf"][:, :], s["cols"][:, 17:18],
                                              start=True, stop=True), waits=[tk_tot], sem=SMM)
    tk_off = P.op("vector", lambda e: e.tensor_copy(s["cols"][:, 18:19], s["pF"][:, 24:25]), waits=[tk_pf], sem=SC)
    tk = P.op("vector", lambda e: e.tensor_scalar(s["A65"][:, 0:64], s["A65"][:, 0:64], s["cols"][:, 18:19], None, op0=ALU.add),
              waits=[tk_off, tk_tot], sem=SC)
    tk = P.op("vector", lambda e: e.tensor_tensor(s["A65"][:, 0:64], s["A65"][:, 0:64], s["piog"][:, :], op=ALU.add),
              waits=[tk, s["_tmp"]["crel"][2]], sem=SC)
    tk = P.op("vector", lambda e: e.tensor_scalar(s["A65"][:, 0:64], s["A65"][:, 0:64], float(512 * KCAP - 1), None, op0=ALU.min),
              waits=[tk], sem=SC)
    tk = P.op("vector", lambda e: e.tensor_scalar(s["A65"][:, 0:64], s["A65"][:, 0:64], 0.0, None, op0=ALU.max),
              waits=[tk], sem=SC)
    tk = emit_mod(P, s, s["B65"][:, :], s["A65"][:, 0:64], KCAP, s["pii"][:, :], [tk], SC)
    tk = P.op("vector", lambda e: e.tensor_tensor(s["C65"][:, :], s["A65"][:, 0:64], s["B65"][:, :], op=ALU.subtract),
              waits=[tk], sem=SC)
    tk = P.op("vector", lambda e: e.tensor_scalar(s["C65"][:, :], s["C65"][:, :], 1.0 / KCAP, None, op0=ALU.mult),
              waits=[tk], sem=SC)
    tk = P.op("vector", lambda e: e.tensor_scalar(s["B65"][:, :], s["B65"][:, :], 512.0, None, op0=ALU.mult),
              waits=[tk], sem=SC)
    tk = P.op("vector", lambda e: e.tensor_tensor(s["C65"][:, :], s["C65"][:, :], s["B65"][:, :], op=ALU.add),
              waits=[tk], sem=SC)
    tk_pii = P.op("vector", lambda e: e.tensor_copy(s["pii"][:, :], s["C65"][:, :]), waits=[tk], sem=SC)

    idv_toks = []
    for g in range(64):
        idv_toks.append(P.op("gpsimd", lambda e, g=g: e.indirect_dma_start(
            s["idvf"][:, g:g + 1], None, s["pagesKB"][:, :], IndirectOffsetOnAxis(s["pii"][:, g:g + 1], axis=0)),
            waits=([tk_pii] + pg_dma) if g == 0 else [], sem=SD, dma=True))
    tk = emit_mod(P, s, s["w64a"][:, :], s["idvf"][:, :], 8192, s["pii"][:, :], idv_toks, SC)
    tk = P.op("vector", lambda e: e.tensor_tensor(s["w64b"][:, :], s["idvf"][:, :], s["w64a"][:, :], op=ALU.subtract),
              waits=[tk], sem=SC)
    tk = P.op("vector", lambda e: e.tensor_scalar(s["w64b"][:, :], s["w64b"][:, :], 1.0 / 8192.0, None, op0=ALU.mult),
              waits=[tk], sem=SC)
    tk = emit_mod(P, s, s["w64c"][:, :], s["w64a"][:, :], 1024, s["pii"][:, :], [tk], SC)
    tk = P.op("vector", lambda e: e.tensor_tensor(s["w64a"][:, :], s["w64a"][:, :], s["w64c"][:, :], op=ALU.subtract),
              waits=[tk], sem=SC)
    tk = P.op("vector", lambda e: e.tensor_scalar(s["w64a"][:, :], s["w64a"][:, :], 8.0, None, op0=ALU.mult),
              waits=[tk], sem=SC)
    tk = P.op("vector", lambda e: e.tensor_scalar(s["w64b"][:, :], s["w64b"][:, :], 1024.0, None, op0=ALU.mult),
              waits=[tk], sem=SC)
    tk = P.op("vector", lambda e: e.tensor_tensor(s["w64a"][:, :], s["w64a"][:, :], s["w64b"][:, :], op=ALU.add),
              waits=[tk], sem=SC)
    tk = P.op("vector", lambda e: e.tensor_tensor(s["w64a"][:, :], s["w64a"][:, :], s["w64c"][:, :], op=ALU.add),
              waits=[tk], sem=SC)
    tk = P.op("vector", lambda e: e.tensor_scalar(s["w64a"][:, :], s["w64a"][:, :], float(N - 1), None, op0=ALU.min),
              waits=[tk], sem=SC)
    tk = P.op("vector", lambda e: e.tensor_scalar(s["w64a"][:, :], s["w64a"][:, :], 0.0, None, op0=ALU.max),
              waits=[tk], sem=SC)
    tk_rows = P.op("vector", lambda e: e.tensor_copy(s["rowsi"][:, :], s["w64a"][:, :]), waits=[tk], sem=SC)

    xg_toks = []
    for g in range(64):
        b = g % 2
        tka = P.op("gpsimd", lambda e, g=g, b=b: e.indirect_dma_start(
            s["gbuf"][b][:, :], None, s["LNall"][:, :], IndirectOffsetOnAxis(s["rowsi"][:, g:g + 1], axis=0)),
            waits=([tk_rows] + lnall_toks if g <= 1 else []) + ([xg_toks[-2]] if len(xg_toks) >= 2 else []),
            sem=t + "xg" + str(b), dma=True)
        xg_toks.append(P.op("gpsimd", lambda e, g=g, b=b: e.dma_start(
            s["CX"][:, :].rearrange("(p g) d -> p g d", p=128)[:, g:g + 1, :].rearrange("p 1 d -> p d"),
            s["gbuf"][b][:, :]), waits=[tka], sem=t + "xg" + str(b), dma=True))
    all_xg = [P.tok(t + "xg0"), P.tok(t + "xg1")]

    co_group_done = []
    prev_out2_last = None
    prev_evac = dict(s["_tmp"].get("Fevac", {}))
    for g in range(NGRP):
        b = g % 2
        ld_toks = []
        for kd in range(8):
            ld_toks.append(P.op("sync", lambda e, g=g, kd=kd, b=b: e.dma_start(
                s["xfm"][b][:, kd * 512:(kd + 1) * 512],
                s["CX"][g * 512:(g + 1) * 512, kd * 128:(kd + 1) * 128], transpose=True),
                waits=all_xg + [P.tok(t + f"xffree{b}")], sem=t + f"xfl{b}", dma=True))
        gelu_toks = []
        for fc in range(32):
            pb = s["pA"] if fc % 2 == 0 else s["pB"]
            mmw = list(ld_toks) if fc < 2 else [gelu_toks[fc - 2]]
            if fc == 0 and prev_out2_last is not None:
                mmw.append(prev_out2_last)
            tkm = None
            for kd in range(8):
                tkm = P.op("tensor", lambda e, fc=fc, kd=kd, b=b, pb=pb: e.matmul(
                    pb[:, :], s["w1sb"][:, kd * 4096 + fc * 128: kd * 4096 + (fc + 1) * 128],
                    s["xfm"][b][:, kd * 512:(kd + 1) * 512], start=(kd == 0), stop=(kd == 7)),
                    waits=mmw if kd == 0 else [], sem=(SMM if kd == 7 else None))
            gelu_toks.append(P.op("scalar", lambda e, fc=fc, pb=pb: e.activation(
                s["hsb"][:, fc * 512:(fc + 1) * 512], pb[:, :], AF.Gelu_apprx_tanh,
                bias=s["b1sb"][:, fc:fc + 1]), waits=[tkm], sem=SEV))
        P.op("vector", lambda e: e.tensor_copy(s["dmy"][0:1, 0:1], s["dmy"][0:1, 0:1]),
             waits=[gelu_toks[-1]], sem=t + f"xffree{b}")
        out2_w = []
        for t4 in range(4):
            for dh in range(2):
                j = t4 * 2 + dh
                pc = s["pC"] if j % 2 == 0 else s["pD"]
                mmw = [gelu_toks[-1]]
                if (j % 2) in prev_evac:
                    mmw.append(prev_evac[j % 2])
                tkm = None
                for fc in range(32):
                    tkm = P.op("tensor", lambda e, fc=fc, t4=t4, dh=dh, pc=pc: e.matmul(
                        pc[:, :], s["hsb"][:, fc * 512 + t4 * 128: fc * 512 + (t4 + 1) * 128],
                        s["w2sb"][:, fc * 1024 + dh * 512: fc * 1024 + (dh + 1) * 512],
                        start=(fc == 0), stop=False), waits=mmw if fc == 0 else [], sem=None)
                tkm = P.op("tensor", lambda e, dh=dh, pc=pc: e.matmul(
                    pc[:, :], s["onesrowb"][0:1, :], s["b2sb"][0:1, dh * 512:(dh + 1) * 512],
                    start=False, stop=True), sem=SMM)
                cowtok = s["_tmp"].get("cowtok")
                tke = P.op("vector", lambda e, dh=dh, pc=pc: e.tensor_copy(
                    s["obufb"][:, dh * 512:(dh + 1) * 512], pc[:, :]),
                    waits=[tkm] + ([cowtok] if cowtok else []), sem=SEV)
                prev_evac[j % 2] = tke
                out2_w.append(tke)
                if dh == 1:
                    P.op("scalar", lambda e, g=g, t4=t4: e.dma_start(
                        s["compact"][g * 512 + t4 * 128: g * 512 + (t4 + 1) * 128, :], s["obufb"][:, :]),
                        waits=out2_w[-2:], sem=t + "cow", dma=True)
                    s["_tmp"]["cowtok"] = P.tok(t + "cow")
        prev_out2_last = out2_w[-1]
        co_group_done.append(P.tok(t + "cow"))
    s["_tmp"]["Fevac"] = prev_evac

    tk = P.op("vector", lambda e: e.memset(s["zbig"][:, :], 0.0), waits=[co_group_done[-1]], sem=SC)
    tk_zs = P.op("scalar", lambda e: e.dma_start(s["compact"][8192:CO_ROWS, :], s["zbig"][:, :]),
                 waits=[tk], sem=t + "cow", dma=True)

    ag_toks = []
    for kch in range(8):
        need_groups = min(NGRP, (CO_CH * (kch + 1) + 511) // 512)
        w = [co_group_done[need_groups - 1]]
        if kch == 7:
            w.append(tk_zs)
        ag_toks.append(P.op("gpsimd", lambda e, kch=kch: e.collective_compute(
            "AllGather", ALU.bypass, replica_groups=RG,
            ins=[s["compact"][kch * CO_CH:(kch + 1) * CO_CH, :].opt()],
            outs=[s["allOut"][kch * 8 * CO_CH:(kch + 1) * 8 * CO_CH, :].opt()]),
            waits=w, sem=SCC))

    hrel = []
    for cn, sn in [("c_emap", "emap"), ("c_eoff3", "eoff3"), ("c_zrowcol", "zrowcol"),
                   ("c_iotaslot", "iotaslot"), ("c_keslot3", "keslot3")]:
        hrel.append(P.op("scalar", lambda e, cn=cn, sn=sn: e.dma_start(s[sn][:, :], s[cn][:, :]),
                         waits=[co_group_done[-1]] if not hrel else [], sem=SD, dma=True))

    def ld_smine(e):
        pid = P.nc.partition_id([mybir.EngineType.SP])
        return e.dma_start(s["smine"][0:8, 0:64], s["Srows"][0:8, bass.ts(pid, 64)])

    def ld_smine2(e):
        pid = P.nc.partition_id([mybir.EngineType.SP])
        return e.dma_start(s["smine"][0:8, 64:128], s["Srows"][0:8, bass.ts(pid + 1, 64)])

    tk_sm = P.op("sync", ld_smine, waits=sD_toks + [co_group_done[-1]], sem=SD, dma=True)
    tk_sm2 = P.op("sync", ld_smine2, waits=[tk_sm], sem=SD, dma=True)

    tk_se = P.op("vector", lambda e: e.tensor_copy(
        s["sEven"][0:8, 0:33],
        s["smine"][0:8, 0:128].rearrange("e (b two) -> e b two", two=2)[:, 0:33, 0:1].rearrange("e b 1 -> e b")),
        waits=[tk_sm, tk_sm2], sem=SC)
    tk_cb = P.op("vector", lambda e: e.tensor_tensor(s["cEven"][0:8, 0:32], s["sEven"][0:8, 1:33],
                                                     s["sEven"][0:8, 0:32], op=ALU.subtract),
                 waits=[tk_se], sem=SC)
    sv_toks = []
    for gc in range(3):
        sv_toks.append(P.op("tensor", lambda e, gc=gc: e.matmul(
            s["pE"][:, gc * 32:(gc + 1) * 32], s["emap"][0:8, gc * 128:(gc + 1) * 128], s["sEven"][0:8, 0:32],
            start=True, stop=True), waits=[tk_se] + hrel, sem=SMM))
        sv_toks.append(P.op("tensor", lambda e, gc=gc: e.matmul(
            s["pF"][:, 128 + gc * 32:128 + (gc + 1) * 32], s["emap"][0:8, gc * 128:(gc + 1) * 128],
            s["cEven"][0:8, 0:32], start=True, stop=True), waits=[tk_cb], sem=SMM))
    tk_sv = P.op("vector", lambda e: e.tensor_copy(s["svals"][:, 0:96], s["pE"][:, 0:96]), waits=sv_toks, sem=SC)
    tk_cvv = P.op("vector", lambda e: e.tensor_copy(s["cvals"][:, 0:96], s["pF"][:, 128:224]), waits=sv_toks, sem=SC)
    kebc = s["keslot3"][:, 0:3].rearrange("p (g one) -> p g one", one=1).to_broadcast([128, 3, 32])
    v96 = lambda ap: ap.rearrange("p (g b) -> p g b", b=32)
    tk = P.op("vector", lambda e: e.tensor_tensor(v96(s["iwork"][:, 0:96]), v96(s["svals"][:, 0:96]), kebc, op=ALU.add),
              waits=[tk_sv] + hrel, sem=SC)
    tkp = P.op("vector", lambda e: e.tensor_tensor(v96(s["ipred"][:, 0:96]), kebc, v96(s["cvals"][:, 0:96]), op=ALU.is_lt),
               waits=[tk_cvv], sem=SC)
    tk = P.op("vector", lambda e: e.tensor_scalar(s["iwork2"][:, 0:96], s["iwork"][:, 0:96], float(ZROW), None, op0=ALU.min),
              waits=[tk], sem=SC)
    tk = P.op("vector", lambda e: e.tensor_tensor(s["iwork"][:, 0:96], s["iwork2"][:, 0:96], s["zrowcol"][:, 0:96],
                                                  op=ALU.subtract), waits=[tk, tkp], sem=SC)
    tk = P.op("vector", lambda e: e.tensor_tensor(s["iwork"][:, 0:96], s["iwork"][:, 0:96], s["ipred"][:, 0:96],
                                                  op=ALU.mult), waits=[tk], sem=SC)
    tk = P.op("vector", lambda e: e.tensor_tensor(s["iwork"][:, 0:96], s["iwork"][:, 0:96], s["zrowcol"][:, 0:96],
                                                  op=ALU.add), waits=[tk], sem=SC)
    tk = emit_mod(P, s, s["iwork2"][:, 0:96], s["iwork"][:, 0:96], CO_CH, s["idxslot"][:, 0:96], [tk], SC)
    tk = P.op("vector", lambda e: e.tensor_tensor(s["iwork3"][:, 0:96], s["iwork"][:, 0:96], s["iwork2"][:, 0:96],
                                                  op=ALU.subtract), waits=[tk], sem=SC)
    tk = P.op("vector", lambda e: e.tensor_scalar(s["iwork3"][:, 0:96], s["iwork3"][:, 0:96], 8.0, None, op0=ALU.mult),
              waits=[tk], sem=SC)
    tk = P.op("vector", lambda e: e.tensor_tensor(s["iwork2"][:, 0:96], s["iwork2"][:, 0:96], s["eoff3"][:, 0:96],
                                                  op=ALU.add), waits=[tk], sem=SC)
    tk = P.op("vector", lambda e: e.tensor_tensor(s["iwork"][:, 0:96], s["iwork3"][:, 0:96], s["iwork2"][:, 0:96],
                                                  op=ALU.add), waits=[tk], sem=SC)
    tk_idx = P.op("vector", lambda e: e.tensor_copy(s["idxslot"][:, 0:96], s["iwork"][:, 0:96]), waits=[tk], sem=SC)

    for blk in range(32):
        db = blk % 2
        g_toks = []
        for gc in range(3):
            g_toks.append(P.op("gpsimd", lambda e, blk=blk, gc=gc, db=db: e.indirect_dma_start(
                s["rslot"][db][:, gc * 1024:(gc + 1) * 1024], None, s["allOut"][:, :],
                IndirectOffsetOnAxis(s["idxslot"][:, gc * 32 + blk:gc * 32 + blk + 1], axis=0)),
                waits=([tk_idx] + ag_toks if blk <= 1 and gc == 0 else []) +
                      [P.tok(t + "rsfree" + str(db))],
                sem=t + "rg" + str(db), dma=True))
        emit_combine_block(P, s, t, blk, db, g_toks, Xin, Xout, tk_thc,
                           [tk_idx] if blk < 2 else [])

    return P.tok(t + "xow")


def emit_combine_block(P, s, t, blk, db, g_toks, Xin, Xout, tk_thc, w0):
    SD, SC, SMM, SEV = t + "d", t + "c", t + "mm", t + "ev"
    pt_toks = []
    half_done = []
    tk_sc8 = None
    for half in range(2):
        a = 2 * blk + half
        ho = half * 8
        tk = P.op("vector", lambda e, a=a, ho=ho: e.tensor_tensor(
            s["selc"][:, ho:ho + 8], s["probs_tm"][:, a * 8:(a + 1) * 8], s["thcols"][:, 0:8], op=ALU.is_gt),
            waits=[tk_thc, P.tok(t + "pfree")] + w0, sem=SC)
        tk_selbf = P.op("vector", lambda e, ho=ho: e.tensor_copy(s["selcb"][:, ho:ho + 8], s["selc"][:, ho:ho + 8]),
                        waits=[tk], sem=SC)
        tk_r = P.op("tensor", lambda e, ho=ho, half=half: e.matmul(
            s["pG"][:, ho:ho + 8], s["lstrictb"][:, :], s["selcb"][:, ho:ho + 8],
            start=True, stop=(half == 0)), waits=[tk_selbf], sem=SMM)
        if half == 0:
            tk_c = P.op("tensor", lambda e: e.matmul(s["pG"][0:1, 16:24], s["onescolb"][:, 0:1], s["selcb"][:, 0:8],
                                                     start=True, stop=True), waits=[tk_selbf], sem=SMM)
            tk_sc8 = P.op("vector", lambda e: e.tensor_copy(s["sc8"][0:1, 0:8], s["pG"][0:1, 16:24]),
                          waits=[tk_c], sem=SC)
        else:
            tk_r = P.op("tensor", lambda e: e.matmul(s["pG"][:, 8:16], s["onesrowb"][0:1, :], s["sc8"][0:1, 0:8],
                                                     start=False, stop=True), waits=[tk_sc8], sem=SMM)
        tk_rv = P.op("vector", lambda e, ho=ho: e.tensor_copy(s["slotof"][:, ho:ho + 8], s["pG"][:, ho:ho + 8]),
                     waits=[tk_r], sem=SC)
        tk = P.op("vector", lambda e, ho=ho: e.tensor_tensor(s["slotof"][:, ho:ho + 8], s["slotof"][:, ho:ho + 8],
                                                             s["selc"][:, ho:ho + 8], op=ALU.mult),
                  waits=[tk_rv], sem=SC)
        tk2 = P.op("vector", lambda e, ho=ho: e.tensor_scalar(s["sel4k"][:, ho:ho + 8], s["selc"][:, ho:ho + 8],
                                                              -4096.0, 4096.0, op0=ALU.mult, op1=ALU.add),
                   waits=[tk], sem=SC)
        tk = P.op("vector", lambda e, ho=ho: e.tensor_tensor(s["slotof"][:, ho:ho + 8], s["slotof"][:, ho:ho + 8],
                                                             s["sel4k"][:, ho:ho + 8], op=ALU.add),
                  waits=[tk2], sem=SC)
        pe_toks = []
        for ee in range(EXP):
            pe_toks.append(P.op("vector", lambda e, ee=ee, a=a, ho=ho: e.tensor_scalar(
                s["Ptile"][:, ee * SLOTC:(ee + 1) * SLOTC], s["iotaslot"][:, :], s["slotof"][:, ho + ee:ho + ee + 1],
                s["probs_tm"][:, a * 8 + ee:a * 8 + ee + 1], op0=ALU.is_equal, op1=ALU.mult),
                waits=[tk] + (pt_toks[-1:] if pe_toks == [] and pt_toks else []), sem=SC))
        for sc in range(3):
            tkt = P.op("tensor", lambda e, sc=sc: e.transpose(
                s["pT"][:, 0:128], s["Ptile"][:, sc * 128:(sc + 1) * 128], s["identbf"][:, :]),
                waits=pe_toks + (pt_toks[-1:] if pt_toks else []), sem=SMM)
            pt_toks.append(P.op("vector", lambda e, half=half, sc=sc: e.tensor_copy(
                s["PT"][half][:, sc * 128:(sc + 1) * 128], s["pT"][:, 0:128]), waits=[tkt], sem=SEV))
        half_done.append(pt_toks[-1])
    P.op("vector", lambda e: e.tensor_copy(s["dmy"][0:1, 0:1], s["dmy"][0:1, 0:1]),
         waits=half_done, sem=t + "pfree")

    tk_x1 = P.op("sync", lambda e, blk=blk, db=db: e.dma_start(s["xcomb"][db][:, :], Xin[blk * 256:blk * 256 + 128, :]),
                 waits=[P.tok(t + "xcfree" + str(db))] + w0, sem=t + f"xc{db}", dma=True)
    tk_x2 = P.op("sync", lambda e, blk=blk, db=db: e.dma_start(s["xcomb2"][db][:, :], Xin[blk * 256 + 128:(blk + 1) * 256, :]),
                 waits=list(w0), sem=t + f"xc{db}", dma=True)
    ev_toks = []
    for half in range(2):
        for dh in range(2):
            j = half * 2 + dh
            pc = s["pC"] if j % 2 == 0 else s["pD"]
            mmw = g_toks + [half_done[half]]
            wk = "Hpsum" + str(j % 2)
            if wk in s["_tmp"]:
                mmw.append(s["_tmp"][wk])
            tkm = None
            for sc in range(3):
                tkm = P.op("tensor", lambda e, half=half, dh=dh, sc=sc, pc=pc, db=db: e.matmul(
                    pc[:, :], s["PT"][half][:, sc * 128:(sc + 1) * 128],
                    s["rslot"][db][:, sc * 1024 + dh * 512: sc * 1024 + (dh + 1) * 512],
                    start=(sc == 0), stop=(sc == 2)),
                    waits=mmw if sc == 0 else [], sem=(SMM if sc == 2 else None))
            xb = s["xcomb"] if half == 0 else s["xcomb2"]
            tke = P.op("vector", lambda e, dh=dh, pc=pc, db=db, xb=xb: e.scalar_tensor_tensor(
                s["ocomb"][:, dh * 512:(dh + 1) * 512], pc[:, :], 0.5, xb[db][:, dh * 512:(dh + 1) * 512],
                op0=ALU.mult, op1=ALU.add),
                waits=[tkm, tk_x1 if half == 0 else tk_x2, P.tok(t + "xow")], sem=SEV)
            s["_tmp"][wk] = tke
            ev_toks.append(tke)
            if dh == 1:
                P.op("scalar", lambda e, blk=blk, half=half: e.dma_start(
                    Xout[blk * 256 + half * 128: blk * 256 + (half + 1) * 128, :], s["ocomb"][:, :]),
                    waits=ev_toks[-2:], sem=t + "xow", dma=True)
    P.op("vector", lambda e: e.tensor_copy(s["dmy"][0:1, 0:1], s["dmy"][0:1, 0:1]),
         waits=[s["_tmp"]["Hpsum0"], s["_tmp"]["Hpsum1"]], sem=t + "rsfree" + str(db))
    P.op("vector", lambda e: e.tensor_copy(s["dmy"][0:1, 0:1], s["dmy"][0:1, 0:1]),
         waits=ev_toks[-2:], sem=t + "xcfree" + str(db))


CH = 128
NCHUNK = 4096 // CH
GC = GAMMA ** CH


def emit_att(P, s, f1_done, dr):
    t = "T_"
    SC, SMM, SEV = t + "c", t + "mm", t + "ev"
    X1 = dr["X1"]
    X2 = dr["X2"]
    myLN = s["myLN"]

    wl = []
    wl.append(P.op("gpsimd", lambda e: e.dma_start(s["wqsb"][:, :], s["wqt"][:, :]), waits=[f1_done], sem=t + "wl", dma=True))
    wl.append(P.op("gpsimd", lambda e: e.dma_start(s["wksb"][:, :], s["wkt"][:, :]), sem=t + "wl", dma=True))
    wl.append(P.op("gpsimd", lambda e: e.dma_start(s["wvsb"][:, :], s["wvt"][:, :]), sem=t + "wl", dma=True))
    wl.append(P.op("gpsimd", lambda e: e.dma_start(s["wosb"][:, :], s["wot"][:, :]), sem=t + "wl", dma=True))

    c = s["cols"]
    for i in range(64):
        b = i % 2
        co = 8 * b
        tk_x = P.op("sync", lambda e, i=i, b=b: e.dma_start(s["xbuf"][b][:, :], X1[i * 128:(i + 1) * 128, :]),
                    waits=[f1_done] + [P.tok(t + f"lnfree{b}")], sem=t + f"ldx{b}", dma=True)
        tk1 = P.op("vector", lambda e, b=b, co=co: e.reduce_sum(c[:, co:co + 1], s["xbuf"][b][:, :], axis=AX.X),
                   waits=[tk_x], sem=SC)
        tk2 = P.op("scalar", lambda e, b=b, co=co: e.activation(s["lnscr"][:, :], s["xbuf"][b][:, :], AF.Square,
                                                                accum_out=c[:, co + 1:co + 2]),
                   waits=[tk_x], sem=SC)
        tk = P.op("vector", lambda e, co=co: e.tensor_scalar(c[:, co + 2:co + 3], c[:, co:co + 1], 1.0 / D, None, op0=ALU.mult),
                  waits=[tk1], sem=SC)
        tk = P.op("vector", lambda e, co=co: e.tensor_scalar(c[:, co + 3:co + 4], c[:, co + 1:co + 2], 1.0 / D, None, op0=ALU.mult),
                  waits=[tk2, tk], sem=SC)
        tk = P.op("vector", lambda e, co=co: e.tensor_tensor(c[:, co + 4:co + 5], c[:, co + 2:co + 3], c[:, co + 2:co + 3], op=ALU.mult),
                  waits=[tk], sem=SC)
        tk = P.op("vector", lambda e, co=co: e.tensor_tensor(c[:, co + 4:co + 5], c[:, co + 3:co + 4], c[:, co + 4:co + 5], op=ALU.subtract),
                  waits=[tk], sem=SC)
        tk = P.op("vector", lambda e, co=co: e.tensor_scalar(c[:, co + 4:co + 5], c[:, co + 4:co + 5], EPS, None, op0=ALU.add),
                  waits=[tk], sem=SC)
        tk = P.op("scalar", lambda e, co=co: e.activation(c[:, co + 4:co + 5], c[:, co + 4:co + 5], AF.Sqrt),
                  waits=[tk], sem=SC)
        tk = P.op("vector", lambda e, co=co: e.reciprocal(c[:, co + 4:co + 5], c[:, co + 4:co + 5]),
                  waits=[tk], sem=SC)
        tk = P.op("vector", lambda e, co=co: e.tensor_tensor(c[:, co + 5:co + 6], c[:, co + 2:co + 3], c[:, co + 4:co + 5], op=ALU.mult),
                  waits=[tk], sem=SC)
        tk = P.op("vector", lambda e, co=co: e.tensor_scalar(c[:, co + 5:co + 6], c[:, co + 5:co + 6], -1.0, None, op0=ALU.mult),
                  waits=[tk], sem=SC)
        tk_ln = P.op("scalar", lambda e, b=b, co=co: e.activation(s["lnbuf"][b][:, :], s["xbuf"][b][:, :], AF.Identity,
                                                                  bias=c[:, co + 5:co + 6], scale=c[:, co + 4:co + 5]),
                     waits=[tk, P.tok(t + "ln")], sem=t + f"lnfree{b}")
        P.op("scalar", lambda e, i=i, b=b: e.dma_start(myLN[i * 128:(i + 1) * 128, :], s["lnbuf"][b][:, :]),
             waits=[tk_ln], sem=t + "ln", dma=True)
    base_ln = P.tok(t + "ln")[1] - 64 * 16

    prev_evac = {}
    prev_sp = {}
    wsbs = {0: "wqsb", 1: "wksb", 2: "wvsb"}
    fmds = {0: "QfmD", 1: "KfmD", 2: "VfmD"}
    for g in range(NGRP):
        pb = g % 2
        lnp = s["lnpA"] if pb == 0 else s["lnpB"]
        wln = [(t + "ln@scalar", base_ln + 16 * 4 * (g + 1))]
        ld_toks = []
        for dc in range(8):
            ld_toks.append(P.op("sync", lambda e, g=g, dc=dc, lnp=lnp: e.dma_start(
                lnp[:, dc * 512:(dc + 1) * 512],
                myLN[g * 512:(g + 1) * 512, dc * 128:(dc + 1) * 128], transpose=True),
                waits=wln + wl + ([P.tok(t + f"prel{pb}")] if g >= 2 else []), sem=t + f"pl{pb}", dma=True))
        mm_last = []
        for j in range(12):
            pj, hdc = j // 4, j % 4
            psn = "pF" if j % 2 == 0 else "pG"
            spb = s["spA"] if j % 2 == 0 else s["spB"]
            mmw = list(ld_toks)
            if (j % 2) in prev_evac:
                mmw.append(prev_evac[j % 2])
            tkm = None
            for dc in range(8):
                tkm = P.op("tensor", lambda e, pj=pj, hdc=hdc, dc=dc, psn=psn, lnp=lnp: e.matmul(
                    s[psn][:, :], s[wsbs[pj]][:, dc * 512 + hdc * 128: dc * 512 + (hdc + 1) * 128],
                    lnp[:, dc * 512:(dc + 1) * 512], start=(dc == 0), stop=(dc == 7)),
                    waits=mmw if dc == 0 else [], sem=(SMM if dc == 7 else None))
            mm_last.append(tkm)
            tke = P.op("vector", lambda e, psn=psn, spb=spb: e.tensor_copy(spb[:, :], s[psn][:, :]),
                       waits=[tkm] + ([prev_sp[j % 2]] if (j % 2) in prev_sp else []), sem=SEV)
            prev_evac[j % 2] = tke
            prev_sp[j % 2] = P.op("scalar", lambda e, pj=pj, hdc=hdc, g=g, spb=spb: e.dma_start(
                s[fmds[pj]][hdc * 128:(hdc + 1) * 128, g * 512:(g + 1) * 512], spb[:, :]),
                waits=[tke], sem=t + "sp", dma=True)
        P.op("vector", lambda e: e.tensor_copy(s["dmy"][0:1, 2:3], s["dmy"][0:1, 0:1]),
             waits=mm_last[-2:], sem=t + f"prel{pb}")
    proj_done = P.tok(t + "sp")
    ml = []
    for cn, sn in [("c_mA4", "mA4"), ("c_mA24", "mA24"), ("c_mB2p4", "mB2p4"), ("c_mX124", "mX124"),
                   ("c_mB2T4", "mB2T4"), ("c_mD", "mD"), ("c_mqd", "mqd"), ("c_mblk", "mblk"),
                   ("c_kgcol", "kgcol")]:
        ml.append(P.op("sync", lambda e, cn=cn, sn=sn: e.dma_start(s[sn][:, :], s[cn][:, :]),
                       waits=[proj_done, P.tok(t + "prel0"), P.tok(t + "prel1")] if not ml else [],
                       sem=t + "ml", dma=True))

    pA, pB, pC, pD, pE, pF2, pG2 = s["pA"], s["pB"], s["pC"], s["pD"], s["pE"], s["pF"], s["pG"]
    v4b = lambda ap: ap.rearrange("p (s i) -> p s i", i=CH)
    m1b = lambda ap: ap.rearrange("p (one i) -> p one i", one=1).to_broadcast([128, 4, CH])
    m1b32 = lambda ap: ap.rearrange("p (one i) -> p one i", one=1).to_broadcast([32, 4, CH])
    T = s["_tmp"]
    for k in ["aQK1", "aQK2", "aP12", "aO", "aPE", "aPF", "aPG", "aOF"]:
        T.pop(k, None)
    ST = {}
    for gp in range(4):
        for sl in (0, 1):
            g8 = 2 * gp + sl
            tkz = P.op("vector", lambda e, g8=g8: e.memset(s["stm"][0:32, g8 * 384:(g8 + 1) * 384], 0.0),
                       waits=[proj_done] + ([T["aSb" + str(sl)]] if ("aSb" + str(sl)) in T else []), sem=SC)
            tkz2 = P.op("vector", lambda e, sl=sl: e.memset(s[f"Sb4{sl}"][0:32, :], 0.0), waits=[tkz], sem=SC)
            tkz2 = P.op("vector", lambda e, sl=sl: e.memset(s[f"Cb4{sl}"][0:32, :], 0.0), waits=[tkz2], sem=SC)
            ST[sl] = P.op("vector", lambda e, sl=sl: e.memset(s[f"Gnb4{sl}"][0:32, :], 0.0), waits=[tkz2], sem=SC)
        for ck in range(NCHUNK):
          for sl in (0, 1):
            g8 = 2 * gp + sl
            seq, hg = g8 // 4, g8 % 4
            db = sl
            tk_st0 = ST[sl]
            tok0 = seq * 4096 + ck * CH
            Sb4_, Cb4_, Gnb4_ = s[f"Sb4{sl}"], s[f"Cb4{sl}"], s[f"Gnb4{sl}"]
            lw = ([proj_done] + ml if gp == 0 and ck == 0 else []) + \
                 ([P.tok(t + f"lr{db}")] if ck >= 1 else [])
            l_toks = []
            for st4 in range(4):
                r0 = 128 * hg + 32 * st4
                l_toks.append(P.op("sync", lambda e, st4=st4, r0=r0, tok0=tok0, db=db: e.dma_start(
                    s[f"qT4{db}"][0:32, st4 * CH:(st4 + 1) * CH], s["QfmD"][r0:r0 + 32, tok0:tok0 + CH]),
                    waits=lw if st4 == 0 else [], sem=t + f"ld{db}", dma=True))
                l_toks.append(P.op("sync", lambda e, st4=st4, r0=r0, tok0=tok0, db=db: e.dma_start(
                    s[f"kT4{db}"][0:32, st4 * CH:(st4 + 1) * CH], s["KfmD"][r0:r0 + 32, tok0:tok0 + CH]),
                    sem=t + f"ld{db}", dma=True))
                l_toks.append(P.op("sync", lambda e, st4=st4, r0=r0, tok0=tok0, db=db: e.dma_start(
                    s[f"q4tm{db}"][:, st4 * 32:(st4 + 1) * 32], s["QfmD"][r0:r0 + 32, tok0:tok0 + CH], transpose=True),
                    sem=t + f"ld{db}", dma=True))
                l_toks.append(P.op("sync", lambda e, st4=st4, r0=r0, tok0=tok0, db=db: e.dma_start(
                    s[f"k4tm{db}"][:, st4 * 32:(st4 + 1) * 32], s["KfmD"][r0:r0 + 32, tok0:tok0 + CH], transpose=True),
                    sem=t + f"ld{db}", dma=True))
                l_toks.append(P.op("sync", lambda e, st4=st4, r0=r0, tok0=tok0, db=db: e.dma_start(
                    s[f"v4tm{db}"][:, st4 * 32:(st4 + 1) * 32], s["VfmD"][r0:r0 + 32, tok0:tok0 + CH], transpose=True),
                    sem=t + f"ld{db}", dma=True))
            ld_tok = l_toks[-1]
            qT4, kT4 = s[f"qT4{db}"], s[f"kT4{db}"]
            q4tm, k4tm, v4tm = s[f"q4tm{db}"], s[f"k4tm{db}"], s[f"v4tm{db}"]

            qk1 = []
            for st4 in range(4):
                qk1.append(P.op("tensor", lambda e, st4=st4, qT4=qT4, kT4=kT4: e.matmul(
                    pA[:, st4 * CH:(st4 + 1) * CH], kT4[0:32, st4 * CH:(st4 + 1) * CH],
                    qT4[0:32, st4 * CH:(st4 + 1) * CH], start=True, stop=True),
                    waits=([ld_tok] if st4 == 0 else []) + ([T["aQK1"]] if st4 == 0 and "aQK1" in T else []) +
                          ([T["aBp"]] if st4 == 0 and "aBp" in T else []),
                    sem=SMM))
            qk2 = []
            for st4 in range(4):
                qk2.append(P.op("tensor", lambda e, st4=st4, qT4=qT4, kT4=kT4: e.matmul(
                    pB[:, st4 * CH:(st4 + 1) * CH], qT4[0:32, st4 * CH:(st4 + 1) * CH],
                    kT4[0:32, st4 * CH:(st4 + 1) * CH], start=True, stop=True),
                    waits=([T["aQK2"]] if st4 == 0 and "aQK2" in T else []), sem=SMM))
            qpp = []
            for st4 in range(4):
                qpp.append(P.op("tensor", lambda e, st4=st4, qT4=qT4, Sb4_=Sb4_: e.matmul(
                    pE[0:32, st4 * CH:(st4 + 1) * CH], Sb4_[0:32, st4 * 32:(st4 + 1) * 32],
                    qT4[0:32, st4 * CH:(st4 + 1) * CH], start=True, stop=True),
                    waits=([tk_st0] if st4 == 0 else []) + ([T["aPE"]] if st4 == 0 and "aPE" in T else []),
                    sem=SMM))

            e_bp = P.op("scalar", lambda e: e.activation(s["Bp4"][:, :], pA[:, :], AF.Copy), waits=qk1, sem=SEV)
            T["aBp"] = e_bp
            tk = None
            for nm, msk in [("Ap4", "mA4"), ("A2n4", "mA24"), ("B2p4", "mB2p4"), ("X12_4", "mX124")]:
                tk = P.op("vector", lambda e, nm=nm, msk=msk: e.tensor_tensor(
                    s[nm][:, :], pA[:, :], s[msk][:, :], op=ALU.mult),
                    waits=qk1 + [e_bp] if tk is None else [], sem=SC)
            e_x12 = tk
            T["aQK1"] = e_x12
            e_b2t = P.op("vector", lambda e: e.tensor_tensor(
                s["B2T4"][:, :], pB[:, :], s["mB2T4"][:, :], op=ALU.mult),
                waits=qk2 + [e_x12], sem=SC)
            T["aQK2"] = e_b2t
            e_qpp = P.op("vector", lambda e: e.tensor_scalar(s["QppT4"][0:32, :], pE[0:32, :], GAMMA, None, op0=ALU.mult),
                         waits=qpp + [e_b2t], sem=SC)

            p12 = []
            for st4 in range(4):
                sl = slice(st4 * CH, (st4 + 1) * CH)
                w0 = [e_bp, e_x12, e_qpp] + ([T["aP12"]] if "aP12" in T else []) if st4 == 0 else []
                P.op("tensor", lambda e, sl=sl: e.matmul(pC[:, sl], s["Bp4"][:, sl], s["Ap4"][:, sl],
                                                         start=True, stop=False), waits=w0, sem=None)
                P.op("tensor", lambda e, sl=sl, qT4=qT4: e.matmul(pC[:, sl], qT4[0:32, sl], s["QppT4"][0:32, sl],
                                                                  start=False, stop=False), sem=None)
                p12.append(P.op("tensor", lambda e, sl=sl: e.matmul(pC[:, sl], s["B2p4"][:, sl], s["A2n4"][:, sl],
                                                                    start=False, stop=True), sem=SMM))
            tk = None
            for st4 in range(4):
                sl = slice(st4 * CH, (st4 + 1) * CH)
                tk = P.op("vector", lambda e, sl=sl: e.tensor_tensor(
                    s["DT4"][:, sl], pC[:, sl], s["mD"][:, :], op=ALU.mult),
                    waits=p12 if tk is None else [], sem=SC)
            e_dt = tk
            T["aP12"] = e_dt

            u_mm = []
            for st4 in range(4):
                u_mm.append(P.op("tensor", lambda e, st4=st4, kT4=kT4, Cb4_=Cb4_: e.matmul(
                    pF2[:, 32 * st4:32 * (st4 + 1)], kT4[0:32, st4 * CH:(st4 + 1) * CH],
                    Cb4_[0:32, st4 * 32:(st4 + 1) * 32], start=True, stop=True),
                    waits=([tk_st0] + ([T["aPF"]] if "aPF" in T else []) if st4 == 0 else []), sem=SMM))
            e_u = P.op("scalar", lambda e: e.activation(s["U4"][:, 0:128], pF2[:, 0:128], AF.Copy),
                       waits=u_mm, sem=SEV)
            z_mm = []
            for st4 in range(4):
                z_mm.append(P.op("tensor", lambda e, st4=st4, Sb4_=Sb4_, Cb4_=Cb4_: e.matmul(
                    pG2[0:32, 32 * st4:32 * (st4 + 1)], Sb4_[0:32, st4 * 32:(st4 + 1) * 32],
                    Cb4_[0:32, st4 * 32:(st4 + 1) * 32], start=True, stop=True),
                    waits=([tk_st0] + ([T["aPG"]] if "aPG" in T else []) if st4 == 0 else []), sem=SMM))
            e_z = P.op("scalar", lambda e: e.activation(s["Z4"][0:32, 0:128], pG2[0:32, 0:128], AF.Copy),
                       waits=z_mm, sem=SEV)
            e_qd1 = P.op("vector", lambda e, qT4=qT4: e.tensor_tensor(
                v4b(s["Qd1T4"][0:32, :]), v4b(qT4[0:32, :]), m1b32(s["mqd"][0:32, 0:CH]), op=ALU.mult),
                waits=[ld_tok] + ([T["aO"]] if "aO" in T else []), sem=SC)
            e_qd2 = P.op("vector", lambda e, qT4=qT4: e.tensor_tensor(
                v4b(s["Qd2T4"][0:32, :]), v4b(qT4[0:32, :]), m1b32(s["mqd"][0:32, CH:2 * CH]), op=ALU.mult),
                waits=[e_qd1], sem=SC)
            T["aQd"] = e_qd2

            o_mm = []
            w0 = [e_dt, e_u, e_z, e_qd2] + ([T["aO"]] if "aO" in T else [])
            P.op("tensor", lambda e, v4tm=v4tm: e.matmul(pD[:, :], v4tm[:, :], s["DT4"][:, :],
                                                         start=True, stop=False), waits=w0, sem=None)
            P.op("tensor", lambda e: e.matmul(pD[:, :], s["U4"][:, 0:128], s["X12_4"][:, :],
                                              start=False, stop=False), sem=None)
            P.op("tensor", lambda e: e.matmul(pD[:, :], s["Z4"][0:32, 0:128], s["Qd2T4"][0:32, :],
                                              start=False, stop=False), sem=None)
            o_mm.append(P.op("tensor", lambda e, Gnb4_=Gnb4_: e.matmul(pD[:, :], Gnb4_[0:32, :], s["Qd1T4"][0:32, :],
                                                                       start=False, stop=True), sem=SMM))
            e_om = P.op("vector", lambda e: e.tensor_tensor(s["Om4"][:, :], pD[:, :], s["mblk"][:, :], op=ALU.mult),
                        waits=o_mm + ([T["aOF"]] if "aOF" in T else []), sem=SC)
            T["aO"] = e_om
            def _fold(e):
                with P.nc.allow_low_precision(reason="band-masked sum: 3 of 4 addends are exactly zero"):
                    return e.reduce_sum(s["ofm"][:, :], s["Om4"][:, :].rearrange("p (s i) -> p i s", s=4), axis=AX.X)
            e_of = P.op("vector", _fold, waits=[e_om] + ([T["aOFD"]] if "aOFD" in T else []), sem=SC)
            T["aOF"] = e_of
            T["aOFD"] = P.op("scalar", lambda e, hg=hg, tok0=tok0: e.dma_start(
                s["OFmD"][128 * hg:128 * (hg + 1), tok0:tok0 + CH], s["ofm"][:, :]),
                waits=[e_of], sem=t + "of", dma=True)

            e_kg = P.op("vector", lambda e, k4tm=k4tm: e.tensor_tensor(
                s["Kg4"][:, :], k4tm[:, :], s["kgcol"][:, 0:1].to_broadcast([128, 128]), op=ALU.mult),
                waits=[ld_tok] + (list(T["aKgR"]) if "aKgR" in T else []), sem=SC)
            e_qg = P.op("vector", lambda e, q4tm=q4tm: e.tensor_tensor(
                s["Qg4"][:, :], q4tm[:, :], s["kgcol"][:, 0:1].to_broadcast([128, 128]), op=ALU.mult),
                waits=[e_kg], sem=SC)
            mm_s = P.op("tensor", lambda e, k4tm=k4tm: e.matmul(pE[:, 0:128], s["Kg4"][:, :], k4tm[:, :],
                                                                start=True, stop=True), waits=[e_qpp, e_kg], sem=SMM)
            mm_c = P.op("tensor", lambda e, v4tm=v4tm: e.matmul(pE[:, 128:256], s["Qg4"][:, :], v4tm[:, :],
                                                                start=True, stop=True), waits=[e_qg], sem=SMM)
            mm_kk = P.op("tensor", lambda e, k4tm=k4tm: e.matmul(pE[:, 256:384], k4tm[:, :], k4tm[:, :],
                                                                 start=True, stop=True), sem=SMM)
            T["aKgR"] = (mm_s, mm_c)
            kkev = []
            for st4 in range(4):
                kkev.append(P.op("vector", lambda e, st4=st4: e.tensor_scalar(
                    s["KTKg4"][0:32, st4 * 32:(st4 + 1) * 32], pE[32 * st4:32 * (st4 + 1), 256 + 32 * st4:256 + 32 * (st4 + 1)],
                    GC, None, op0=ALU.mult), waits=[mm_kk] if st4 == 0 else [], sem=SC))
            b2v_mm = []
            for st4 in range(4):
                b2v_mm.append(P.op("tensor", lambda e, st4=st4, v4tm=v4tm: e.matmul(
                    pG2[:, 128 + 32 * st4:128 + 32 * (st4 + 1)], s["B2T4"][:, st4 * CH:(st4 + 1) * CH],
                    v4tm[:, st4 * 32:(st4 + 1) * 32], start=True, stop=True),
                    waits=[e_z, e_b2t] if st4 == 0 else [], sem=SMM))
            e_b2v = P.op("scalar", lambda e: e.activation(s["B2V4"][:, :], pG2[:, 128:256], AF.Copy),
                         waits=b2v_mm, sem=SEV)
            mm_t2 = P.op("tensor", lambda e, k4tm=k4tm: e.matmul(pF2[:, 128:256], k4tm[:, :], s["B2V4"][:, :],
                                                                 start=True, stop=True), waits=[e_u, e_b2v], sem=SMM)
            t3_mm = []
            for st4 in range(4):
                t3_mm.append(P.op("tensor", lambda e, st4=st4, Cb4_=Cb4_: e.matmul(
                    pF2[0:32, 256 + 32 * st4:256 + 32 * (st4 + 1)], s["KTKg4"][0:32, st4 * 32:(st4 + 1) * 32],
                    Cb4_[0:32, st4 * 32:(st4 + 1) * 32], start=True, stop=True),
                    waits=kkev if st4 == 0 else [], sem=SMM))
            stw = []
            for st4 in range(4):
                mo = (g8 * 4 + st4) * 96
                stw.append(P.op("vector", lambda e, st4=st4, mo=mo: e.scalar_tensor_tensor(
                    s["stm"][0:32, mo:mo + 32], s["stm"][0:32, mo:mo + 32], GC,
                    pE[32 * st4:32 * (st4 + 1), 32 * st4:32 * (st4 + 1)], op0=ALU.mult, op1=ALU.add),
                    waits=[mm_kk] if st4 == 0 else [], sem=SC))
                stw.append(P.op("vector", lambda e, st4=st4, mo=mo: e.scalar_tensor_tensor(
                    s["stm"][0:32, mo + 32:mo + 64], s["stm"][0:32, mo + 32:mo + 64], GC,
                    pE[32 * st4:32 * (st4 + 1), 128 + 32 * st4:128 + 32 * (st4 + 1)], op0=ALU.mult, op1=ALU.add),
                    waits=[], sem=SC))
                stw.append(P.op("vector", lambda e, st4=st4, mo=mo: e.scalar_tensor_tensor(
                    s["stm"][0:32, mo + 64:mo + 96], s["stm"][0:32, mo + 64:mo + 96], GC,
                    pF2[32 * st4:32 * (st4 + 1), 128 + 32 * st4:128 + 32 * (st4 + 1)], op0=ALU.mult, op1=ALU.subtract),
                    waits=[t3_mm[3]] if st4 == 0 else [], sem=SC))
                stw.append(P.op("vector", lambda e, st4=st4, mo=mo: e.tensor_tensor(
                    s["stm"][0:32, mo + 64:mo + 96], s["stm"][0:32, mo + 64:mo + 96],
                    pF2[0:32, 256 + 32 * st4:256 + 32 * (st4 + 1)], op=ALU.subtract),
                    waits=[t3_mm[st4]], sem=SC))
            T["aPE"] = stw[-3]
            T["aPF"] = stw[-1]
            T["aPG"] = e_b2v
            cpw = stw + [e_om]
            for st4 in range(4):
                mo = (g8 * 4 + st4) * 96
                tkc = P.op("vector", lambda e, st4=st4, mo=mo, Sb4_=Sb4_: e.tensor_copy(
                    Sb4_[0:32, st4 * 32:(st4 + 1) * 32], s["stm"][0:32, mo:mo + 32]),
                    waits=cpw if st4 == 0 else [], sem=SC)
                tkc = P.op("vector", lambda e, st4=st4, mo=mo, Cb4_=Cb4_: e.tensor_copy(
                    Cb4_[0:32, st4 * 32:(st4 + 1) * 32], s["stm"][0:32, mo + 32:mo + 64]), sem=SC)
                tkc = P.op("vector", lambda e, st4=st4, mo=mo, Gnb4_=Gnb4_: e.tensor_copy(
                    Gnb4_[0:32, st4 * 32:(st4 + 1) * 32], s["stm"][0:32, mo + 64:mo + 96]), sem=SC)
            ST[sl] = tkc
            T["aSb" + str(sl)] = tkc
            P.op("vector", lambda e: e.tensor_copy(s["dmy"][0:1, 3:4], s["dmy"][0:1, 0:1]),
                 waits=[e_om, stw[-1], e_qd2, e_qg], sem=t + f"lr{db}")

    scan_done = P.tok(t + "of")

    prev_ev = {}
    for i in range(64):
        db = i % 2
        tk_x = P.op("sync", lambda e, i=i, db=db: e.dma_start(s["xbuf"][db][:, :], X1[i * 128:(i + 1) * 128, :]),
                    waits=[scan_done] + ([P.tok(t + f"cr{db}")] if i >= 2 else []), sem=t + f"xl{db}", dma=True)
        o_toks = []
        for hgg in range(4):
            o_toks.append(P.op("sync", lambda e, i=i, hgg=hgg, db=db: e.dma_start(
                s[f"of4{db}"][:, hgg * 128:(hgg + 1) * 128], s["OFmD"][128 * hgg:128 * (hgg + 1), i * 128:(i + 1) * 128]),
                sem=t + f"xl{db}", dma=True))
        ev2 = []
        for dh in range(2):
            psn = "pF" if dh == 0 else "pG"
            tkm = None
            for hgg in range(4):
                w0 = []
                if hgg == 0:
                    w0 = o_toks + wl + ([prev_ev[dh]] if dh in prev_ev else [])
                    if i == 0:
                        w0 += [T[k] for k in ("aPF", "aPG") if k in T]
                tkm = P.op("tensor", lambda e, hgg=hgg, dh=dh, psn=psn, db=db: e.matmul(
                    s[psn][:, :], s[f"of4{db}"][:, hgg * 128:(hgg + 1) * 128],
                    s["wosb"][:, hgg * 1024 + dh * 512: hgg * 1024 + (dh + 1) * 512],
                    start=(hgg == 0), stop=(hgg == 3)),
                    waits=w0, sem=(SMM if hgg == 3 else None))
            tke = P.op("vector", lambda e, dh=dh, psn=psn, db=db: e.tensor_tensor(
                s["ocomb"][:, dh * 512:(dh + 1) * 512], s[psn][:, :], s["xbuf"][db][:, dh * 512:(dh + 1) * 512],
                op=ALU.add), waits=[tkm, tk_x, P.tok(t + "xw")], sem=SEV)
            prev_ev[dh] = tke
            ev2.append(tke)
        P.op("scalar", lambda e, i=i: e.dma_start(X2[i * 128:(i + 1) * 128, :], s["ocomb"][:, :]),
             waits=ev2, sem=t + "xw", dma=True)
        P.op("vector", lambda e: e.tensor_copy(s["dmy"][0:1, 3:4], s["dmy"][0:1, 0:1]),
             waits=ev2, sem=t + f"cr{db}")
    return P.tok(t + "xw")


def build(nc):
    import contextlib
    stack = contextlib.ExitStack()

    x0 = nc.declare_dram_parameter("x0", [NT, D], F32, isOutput=False)
    w1t = nc.declare_dram_parameter("w1t", [128, 32768], F32, isOutput=False)
    w2t = nc.declare_dram_parameter("w2t", [128, 32768], F32, isOutput=False)
    b1t = nc.declare_dram_parameter("b1t", [128, 32], F32, isOutput=False)
    b2r = nc.declare_dram_parameter("b2r", [1, D], F32, isOutput=False)
    wst = nc.declare_dram_parameter("wst", [128, 64], F32, isOutput=False)
    bsI = nc.declare_dram_parameter("bsI", [8, 1], F32, isOutput=False)
    ehotI = nc.declare_dram_parameter("ehotI", [128, 8], F32, isOutput=False)
    xout = nc.declare_dram_parameter("xout", [NT, D], F32, isOutput=True)
    wqt = nc.declare_dram_parameter("wqt", [128, 4096], F32, isOutput=False)
    wkt = nc.declare_dram_parameter("wkt", [128, 4096], F32, isOutput=False)
    wvt = nc.declare_dram_parameter("wvt", [128, 4096], F32, isOutput=False)
    wot = nc.declare_dram_parameter("wot", [128, 4096], F32, isOutput=False)
    cdefs = [
        ("c_mA4", [128, 512], F32), ("c_mA24", [128, 512], F32), ("c_mB2p4", [128, 512], F32),
        ("c_mX124", [128, 512], F32), ("c_mB2T4", [128, 512], F32), ("c_mD", [128, 128], F32),
        ("c_mqd", [32, 256], BF16), ("c_mblk", [128, 512], F32), ("c_kgcol", [128, 1], BF16),
        ("c_lstrictb", [128, 128], BF16), ("c_lstrictf", [128, 128], F32),
        ("c_identbf", [128, 128], BF16), ("c_onescolb", [128, 1], BF16),
        ("c_onescolf", [128, 1], F32), ("c_onesrowb", [1, 128], BF16), ("c_onesrowf", [1, 128], F32),
        ("c_onesff", [128, 128], F32),
        ("c_iotaslot", [128, SLOTC], F32), ("c_gid", [128, 512], F32),
        ("c_piog", [128, 64], F32),
        ("c_keslot3", [128, 3], F32), ("c_eoff3", [128, 96], F32), ("c_zrowcol", [128, 96], F32),
        ("c_emap", [8, 384], F32), ("c_zrow", [1, 512], F32),
    ]
    consts = {}
    for name, shape, dt in cdefs:
        consts[name] = nc.declare_dram_parameter(name, shape, dt, isOutput=False)

    dr = {}
    dr["X1"] = nc.dram_tensor("X1", [NT, D], F32)
    dr["X2"] = nc.dram_tensor("X2", [NT, D], F32)
    dr["myLN"] = nc.dram_tensor("myLN", [NT, D], BF16)
    dr["LNall"] = nc.dram_tensor("LNall", [N, D], BF16)
    dr["probsMine"] = nc.dram_tensor("probsMine", [NT, 8], F32)
    dr["probsAll"] = nc.dram_tensor("probsAll", [N, 8], F32)
    dr["thMine"] = nc.dram_tensor("thMine", [1, 1], F32)
    dr["thAll"] = nc.dram_tensor("thAll", [8, 1], F32)
    dr["Srows"] = nc.dram_tensor("Srows", [8, 640], F32)
    dr["pagesKB"] = nc.dram_tensor("pagesKB", [KCAP * 512, 1], F32)
    dr["piAdjD"] = nc.dram_tensor("piAdjD", [128 * 65, 1], F32)
    dr["rowD"] = nc.dram_tensor("rowD", [3, 512], F32)
    dr["CX"] = nc.dram_tensor("CX", [NT, D], BF16)
    dr["compact"] = nc.dram_tensor("compact", [CO_ROWS, D], BF16)
    dr["allOut"] = nc.dram_tensor("allOut", [8 * CO_ROWS, D], BF16)
    dr["QfmD"] = nc.dram_tensor("QfmD", [512, NT], BF16)
    dr["KfmD"] = nc.dram_tensor("KfmD", [512, NT], BF16)
    dr["VfmD"] = nc.dram_tensor("VfmD", [512, NT], BF16)
    dr["OFmD"] = nc.dram_tensor("OFmD", [512, NT], BF16)

    sb = {}

    def S(name, shape, dt):
        sb[name] = stack.enter_context(nc.sbuf_tensor(name, shape, dt))

    S("w1sb", [128, 32768], BF16)
    S("w2sb", [128, 32768], BF16)
    hsb_off = nc.sbuf_base
    S("hsb", [128, 16384], BF16)
    A = lambda name, shape, dt, off: sb.__setitem__(name, nc.alloc_sbuf_tensor_at(name, shape, dt, offset=off))
    A("pAll", [128, 4096], F32, hsb_off)
    A("Ptile", [128, NSLOT], BF16, hsb_off + 0)
    A("PT0", [128, NSLOT], BF16, hsb_off + 768)
    A("PT1", [128, NSLOT], BF16, hsb_off + 1536)
    A("selc", [128, 16], F32, hsb_off + 2304)
    A("selcb", [128, 16], BF16, hsb_off + 2368)
    A("slotof", [128, 16], F32, hsb_off + 2400)
    A("sel4k", [128, 16], F32, hsb_off + 2464)
    A("sc8", [1, 8], BF16, hsb_off + 2528)
    A("iwork2", [128, 96], F32, hsb_off + 2560)
    A("iwork3", [128, 96], F32, hsb_off + 2944)
    A("ipred", [128, 96], F32, hsb_off + 3328)
    A("idxslot", [128, 96], I32, hsb_off + 3712)
    A("smine", [8, 128], F32, hsb_off + 4096)
    A("sEven", [8, 34], F32, hsb_off + 4608)
    A("cEven", [8, 32], F32, hsb_off + 4768)
    A("emap", [8, 384], F32, hsb_off + 4896)
    A("eoff3", [128, 96], F32, hsb_off + 6432)
    A("zrowcol", [128, 96], F32, hsb_off + 6816)
    A("iotaslot", [128, SLOTC], F32, hsb_off + 7200)
    A("keslot3", [128, 3], F32, hsb_off + 7392)
    A("gbuf0", [128, 1024], BF16, hsb_off + 8192)
    A("gbuf1", [128, 1024], BF16, hsb_off + 10240)
    A("xcomb21", [128, 1024], F32, hsb_off + 8192)
    A("cmpA", [128, 512], F32, hsb_off + 16384)
    A("cmpB", [128, 512], F32, hsb_off + 18432)
    A("rmine", [128, 512], F32, hsb_off + 20480)
    A("selbf", [128, 512], BF16, hsb_off + 22528)
    A("row1", [1, 512], F32, hsb_off + 23552)
    A("row2", [1, 512], F32, hsb_off + 25600)
    A("row3", [1, 512], F32, hsb_off + 27648)
    A("row4", [1, 512], F32, hsb_off + 29696)
    xfm_off = nc.sbuf_base
    S("xfm0", [128, 4096], BF16)
    A("srow", [1, 640], F32, xfm_off)
    A("cntrow", [1, 512], F32, xfm_off + 2560)
    A("sown", [1, 513], F32, xfm_off + 4608)
    A("svals", [128, 96], F32, xfm_off + 6688)
    A("cvals", [128, 96], F32, xfm_off + 7072)
    A("iwork", [128, 96], F32, xfm_off + 7456)
    rslot_off = nc.sbuf_base
    S("rslot0", [128, 3072], BF16)
    S("rslot1", [128, 3072], BF16)
    A("gid", [128, 512], F32, rslot_off)
    A("zrow", [1, 512], F32, rslot_off + 2048)
    A("row5", [1, 512], F32, rslot_off + 4096)
    A("ztile", [128, 65], F32, rslot_off + 6144)
    A("atile", [128, 65], F32, rslot_off + 6432)
    A("A65", [128, 65], F32, rslot_off + 6720)
    A("B65", [128, 64], F32, rslot_off + 7008)
    A("C65", [128, 64], F32, rslot_off + 7264)
    A("pii", [128, 64], I32, rslot_off + 7520)
    A("idvf", [128, 64], F32, rslot_off + 7776)
    A("w64a", [128, 64], F32, rslot_off + 8032)
    A("w64b", [128, 64], F32, rslot_off + 8288)
    A("w64c", [128, 64], F32, rslot_off + 8544)
    A("rowsi", [128, 64], I32, rslot_off + 8800)
    A("piog", [128, 64], F32, rslot_off + 9056)
    A("logit_bf", [8, 512], BF16, rslot_off + 9312)
    A("lnfm0", [128, 512], BF16, rslot_off + 10336)
    A("ltm", [128, 64], F32, rslot_off + 11360)
    A("red1", [128, 8], F32, rslot_off + 11616)
    xbuf_off = nc.sbuf_base
    S("xbuf0", [128, 1024], F32)
    S("xbuf1", [128, 1024], F32)
    A("xfm1", [128, 4096], BF16, xbuf_off)
    A("lnfm1", [128, 512], BF16, hsb_off + 14336)
    A("lnpA", [128, 4096], BF16, hsb_off + 0)
    A("lnpB", [128, 4096], BF16, rslot_off + 0)
    A("wqsb", [128, 4096], BF16, hsb_off + 8192)
    A("wksb", [128, 4096], BF16, hsb_off + 16384)
    A("wvsb", [128, 4096], BF16, hsb_off + 24576)
    A("wosb", [128, 4096], BF16, xfm_off)
    A("Bp4", [128, 512], BF16, hsb_off + 0)
    A("Ap4", [128, 512], BF16, hsb_off + 1024)
    A("A2n4", [128, 512], BF16, hsb_off + 2048)
    A("B2p4", [128, 512], BF16, hsb_off + 3072)
    A("X12_4", [128, 512], BF16, hsb_off + 4096)
    A("B2T4", [128, 512], BF16, hsb_off + 5120)
    A("QppT4", [32, 512], BF16, hsb_off + 6144)
    A("DT4", [128, 512], BF16, hsb_off + 7168)
    A("U4", [128, 128], BF16, hsb_off + 8192)
    A("Z4", [32, 128], BF16, hsb_off + 8448)
    A("Qd1T4", [32, 512], BF16, hsb_off + 8704)
    A("Qd2T4", [32, 512], BF16, hsb_off + 9728)
    A("Kg4", [128, 128], BF16, hsb_off + 10752)
    A("Qg4", [128, 128], BF16, hsb_off + 11008)
    A("B2V4", [128, 128], BF16, hsb_off + 11264)
    A("KTKg4", [32, 128], BF16, hsb_off + 11520)
    A("Sb40", [32, 128], BF16, hsb_off + 11776)
    A("Cb40", [32, 128], BF16, hsb_off + 12032)
    A("Gnb40", [32, 128], BF16, hsb_off + 12288)
    A("Sb41", [32, 128], BF16, hsb_off + 28672)
    A("Cb41", [32, 128], BF16, hsb_off + 28928)
    A("Gnb41", [32, 128], BF16, hsb_off + 29184)
    A("Om4", [128, 512], BF16, hsb_off + 12544)
    A("ofm", [128, 128], BF16, hsb_off + 13568)
    A("qT40", [32, 512], BF16, hsb_off + 13824)
    A("qT41", [32, 512], BF16, hsb_off + 14848)
    A("kT40", [32, 512], BF16, hsb_off + 15872)
    A("kT41", [32, 512], BF16, hsb_off + 16896)
    A("q4tm0", [128, 128], BF16, hsb_off + 17920)
    A("q4tm1", [128, 128], BF16, hsb_off + 18176)
    A("k4tm0", [128, 128], BF16, hsb_off + 18432)
    A("k4tm1", [128, 128], BF16, hsb_off + 18688)
    A("v4tm0", [128, 128], BF16, hsb_off + 18944)
    A("v4tm1", [128, 128], BF16, hsb_off + 19200)
    A("of40", [128, 512], BF16, hsb_off + 0)
    A("of41", [128, 512], BF16, hsb_off + 1024)
    A("stm", [32, 3072], F32, rslot_off + 0)
    A("mA4", [128, 512], F32, hsb_off + 22528)
    A("mA24", [128, 512], F32, hsb_off + 24576)
    A("mB2p4", [128, 512], F32, hsb_off + 26624)
    A("mX124", [128, 512], F32, hsb_off + 29440)
    A("mqd", [32, 256], BF16, hsb_off + 19456)
    A("kgcol", [128, 1], BF16, hsb_off + 19968)
    A("mblk", [128, 512], F32, hsb_off + 20480)
    S("lnbuf0", [128, 1024], BF16)
    lnscr_off = nc.sbuf_base
    S("lnscr", [128, 1024], F32)
    ocomb_off = nc.sbuf_base
    S("ocomb", [128, 1024], F32)
    A("obufb", [128, 1024], BF16, ocomb_off)
    A("spA", [128, 512], BF16, ocomb_off + 0)
    A("spB", [128, 512], BF16, ocomb_off + 1024)
    A("mB2T4", [128, 512], F32, lnscr_off + 0)
    A("mD", [128, 128], F32, lnscr_off + 2048)
    S("probs_tm", [128, 512], F32)
    S("cols", [128, 24], F32)
    S("bis", [128, 8], F32)
    S("jtile", [128, 4], F32)
    S("postile", [128, 4], F32)
    S("positile", [128, 4], I32)
    S("stile", [128, 4], F32)
    S("mtile", [128, 4], F32)
    S("dmy", [1, 4], F32)
    S("lstrictb", [128, 128], BF16)
    S("lstrictf", [128, 128], F32)
    S("identbf", [128, 128], BF16)
    S("onescolb", [128, 1], BF16)
    S("onescolf", [128, 1], F32)
    S("onesrowb", [1, 128], BF16)
    S("onesrowf", [1, 128], F32)
    S("onesff", [128, 128], F32)
    S("b2sb", [1, 1024], BF16)
    S("wsb", [128, 64], BF16)
    S("b1sb", [128, 32], F32)
    S("bscol", [8, 1], F32)
    S("thcols", [128, 8], F32)
    S("throw", [1, 8], F32)
    S("ehot", [128, 8], F32)

    ps = {}
    for nm in ["pA", "pB", "pC", "pD", "pE", "pF", "pG"]:
        ps[nm] = stack.enter_context(nc.psum_tensor(nm, [128, 512], F32))
    ps["pT"] = stack.enter_context(nc.psum_tensor("pT", [128, 256], BF16))

    s = {}
    s.update(sb)
    s.update(ps)
    s.update({k: v for k, v in dr.items()})
    s["_tmp"] = {}
    s["xbuf"] = [sb["xbuf0"], sb["xbuf1"]]
    s["xcomb"] = s["xbuf"]
    s["plane"] = sb["lnscr"][:, 0:512]
    s["gidsel"] = sb["ocomb"][:, 0:512]
    s["obuf"] = sb["ocomb"]
    s["zbig"] = sb["lnbuf0"]
    s["lnbuf"] = [sb["lnbuf0"], sb["lnbuf0"]]
    s["gbuf"] = [sb["gbuf0"], sb["gbuf1"]]
    s["xfm"] = [sb["xfm0"], sb["xfm1"]]
    s["lnfm"] = [sb["lnfm0"], sb["lnfm1"]]
    s["rslot"] = [sb["rslot0"], sb["rslot1"]]
    s["xcomb2"] = [sb["lnscr"], sb["xcomb21"]]
    s["PT"] = [sb["PT0"], sb["PT1"]]

    P = Planner(nc)

    P.op("gpsimd", lambda e: e.dma_start(s["w1sb"][:, :], w1t[:, :]), sem="initg", dma=True)
    P.op("gpsimd", lambda e: e.dma_start(s["w2sb"][:, :], w2t[:, :]), sem="initg", dma=True)
    P.op("gpsimd", lambda e: e.dma_start(s["wsb"][:, :], wst[:, :]), sem="initg", dma=True)
    P.op("gpsimd", lambda e: e.dma_start(s["b2sb"][:, :], b2r[:, :]), sem="initg", dma=True)
    P.op("sync", lambda e: e.dma_start(s["b1sb"][:, :], b1t[:, :]), sem="init", dma=True)
    P.op("sync", lambda e: e.dma_start(s["ehot"][:, :], ehotI[:, :]), sem="init", dma=True)
    P.op("sync", lambda e: e.dma_start(s["bscol"][:, :], bsI[:, :]), sem="init", dma=True)
    cmap = [("c_lstrictb", "lstrictb"), ("c_lstrictf", "lstrictf"), ("c_identbf", "identbf"),
            ("c_onescolb", "onescolb"), ("c_onescolf", "onescolf"), ("c_onesrowb", "onesrowb"),
            ("c_onesrowf", "onesrowf"), ("c_onesff", "onesff")]
    for cname, sbname in cmap:
        P.op("sync", lambda e, cname=cname, sbname=sbname: e.dma_start(s[sbname][:, :], consts[cname][:, :]),
             sem="init", dma=True)
    for cn in ["c_gid", "c_zrow", "c_piog", "c_iotaslot", "c_keslot3", "c_eoff3", "c_zrowcol", "c_emap",
               "c_mA4", "c_mA24", "c_mB2p4", "c_mX124", "c_mB2T4", "c_mD", "c_mqd", "c_mblk", "c_kgcol"]:
        s[cn] = consts[cn]
    s["wqt"], s["wkt"], s["wvt"], s["wot"] = wqt, wkt, wvt, wot
    P.gate_engines(P.tok("init"))
    P.gate_engines(P.tok("initg"))

    emit_ffn(P, s, x0.ap(), dr["X1"].ap())
    f1_done = P.tok("F_xow")

    att_done = emit_att(P, s, f1_done, dr)
    P.gate_engines(att_done)

    emit_ffn(P, s, dr["X2"].ap(), xout.ap())

    P.op("vector", lambda e: e.tensor_copy(s["dmy"][0:1, 1:2], s["dmy"][0:1, 0:1]),
         waits=[P.tok("F_xow")], sem="fin")
    P.finalize_gates()
    return P, stack


def host_constants():
    c = {}
    jj = np.arange(CH, dtype=np.float64)
    J, I = np.meshgrid(jj, jj, indexing="ij")
    le = (J <= I).astype(np.float64)
    lt = (I < J).astype(np.float64)
    g = float(GAMMA)
    c["c_mA4"] = np.tile((g ** (-J) * le).astype(np.float32), (1, 4))
    c["c_mA24"] = np.tile((-(g ** (-I)) * le).astype(np.float32), (1, 4))
    c["c_mB2p4"] = np.tile(lt.astype(np.float32), (1, 4))
    c["c_mX124"] = np.tile(((g ** (2 * I + 1 - J) - g ** (I + 1)) * le).astype(np.float32), (1, 4))
    U2, J2 = np.meshgrid(jj, jj, indexing="ij")
    c["c_mB2T4"] = np.tile((g ** (CH - 1 - U2) * (U2 < J2)).astype(np.float32), (1, 4))
    c["c_mD"] = (g ** (2 * I - J) * le).astype(np.float32)
    mqd = np.zeros((32, 2 * CH), np.float64)
    mqd[:, 0:CH] = g ** (jj + 1)
    mqd[:, CH:2 * CH] = g ** (2 * jj + 2)
    c["c_mqd"] = mqd.astype(np.float32).astype(bfdt)
    pp = np.arange(128)[:, None] // 32
    cc2 = np.arange(512)[None, :] // 128
    c["c_mblk"] = (pp == cc2).astype(np.float32)
    c["c_kgcol"] = (g ** (CH - 1 - jj)).reshape(128, 1).astype(np.float32).astype(bfdt)
    c["c_lstrictb"] = np.triu(np.ones((128, 128), np.float32), 1).astype(bfdt)
    c["c_lstrictf"] = np.triu(np.ones((128, 128), np.float32), 1)
    c["c_identbf"] = np.eye(128, dtype=np.float32).astype(bfdt)
    c["c_onescolb"] = np.ones((128, 1), np.float32).astype(bfdt)
    c["c_onescolf"] = np.ones((128, 1), np.float32)
    c["c_onesrowb"] = np.ones((1, 128), np.float32).astype(bfdt)
    c["c_onesrowf"] = np.ones((1, 128), np.float32)
    c["c_onesff"] = np.ones((128, 128), np.float32)
    c["c_iotaslot"] = np.broadcast_to(np.arange(SLOTC, dtype=np.float32), (128, SLOTC)).copy()
    p = np.arange(128, dtype=np.float32)[:, None]
    a = np.arange(512, dtype=np.float32)[None, :]
    c["c_gid"] = (a * 128 + p).astype(np.float32)
    g64 = np.arange(64, dtype=np.float32)[None, :]
    c["c_piog"] = (p * 64 + g64).astype(np.float32)
    sl = np.arange(128)[:, None] + np.arange(3)[None, :] * 128
    c["c_keslot3"] = (sl % SLOTC).astype(np.float32)
    es = (sl // SLOTC).astype(np.float32)
    c["c_eoff3"] = np.repeat((es * CO_CH)[:, :, None], 32, axis=2).reshape(128, 96).astype(np.float32)
    c["c_zrowcol"] = np.full((128, 96), float(ZROW), np.float32)
    em = np.zeros((8, 384), np.float32)
    for sgl in range(384):
        em[sgl // SLOTC, sgl] = 1.0
    c["c_emap"] = em
    c["c_zrow"] = np.zeros((1, 512), np.float32)
    return c


_PROG = {}


def get_prog():
    if "nc" not in _PROG:
        nc = bass.Bass("TRN2", target_bir_lowering=False, num_devices=NCORE)
        P, stack = build(nc)
        import contextlib
        semstack = contextlib.ExitStack()
        handles = {}
        for name in P.cnt.keys():
            handles[name] = semstack.enter_context(nc.semaphore(name))
        with nc.Block() as block:
            P.replay(block, handles)
        semstack.close()
        stack.close()
        _PROG["nc"] = nc
    return _PROG["nc"]


def kernel(**inputs):
    x = np.ascontiguousarray(np.asarray(inputs["L0_attn_x"], np.float32).reshape(N, D))
    W1 = np.asarray(inputs["L0_ffn_W1"], np.float32)
    W2 = np.asarray(inputs["L0_ffn_W2"], np.float32)
    b1 = np.asarray(inputs["L0_ffn_b1"], np.float32)
    b2 = np.asarray(inputs["L0_ffn_b2"], np.float32)
    Ws = np.ascontiguousarray(np.asarray(inputs["L0_ffn_Ws"], np.float32))
    bs = np.asarray(inputs["L0_ffn_bs"], np.float32)

    Wq = np.asarray(inputs["L0_attn_Wq"], np.float32)
    Wk = np.asarray(inputs["L0_attn_Wk"], np.float32)
    Wv = np.asarray(inputs["L0_attn_Wv"], np.float32)
    Wo = np.asarray(inputs["L0_attn_Wo"], np.float32)

    nc = get_prog()
    consts = host_constants()
    wst = Ws.reshape(8, 128, 8).transpose(1, 0, 2).reshape(128, 64)
    wqt = np.ascontiguousarray(Wq.reshape(8, 128, 512).transpose(1, 0, 2).reshape(128, 4096))
    wkt = np.ascontiguousarray(Wk.reshape(8, 128, 512).transpose(1, 0, 2).reshape(128, 4096))
    wvt = np.ascontiguousarray(Wv.reshape(8, 128, 512).transpose(1, 0, 2).reshape(128, 4096))
    wot_ = np.ascontiguousarray(Wo.reshape(4, 128, 1024).transpose(1, 0, 2).reshape(128, 4096))

    in_maps = []
    for c in range(NCORE):
        m = {
            "x0": np.ascontiguousarray(x[c * NT:(c + 1) * NT]),
            "w1t": np.ascontiguousarray(W1[c].reshape(8, 128, FF).transpose(1, 0, 2).reshape(128, 8 * FF)),
            "w2t": np.ascontiguousarray(W2[c].reshape(32, 128, D).transpose(1, 0, 2).reshape(128, 32 * D)),
            "b1t": np.ascontiguousarray(b1[c].reshape(32, 128).T),
            "b2r": np.ascontiguousarray(b2[c].reshape(1, D)),
            "wst": np.ascontiguousarray(wst),
            "bsI": np.ascontiguousarray(bs.reshape(8, 1)),
            "ehotI": np.broadcast_to(np.eye(8, dtype=np.float32)[c], (128, 8)).copy(),
            "wqt": wqt, "wkt": wkt, "wvt": wvt, "wot": wot_,
        }
        m.update(consts)
        in_maps.append(m)

    res = run_bass_kernel_spmd(nc, in_maps, core_ids=list(range(NCORE)))
    out = np.concatenate([res.results[c]["xout"] for c in range(NCORE)], axis=0)
    return out.reshape(B_SHAPE)


B_SHAPE = (16, 4096, 1024)


# version 08 (final)
# speedup vs baseline: 1.0470x; 1.0175x over previous
import numpy as np
import ml_dtypes

import concourse.bass as bass
import concourse.mybir as mybir
from concourse.bass_core_rust import IndirectOffsetOnAxis
from concourse.bass_utils import run_bass_kernel_spmd

F32 = mybir.dt.float32
BF16 = mybir.dt.bfloat16
I32 = mybir.dt.int32
AF = mybir.ActivationFunctionType
ALU = mybir.AluOpType
AX = mybir.AxisListType
bfdt = ml_dtypes.bfloat16

NCORE = 8
N = 65536
NT = 8192
D = 1024
EXP = 8
FF = 4096
KSEL = 8192
GAMMA = 0.99
EPS = 1e-5
KCAP = 34
SLOTC = 48
NSLOT = EXP * SLOTC
CO_CH = 1040
CO_ROWS = 8 * CO_CH
ZROW = 8200
NBISECT = 30
NGRP = 16
RG = [list(range(NCORE))]


class Planner:
    def __init__(self, nc):
        self.nc = nc
        self.ops = []
        self.cnt = {}

    def tok(self, sem):
        if sem not in self.cnt:
            hits = [k for k in self.cnt if k.startswith(sem + "@")]
            if len(hits) == 1:
                sem = hits[0]
            elif len(hits) > 1:
                raise KeyError(f"ambiguous sem {sem}: {hits}")
        return (sem, self.cnt.get(sem, 0))

    def op(self, eng, fn, waits=(), sem=None, dma=False, n=None):
        w = [t for t in waits if t is not None and t[1] > 0]
        incs = []
        token = None
        if sem is not None:
            sem = sem + "@" + eng
            amt = n if n is not None else (16 if dma else 1)
            self.cnt[sem] = self.cnt.get(sem, 0) + amt
            incs.append((sem, amt))
            token = (sem, self.cnt[sem])
        self.ops.append((eng, fn, w, incs))
        return token

    def gate_engines(self, token):
        pend = {e: True for e in ("sync", "vector", "scalar", "tensor", "gpsimd")}
        self._gates = getattr(self, "_gates", [])
        self._gates.append((len(self.ops), token, pend))

    def finalize_gates(self):
        for start, token, pend in getattr(self, "_gates", []):
            for idx in range(start, len(self.ops)):
                eng, fn, w, incs = self.ops[idx]
                if pend.get(eng):
                    w.append(token)
                    pend[eng] = False

    def replay(self, block, handles):
        def runner(myname, engine):
            for eng, fn, waits, incs in self.ops:
                if eng != myname:
                    continue
                for (sname, v) in waits:
                    if v > 0:
                        engine.wait_ge(handles[sname], v)
                inst = fn(engine)
                for (sname, a) in incs:
                    inst.then_inc(handles[sname], a)

        @block.sync
        def _(e):
            runner("sync", e)

        @block.vector
        def _(e):
            runner("vector", e)

        @block.scalar
        def _(e):
            runner("scalar", e)

        @block.tensor
        def _(e):
            runner("tensor", e)

        @block.gpsimd
        def _(e):
            runner("gpsimd", e)


def v3e(ap):
    return ap.rearrange("p (a e) -> p a e", e=8)


def emit_mod(P, s, dst, src, m, iscr, waits, sem):
    inv = 1.0 / float(m)
    tk = P.op("vector", lambda e: e.tensor_scalar(dst, src, inv, 0.5 * inv - 0.5, op0=ALU.mult, op1=ALU.add),
              waits=waits, sem=sem)
    tk = P.op("vector", lambda e: e.tensor_copy(iscr, dst), waits=[tk], sem=sem)
    tk = P.op("vector", lambda e: e.tensor_copy(dst, iscr), waits=[tk], sem=sem)
    tk = P.op("vector", lambda e: e.scalar_tensor_tensor(dst, dst, -float(m), src, op0=ALU.mult, op1=ALU.add),
              waits=[tk], sem=sem)
    return tk


def emit_ffn(P, s, Xin, Xout):
    t = "F_"
    SD, SC, SCC, SMM, SEV = t + "d", t + "c", t + "cc", t + "mm", t + "ev"

    tk_cg = P.op("sync", lambda e: e.dma_start(s["gid"][:, :], s["c_gid"][:, :]), sem=SD, dma=True)
    tk_cz = P.op("sync", lambda e: e.dma_start(s["zrow"][:, :], s["c_zrow"][:, :]), sem=SD, dma=True)
    tk_cp = P.op("sync", lambda e: e.dma_start(s["piog"][:, :], s["c_piog"][:, :]), sem=SD, dma=True)
    s["_tmp"]["crel"] = (tk_cg, tk_cz, tk_cp)

    ln_chunk_toks = []
    probs_done = []
    for i in range(64):
        b = i % 2
        co = 8 * b
        c = s["cols"]
        tk_x = P.op("sync", lambda e, i=i, b=b: e.dma_start(s["xbuf"][b][:, :], Xin[i * 128:(i + 1) * 128, :]),
                    waits=[P.tok(t + f"lnfree{b}")], sem=t + f"ldx{b}", dma=True)
        tk1 = P.op("vector", lambda e, b=b, co=co: e.reduce_sum(c[:, co:co + 1], s["xbuf"][b][:, :], axis=AX.X),
                   waits=[tk_x], sem=SC)
        tk2 = P.op("scalar", lambda e, b=b, co=co: e.activation(s["lnscr"][:, :], s["xbuf"][b][:, :], AF.Square,
                                                                accum_out=c[:, co + 1:co + 2]),
                   waits=[tk_x], sem=SC)
        tk = P.op("vector", lambda e, co=co: e.tensor_scalar(c[:, co + 2:co + 3], c[:, co:co + 1], 1.0 / D, None, op0=ALU.mult),
                  waits=[tk1], sem=SC)
        tk = P.op("vector", lambda e, co=co: e.tensor_scalar(c[:, co + 3:co + 4], c[:, co + 1:co + 2], 1.0 / D, None, op0=ALU.mult),
                  waits=[tk2, tk], sem=SC)
        tk = P.op("vector", lambda e, co=co: e.tensor_tensor(c[:, co + 4:co + 5], c[:, co + 2:co + 3], c[:, co + 2:co + 3], op=ALU.mult),
                  waits=[tk], sem=SC)
        tk = P.op("vector", lambda e, co=co: e.tensor_tensor(c[:, co + 4:co + 5], c[:, co + 3:co + 4], c[:, co + 4:co + 5], op=ALU.subtract),
                  waits=[tk], sem=SC)
        tk = P.op("vector", lambda e, co=co: e.tensor_scalar(c[:, co + 4:co + 5], c[:, co + 4:co + 5], EPS, None, op0=ALU.add),
                  waits=[tk], sem=SC)
        tk = P.op("scalar", lambda e, co=co: e.activation(c[:, co + 4:co + 5], c[:, co + 4:co + 5], AF.Sqrt),
                  waits=[tk], sem=SC)
        tk = P.op("vector", lambda e, co=co: e.reciprocal(c[:, co + 4:co + 5], c[:, co + 4:co + 5]),
                  waits=[tk], sem=SC)
        tk = P.op("vector", lambda e, co=co: e.tensor_tensor(c[:, co + 5:co + 6], c[:, co + 2:co + 3], c[:, co + 4:co + 5], op=ALU.mult),
                  waits=[tk], sem=SC)
        tk = P.op("vector", lambda e, co=co: e.tensor_scalar(c[:, co + 5:co + 6], c[:, co + 5:co + 6], -1.0, None, op0=ALU.mult),
                  waits=[tk], sem=SC)
        tk_ln = P.op("scalar", lambda e, b=b, co=co: e.activation(s["lnbuf"][b][:, :], s["xbuf"][b][:, :], AF.Identity,
                                                                  bias=c[:, co + 5:co + 6], scale=c[:, co + 4:co + 5]),
                     waits=[tk, P.tok(t + "lnout")], sem=t + f"lnfree{b}")
        P.op("scalar", lambda e, i=i, b=b: e.dma_start(s["myLN"][i * 128:(i + 1) * 128, :], s["lnbuf"][b][:, :]),
             waits=[tk_ln], sem=t + "lnout", dma=True)
        if i % 8 == 7:
            ln_chunk_toks.append(P.tok(t + "lnout"))

    lnall_toks = []
    for kch in range(8):
        lnall_toks.append(P.op("gpsimd", lambda e, kch=kch: e.collective_compute(
            "AllGather", ALU.bypass, replica_groups=RG,
            ins=[s["myLN"][kch * 1024:(kch + 1) * 1024, :].opt()],
            outs=[s["LNall"][kch * 8192:(kch + 1) * 8192, :].opt()]),
            waits=[ln_chunk_toks[kch]], sem=SCC))

    base_lnout = P.tok(t + "lnout")[1] - 64 * 16
    prev_logit = None
    prev_ltm = None
    for g in range(NGRP):
        wln = [(t + "lnout@scalar", base_lnout + 16 * 4 * (g + 1))]
        mm_toks = []
        for kd in range(8):
            b = kd % 2
            tk_l = P.op("sync", lambda e, g=g, kd=kd, b=b: e.dma_start(
                s["lnfm"][b][:, :], s["myLN"][g * 512:(g + 1) * 512, kd * 128:(kd + 1) * 128], transpose=True),
                waits=wln + [P.tok(t + f"rfree{b}")], sem=t + f"lfm{b}", dma=True)
            mm_toks.append(P.op("tensor", lambda e, kd=kd, b=b: e.matmul(
                s["pE"][0:8, :], s["wsb"][:, kd * 8:(kd + 1) * 8], s["lnfm"][b][:, :],
                start=(kd == 0), stop=(kd == 7)),
                waits=[tk_l] + ([prev_logit] if kd == 0 and prev_logit else []), sem=t + f"rfree{b}"))
        tk = P.op("scalar", lambda e: e.activation(s["logit_bf"][0:8, :], s["pE"][0:8, :], AF.Identity,
                                                   bias=s["bscol"][0:8, 0:1]),
                  waits=mm_toks[-2:], sem=SEV)
        prev_logit = tk
        tr_toks = []
        for a4 in range(4):
            tr_toks.append(P.op("tensor", lambda e, a4=a4: e.transpose(
                s["pT"][:, a4 * 8:(a4 + 1) * 8], s["logit_bf"][0:8, a4 * 128:(a4 + 1) * 128], s["identbf"][0:8, 0:8]),
                waits=[tk] + ([prev_ltm] if a4 == 0 and prev_ltm else []), sem=SMM))
        tk = P.op("vector", lambda e: e.tensor_copy(s["ltm"][:, 0:32], s["pT"][:, 0:32]), waits=tr_toks, sem=SEV)
        prev_ltm = tk
        tk = P.op("vector", lambda e: e.reduce_max(s["red1"][:, 0:4], v3e(s["ltm"][:, 0:32]), axis=AX.X),
                  waits=[tk], sem=SC)
        tk = P.op("vector", lambda e: e.tensor_tensor(
            v3e(s["ltm"][:, 32:64]), v3e(s["ltm"][:, 0:32]),
            s["red1"][:, 0:4].rearrange("p (a one) -> p a one", one=1).to_broadcast([128, 4, 8]), op=ALU.subtract),
            waits=[tk], sem=SC)
        tk = P.op("scalar", lambda e: e.activation(s["ltm"][:, 32:64], s["ltm"][:, 32:64], AF.Exp), waits=[tk], sem=SC)
        tk = P.op("vector", lambda e: e.reduce_sum(s["red1"][:, 4:8], v3e(s["ltm"][:, 32:64]), axis=AX.X),
                  waits=[tk], sem=SC)
        tk = P.op("vector", lambda e: e.reciprocal(s["red1"][:, 4:8], s["red1"][:, 4:8]), waits=[tk], sem=SC)
        probs_done.append(P.op("vector", lambda e, g=g: e.tensor_tensor(
            v3e(s["probs_tm"][:, g * 32:(g + 1) * 32]), v3e(s["ltm"][:, 32:64]),
            s["red1"][:, 4:8].rearrange("p (a one) -> p a one", one=1).to_broadcast([128, 4, 8]), op=ALU.mult),
            waits=[tk], sem=t + "probs"))

    tk_pw = P.op("scalar", lambda e: e.dma_start(
        s["probsMine"][:, :].rearrange("(a p) e -> p a e", p=128), v3e(s["probs_tm"][:, :])),
        waits=probs_done, sem=SD, dma=True)

    tk = P.op("gpsimd", lambda e: e.collective_compute(
        "AllGather", ALU.bypass, replica_groups=RG,
        ins=[s["probsMine"][:, :].opt()], outs=[s["probsAll"][:, :].opt()]),
        waits=[tk_pw], sem=SCC)
    tk_pa = P.op("sync", lambda e: e.dma_start(
        v3e(s["pAll"][:, :]), s["probsAll"][:, :].rearrange("(a p) e -> p a e", p=128)),
        waits=[tk], sem=SD, dma=True)
    tk = P.op("vector", lambda e: e.memset(s["plane"][:, :], 0.0), waits=[tk_pa], sem=SC)
    for j in range(EXP):
        tk2 = P.op("vector", lambda e, j=j: e.tensor_scalar(
            s["cmpA"][:, :], v3e(s["pAll"][:, :])[:, :, j:j + 1].rearrange("p a 1 -> p a"),
            s["ehot"][:, j:j + 1], None, op0=ALU.mult), waits=[tk_pa] if j == 0 else [tk], sem=SC)
        tk = P.op("vector", lambda e: e.tensor_tensor(s["plane"][:, :], s["plane"][:, :], s["cmpA"][:, :],
                                                      op=ALU.add), waits=[tk, tk2], sem=SC)
    tk_plane = tk

    bis = s["bis"]
    lo, hi, th, pred = bis[:, 0:1], bis[:, 1:2], bis[:, 2:3], bis[:, 3:4]
    tk1 = P.op("vector", lambda e: e.memset(lo, 0.0), waits=[tk_plane], sem=SC)
    tk2 = P.op("vector", lambda e: e.memset(hi, 1.0), sem=SC)
    tk = P.op("vector", lambda e: e.memset(th, 0.5), sem=SC)
    for it in range(NBISECT):
        tk = P.op("vector", lambda e: e.tensor_scalar(s["cmpA"][:, :], s["plane"][:, :], th[:, 0:1], None, op0=ALU.is_gt),
                  waits=[tk, tk1, tk2] if it == 0 else [tk], sem=SC)
        tk = P.op("vector", lambda e: e.reduce_sum(s["cols"][:, 16:17], s["cmpA"][:, :], axis=AX.X), waits=[tk], sem=SC)
        tk = P.op("tensor", lambda e: e.matmul(s["pF"][:, 0:1], s["onesff"][:, :], s["cols"][:, 16:17],
                                               start=True, stop=True), waits=[tk], sem=SMM)
        tk = P.op("vector", lambda e: e.tensor_scalar(pred, s["pF"][:, 0:1], KSEL - 0.5, None, op0=ALU.is_gt),
                  waits=[tk], sem=SC)
        d1, d2 = bis[:, 4:5], bis[:, 5:6]
        tka = P.op("vector", lambda e: e.tensor_tensor(d1, th, lo, op=ALU.subtract), waits=[tk], sem=SC)
        tka = P.op("vector", lambda e: e.tensor_tensor(d1, d1, pred, op=ALU.mult), waits=[tka], sem=SC)
        tkb = P.op("vector", lambda e: e.tensor_tensor(d2, hi, th, op=ALU.subtract), waits=[tk], sem=SC)
        tkb = P.op("vector", lambda e: e.tensor_tensor(d2, d2, pred, op=ALU.mult), waits=[tkb], sem=SC)
        tk1 = P.op("vector", lambda e: e.tensor_tensor(lo, lo, d1, op=ALU.add), waits=[tka], sem=SC)
        tk2 = P.op("vector", lambda e: e.tensor_tensor(hi, th, d2, op=ALU.add), waits=[tkb, tk1], sem=SC)
        tk = P.op("vector", lambda e: e.tensor_tensor(th, lo, hi, op=ALU.add), waits=[tk1, tk2], sem=SC)
        tk = P.op("vector", lambda e: e.tensor_scalar(th, th, 0.5, None, op0=ALU.mult), waits=[tk], sem=SC)
    tk_lo = tk1
    tk = P.op("scalar", lambda e: e.dma_start(s["thMine"][:, :], lo[0:1, 0:1]), waits=[tk, tk_lo], sem=SD, dma=True)
    tk = P.op("gpsimd", lambda e: e.collective_compute(
        "AllGather", ALU.bypass, replica_groups=RG,
        ins=[s["thMine"][:, :].opt()], outs=[s["thAll"][:, :].opt()]),
        waits=[tk], sem=SCC)
    tk = P.op("sync", lambda e: e.dma_start(s["throw"][0:1, 0:8], s["thAll"][:, :].rearrange("e one -> (one e)").rearrange("(o e) -> o e", o=1)),
              waits=[tk], sem=SD, dma=True)
    tk = P.op("tensor", lambda e: e.matmul(s["pF"][:, 8:16], s["onesrowf"][0:1, :], s["throw"][0:1, 0:8],
                                           start=True, stop=True), waits=[tk], sem=SMM)
    tk_thc = P.op("vector", lambda e: e.tensor_copy(s["thcols"][:, :], s["pF"][:, 8:16]), waits=[tk], sem=SC)

    sD_toks = []
    tk_free = None
    prev_sD = None
    for ee in range(EXP):
        tk = P.op("vector", lambda e, ee=ee: e.tensor_scalar(
            s["selbf"][:, :], v3e(s["pAll"][:, :])[:, :, ee:ee + 1].rearrange("p a 1 -> p a"),
            s["thcols"][:, ee:ee + 1], None, op0=ALU.is_gt),
            waits=[tk_thc, tk_pa] + ([tk_free] if tk_free else []), sem=SC)
        tk_cnt = P.op("tensor", lambda e: e.matmul(s["pE"][0:1, 0:512], s["onescolb"][:, 0:1], s["selbf"][:, :],
                                                   start=True, stop=True), waits=[tk], sem=SMM)
        tk = P.op("vector", lambda e: e.tensor_copy(s["cntrow"][0:1, :], s["pE"][0:1, 0:512]), waits=[tk_cnt], sem=SC)
        tk_free = tk
        tk = P.op("vector", lambda e: e.tensor_tensor_scan(s["srow"][0:1, 1:513], s["cntrow"][0:1, :],
                                                           s["zrow"][0:1, 0:512], 0.0, op0=ALU.add, op1=ALU.add),
                  waits=[tk, s["_tmp"]["crel"][1]] + ([prev_sD] if prev_sD else []), sem=SC)
        tkz = P.op("vector", lambda e: e.memset(s["srow"][0:1, 0:1], 0.0), waits=[tk], sem=SC)
        prev_sD = P.op("scalar", lambda e, ee=ee: e.dma_start(s["Srows"][ee:ee + 1, 0:640], s["srow"][0:1, 0:640]),
                       waits=[tk, tkz], sem=SD, dma=True)
        sD_toks.append(prev_sD)

    tk_sel = P.op("vector", lambda e: e.tensor_scalar(s["selbf"][:, :], s["plane"][:, :], lo[:, 0:1], None, op0=ALU.is_gt),
                  waits=[tk_lo, tk_free], sem=SC)
    tk_r = P.op("tensor", lambda e: e.matmul(s["pG"][:, 0:512], s["lstrictb"][:, :], s["selbf"][:, :],
                                             start=True, stop=True), waits=[tk_sel], sem=SMM)
    tk_rv = P.op("vector", lambda e: e.tensor_copy(s["rmine"][:, :], s["pG"][:, 0:512]), waits=[tk_r], sem=SC)
    tk_cnt = P.op("tensor", lambda e: e.matmul(s["pE"][0:1, 0:512], s["onescolb"][:, 0:1], s["selbf"][:, :],
                                               start=True, stop=True), waits=[tk_sel, tk_rv], sem=SMM)
    tk_cv = P.op("vector", lambda e: e.tensor_copy(s["cntrow"][0:1, :], s["pE"][0:1, 0:512]),
                 waits=[tk_cnt, prev_sD], sem=SC)
    tk = P.op("vector", lambda e: e.tensor_tensor_scan(s["srow"][0:1, 1:513], s["cntrow"][0:1, :],
                                                       s["zrow"][0:1, 0:512], 0.0, op0=ALU.add, op1=ALU.add),
              waits=[tk_cv], sem=SC)
    tkz = P.op("vector", lambda e: e.memset(s["srow"][0:1, 0:1], 0.0), waits=[tk], sem=SC)
    tk_sown = P.op("vector", lambda e: e.tensor_copy(s["sown"][0:1, 0:513], s["srow"][0:1, 0:513]),
                   waits=[tk, tkz], sem=SC)

    tk_gs = P.op("vector", lambda e: e.tensor_tensor(s["gidsel"][:, :], s["gid"][:, :], s["selbf"][:, :], op=ALU.mult),
                 waits=[tk_sel, s["_tmp"]["crel"][0]], sem=SC)
    last_pg = None
    pg_dma = []
    for k in range(KCAP):
        tk = P.op("vector", lambda e, k=k: e.tensor_scalar(s["cmpA"][:, :], s["rmine"][:, :], float(k), None, op0=ALU.is_equal),
                  waits=[tk_rv, tk_gs] if k == 0 else [last_pg], sem=SC)
        tk = P.op("vector", lambda e: e.tensor_tensor(s["cmpB"][:, :], s["cmpA"][:, :], s["gidsel"][:, :], op=ALU.mult),
                  waits=[tk], sem=SC)
        tk = P.op("tensor", lambda e: e.matmul(s["pE"][0:1, 0:512], s["onescolf"][:, 0:1], s["cmpB"][:, :],
                                               start=True, stop=True), waits=[tk], sem=SMM)
        tk = P.op("vector", lambda e, k=k: e.tensor_copy(s["srow"][0:1, 0:512], s["pE"][0:1, 0:512]),
                  waits=[tk, tk_sown] + pg_dma[-1:], sem=SC)
        last_pg = tk
        pg_dma.append(P.op("scalar", lambda e, k=k: e.dma_start(
            s["pagesKB"][k * 512:(k + 1) * 512, :].rearrange("r one -> (one r)").rearrange("(o r) -> o r", o=1),
            s["srow"][0:1, 0:512]),
            waits=[tk], sem=SD, dma=True))

    w0 = [tk_sown, tk_cv, last_pg]
    tk_ne = P.op("vector", lambda e: e.tensor_scalar(s["row1"][0:1, :], s["cntrow"][0:1, :], 0.0, None, op0=ALU.is_gt),
                 waits=w0, sem=SC)
    tk = P.op("vector", lambda e: e.tensor_scalar(s["row2"][0:1, :], s["cntrow"][0:1, :], -1.0, float(KCAP),
                                                  op0=ALU.mult, op1=ALU.add), waits=w0, sem=SC)
    tk = P.op("vector", lambda e: e.tensor_tensor_scan(s["row3"][0:1, :], s["row2"][0:1, :], s["zrow"][0:1, :], 0.0,
                                                       op0=ALU.add, op1=ALU.add), waits=[tk], sem=SC)
    tkA = P.op("vector", lambda e: e.tensor_copy(s["row2"][0:1, 1:512], s["row3"][0:1, 0:511]), waits=[tk], sem=SC)
    tkB = P.op("vector", lambda e: e.memset(s["row2"][0:1, 0:1], 0.0), waits=[tk], sem=SC)
    tk_V = P.op("vector", lambda e: e.tensor_copy(s["row2"][0:1, 0:1], s["row2"][0:1, 0:1]), waits=[tkA, tkB], sem=SC)
    tk_d0 = P.op("vector", lambda e: e.tensor_scalar(s["row3"][0:1, :], s["row1"][0:1, :], -1.0, 1.0,
                                                     op0=ALU.mult, op1=ALU.add), waits=[tk_ne, tkA], sem=SC)
    tk_d1 = P.op("vector", lambda e: e.tensor_tensor(s["row4"][0:1, :], s["row2"][0:1, :], s["row1"][0:1, :],
                                                     op=ALU.mult), waits=[tk_ne, tk_V], sem=SC)
    tk = P.op("vector", lambda e: e.tensor_tensor_scan(s["row5"][0:1, :], s["row3"][0:1, :], s["row4"][0:1, :], 0.0,
                                                       op0=ALU.mult, op1=ALU.add), waits=[tk_d0, tk_d1], sem=SC)
    tkA = P.op("vector", lambda e: e.tensor_copy(s["row3"][0:1, 1:512], s["row5"][0:1, 0:511]), waits=[tk], sem=SC)
    tkB = P.op("vector", lambda e: e.memset(s["row3"][0:1, 0:1], 0.0), waits=[tk, tkA], sem=SC)
    tk_J = P.op("vector", lambda e: e.tensor_tensor(s["row4"][0:1, :], s["row2"][0:1, :], s["row3"][0:1, :],
                                                    op=ALU.subtract), waits=[tkA, tkB, tk_V, tk_d1], sem=SC)
    tkd0 = P.op("scalar", lambda e: e.dma_start(s["rowD"][0:1, :], s["row4"][0:1, :]), waits=[tk_J], sem=SD, dma=True)
    tkd1 = P.op("scalar", lambda e: e.dma_start(s["rowD"][1:2, :], s["sown"][0:1, 0:512]),
                waits=[tk_sown, tkd0], sem=SD, dma=True)
    tkd2 = P.op("scalar", lambda e: e.dma_start(s["rowD"][2:3, :], s["row1"][0:1, :]),
                waits=[tk_ne, tkd1], sem=SD, dma=True)
    tk_jt = P.op("sync", lambda e: e.dma_start(s["jtile"][:, 0:4], s["rowD"][0:1, :].rearrange("one (p x) -> p (x one)", p=128)),
                 waits=[tkd0], sem=SD, dma=True)
    tk_st = P.op("sync", lambda e: e.dma_start(s["stile"][:, 0:4], s["rowD"][1:2, :].rearrange("one (p x) -> p (x one)", p=128)),
                 waits=[tkd1], sem=SD, dma=True)
    tk_mt = P.op("sync", lambda e: e.dma_start(s["mtile"][:, 0:4], s["rowD"][2:3, :].rearrange("one (p x) -> p (x one)", p=128)),
                 waits=[tkd2], sem=SD, dma=True)
    tk = P.op("vector", lambda e: e.tensor_scalar(s["postile"][:, 0:4], s["stile"][:, 0:4], 1.0 / 64.0, 0.5 / 64.0 - 0.5,
                                                  op0=ALU.mult, op1=ALU.add), waits=[tk_st], sem=SC)
    tk = P.op("vector", lambda e: e.tensor_copy(s["positile"][:, 0:4], s["postile"][:, 0:4]), waits=[tk], sem=SC)
    tk = P.op("vector", lambda e: e.tensor_copy(s["postile"][:, 0:4], s["positile"][:, 0:4]), waits=[tk], sem=SC)
    tk = P.op("vector", lambda e: e.tensor_tensor(s["postile"][:, 0:4], s["postile"][:, 0:4], s["stile"][:, 0:4],
                                                  op=ALU.add), waits=[tk], sem=SC)
    tk = P.op("vector", lambda e: e.tensor_scalar(s["postile"][:, 0:4], s["postile"][:, 0:4], -8319.0, None, op0=ALU.add),
              waits=[tk], sem=SC)
    tk = P.op("vector", lambda e: e.tensor_tensor(s["postile"][:, 0:4], s["postile"][:, 0:4], s["mtile"][:, 0:4],
                                                  op=ALU.mult), waits=[tk, tk_mt], sem=SC)
    tk = P.op("vector", lambda e: e.tensor_scalar(s["postile"][:, 0:4], s["postile"][:, 0:4], 8319.0, None, op0=ALU.add),
              waits=[tk], sem=SC)
    tk_pi32 = P.op("vector", lambda e: e.tensor_copy(s["positile"][:, 0:4], s["postile"][:, 0:4]),
                   waits=[tk], sem=SC)
    tk = P.op("vector", lambda e: e.memset(s["ztile"][:, :], 0.0), waits=[tk_pi32], sem=SC)
    tk_za = P.op("scalar", lambda e: e.dma_start(s["piAdjD"][:, :].rearrange("(p g) one -> p g", p=128),
                                                 s["ztile"][:, 0:65]), waits=[tk], sem=SD, dma=True)
    sc_toks = []
    for x in range(4):
        sc_toks.append(P.op("gpsimd", lambda e, x=x: e.indirect_dma_start(
            s["piAdjD"][:, :], IndirectOffsetOnAxis(s["positile"][:, x:x + 1], axis=0),
            s["jtile"][:, x:x + 1], None),
            waits=[tk_pi32, tk_za, tk_jt] + sc_toks[-1:], sem=SD, dma=True))
    tk_lA = P.op("sync", lambda e: e.dma_start(s["atile"][:, 0:65],
                                               s["piAdjD"][:, :].rearrange("(p g) one -> p g", p=128)),
                 waits=sc_toks, sem=SD, dma=True)
    tk = P.op("vector", lambda e: e.tensor_tensor_scan(s["A65"][:, 0:65], s["atile"][:, 0:65], s["ztile"][:, 0:65], 0.0,
                                                       op0=ALU.add, op1=ALU.add), waits=[tk_lA], sem=SC)
    tk_tot = P.op("vector", lambda e: e.tensor_copy(s["cols"][:, 17:18], s["A65"][:, 64:65]), waits=[tk], sem=SC)
    tk_pf = P.op("tensor", lambda e: e.matmul(s["pF"][:, 24:25], s["lstrictf"][:, :], s["cols"][:, 17:18],
                                              start=True, stop=True), waits=[tk_tot], sem=SMM)
    tk_off = P.op("vector", lambda e: e.tensor_copy(s["cols"][:, 18:19], s["pF"][:, 24:25]), waits=[tk_pf], sem=SC)
    tk = P.op("vector", lambda e: e.tensor_scalar(s["A65"][:, 0:64], s["A65"][:, 0:64], s["cols"][:, 18:19], None, op0=ALU.add),
              waits=[tk_off, tk_tot], sem=SC)
    tk = P.op("vector", lambda e: e.tensor_tensor(s["A65"][:, 0:64], s["A65"][:, 0:64], s["piog"][:, :], op=ALU.add),
              waits=[tk, s["_tmp"]["crel"][2]], sem=SC)
    tk = P.op("vector", lambda e: e.tensor_scalar(s["A65"][:, 0:64], s["A65"][:, 0:64], float(512 * KCAP - 1), None, op0=ALU.min),
              waits=[tk], sem=SC)
    tk = P.op("vector", lambda e: e.tensor_scalar(s["A65"][:, 0:64], s["A65"][:, 0:64], 0.0, None, op0=ALU.max),
              waits=[tk], sem=SC)
    tk = emit_mod(P, s, s["B65"][:, :], s["A65"][:, 0:64], KCAP, s["pii"][:, :], [tk], SC)
    tk = P.op("vector", lambda e: e.tensor_tensor(s["C65"][:, :], s["A65"][:, 0:64], s["B65"][:, :], op=ALU.subtract),
              waits=[tk], sem=SC)
    tk = P.op("vector", lambda e: e.tensor_scalar(s["C65"][:, :], s["C65"][:, :], 1.0 / KCAP, None, op0=ALU.mult),
              waits=[tk], sem=SC)
    tk = P.op("vector", lambda e: e.tensor_scalar(s["B65"][:, :], s["B65"][:, :], 512.0, None, op0=ALU.mult),
              waits=[tk], sem=SC)
    tk = P.op("vector", lambda e: e.tensor_tensor(s["C65"][:, :], s["C65"][:, :], s["B65"][:, :], op=ALU.add),
              waits=[tk], sem=SC)
    tk_pii = P.op("vector", lambda e: e.tensor_copy(s["pii"][:, :], s["C65"][:, :]), waits=[tk], sem=SC)

    idv_toks = []
    for g in range(64):
        idv_toks.append(P.op("gpsimd", lambda e, g=g: e.indirect_dma_start(
            s["idvf"][:, g:g + 1], None, s["pagesKB"][:, :], IndirectOffsetOnAxis(s["pii"][:, g:g + 1], axis=0)),
            waits=([tk_pii] + pg_dma) if g == 0 else [], sem=SD, dma=True))
    tk = emit_mod(P, s, s["w64a"][:, :], s["idvf"][:, :], 8192, s["pii"][:, :], idv_toks, SC)
    tk = P.op("vector", lambda e: e.tensor_tensor(s["w64b"][:, :], s["idvf"][:, :], s["w64a"][:, :], op=ALU.subtract),
              waits=[tk], sem=SC)
    tk = P.op("vector", lambda e: e.tensor_scalar(s["w64b"][:, :], s["w64b"][:, :], 1.0 / 8192.0, None, op0=ALU.mult),
              waits=[tk], sem=SC)
    tk = emit_mod(P, s, s["w64c"][:, :], s["w64a"][:, :], 1024, s["pii"][:, :], [tk], SC)
    tk = P.op("vector", lambda e: e.tensor_tensor(s["w64a"][:, :], s["w64a"][:, :], s["w64c"][:, :], op=ALU.subtract),
              waits=[tk], sem=SC)
    tk = P.op("vector", lambda e: e.tensor_scalar(s["w64a"][:, :], s["w64a"][:, :], 8.0, None, op0=ALU.mult),
              waits=[tk], sem=SC)
    tk = P.op("vector", lambda e: e.tensor_scalar(s["w64b"][:, :], s["w64b"][:, :], 1024.0, None, op0=ALU.mult),
              waits=[tk], sem=SC)
    tk = P.op("vector", lambda e: e.tensor_tensor(s["w64a"][:, :], s["w64a"][:, :], s["w64b"][:, :], op=ALU.add),
              waits=[tk], sem=SC)
    tk = P.op("vector", lambda e: e.tensor_tensor(s["w64a"][:, :], s["w64a"][:, :], s["w64c"][:, :], op=ALU.add),
              waits=[tk], sem=SC)
    tk = P.op("vector", lambda e: e.tensor_scalar(s["w64a"][:, :], s["w64a"][:, :], float(N - 1), None, op0=ALU.min),
              waits=[tk], sem=SC)
    tk = P.op("vector", lambda e: e.tensor_scalar(s["w64a"][:, :], s["w64a"][:, :], 0.0, None, op0=ALU.max),
              waits=[tk], sem=SC)
    tk_rows = P.op("vector", lambda e: e.tensor_copy(s["rowsi"][:, :], s["w64a"][:, :]), waits=[tk], sem=SC)

    xg_toks = []
    for g in range(64):
        b = g % 2
        tka = P.op("gpsimd", lambda e, g=g, b=b: e.indirect_dma_start(
            s["gbuf"][b][:, :], None, s["LNall"][:, :], IndirectOffsetOnAxis(s["rowsi"][:, g:g + 1], axis=0)),
            waits=([tk_rows] + lnall_toks if g <= 1 else []) + ([xg_toks[-2]] if len(xg_toks) >= 2 else []),
            sem=t + "xg" + str(b), dma=True)
        xg_toks.append(P.op("gpsimd", lambda e, g=g, b=b: e.dma_start(
            s["CX"][:, :].rearrange("(p g) d -> p g d", p=128)[:, g:g + 1, :].rearrange("p 1 d -> p d"),
            s["gbuf"][b][:, :]), waits=[tka], sem=t + "xg" + str(b), dma=True))
    all_xg = [P.tok(t + "xg0"), P.tok(t + "xg1")]

    co_group_done = []
    prev_out2_last = None
    prev_evac = dict(s["_tmp"].get("Fevac", {}))
    for g in range(NGRP):
        b = g % 2
        ld_toks = []
        for kd in range(8):
            ld_toks.append(P.op("sync", lambda e, g=g, kd=kd, b=b: e.dma_start(
                s["xfm"][b][:, kd * 512:(kd + 1) * 512],
                s["CX"][g * 512:(g + 1) * 512, kd * 128:(kd + 1) * 128], transpose=True),
                waits=all_xg + [P.tok(t + f"xffree{b}")], sem=t + f"xfl{b}", dma=True))
        gelu_toks = []
        for fc in range(32):
            pb = s["pA"] if fc % 2 == 0 else s["pB"]
            mmw = list(ld_toks) if fc < 2 else [gelu_toks[fc - 2]]
            if fc == 0 and prev_out2_last is not None:
                mmw.append(prev_out2_last)
            tkm = None
            for kd in range(8):
                tkm = P.op("tensor", lambda e, fc=fc, kd=kd, b=b, pb=pb: e.matmul(
                    pb[:, :], s["w1sb"][:, kd * 4096 + fc * 128: kd * 4096 + (fc + 1) * 128],
                    s["xfm"][b][:, kd * 512:(kd + 1) * 512], start=(kd == 0), stop=(kd == 7)),
                    waits=mmw if kd == 0 else [], sem=(SMM if kd == 7 else None))
            gelu_toks.append(P.op("scalar", lambda e, fc=fc, pb=pb: e.activation(
                s["hsb"][:, fc * 512:(fc + 1) * 512], pb[:, :], AF.Gelu_apprx_tanh,
                bias=s["b1sb"][:, fc:fc + 1]), waits=[tkm], sem=SEV))
        P.op("vector", lambda e: e.tensor_copy(s["dmy"][0:1, 0:1], s["dmy"][0:1, 0:1]),
             waits=[gelu_toks[-1]], sem=t + f"xffree{b}")
        out2_w = []
        for t4 in range(4):
            for dh in range(2):
                j = t4 * 2 + dh
                pc = s["pC"] if j % 2 == 0 else s["pD"]
                mmw = [gelu_toks[-1]]
                if (j % 2) in prev_evac:
                    mmw.append(prev_evac[j % 2])
                tkm = None
                for fc in range(32):
                    tkm = P.op("tensor", lambda e, fc=fc, t4=t4, dh=dh, pc=pc: e.matmul(
                        pc[:, :], s["hsb"][:, fc * 512 + t4 * 128: fc * 512 + (t4 + 1) * 128],
                        s["w2sb"][:, fc * 1024 + dh * 512: fc * 1024 + (dh + 1) * 512],
                        start=(fc == 0), stop=False), waits=mmw if fc == 0 else [], sem=None)
                tkm = P.op("tensor", lambda e, dh=dh, pc=pc: e.matmul(
                    pc[:, :], s["onesrowb"][0:1, :], s["b2sb"][0:1, dh * 512:(dh + 1) * 512],
                    start=False, stop=True), sem=SMM)
                bq = "obufb" if t4 % 2 == 0 else "obufb2"
                cowtok = s["_tmp"].get("cowtok" + str(t4 % 2))
                tke = P.op("vector", lambda e, dh=dh, pc=pc, bq=bq: e.tensor_copy(
                    s[bq][:, dh * 512:(dh + 1) * 512], pc[:, :]),
                    waits=[tkm] + ([cowtok] if cowtok else []), sem=SEV)
                prev_evac[j % 2] = tke
                out2_w.append(tke)
                if dh == 1:
                    s["_tmp"]["cowtok" + str(t4 % 2)] = P.op("scalar", lambda e, g=g, t4=t4, bq=bq: e.dma_start(
                        s["compact"][g * 512 + t4 * 128: g * 512 + (t4 + 1) * 128, :], s[bq][:, :]),
                        waits=out2_w[-2:], sem=t + "cow", dma=True)
        prev_out2_last = out2_w[-1]
        co_group_done.append(P.tok(t + "cow"))
    s["_tmp"]["Fevac"] = prev_evac

    tk = P.op("vector", lambda e: e.memset(s["zbig"][:, :], 0.0), waits=[co_group_done[-1]], sem=SC)
    tk_zs = P.op("scalar", lambda e: e.dma_start(s["compact"][8192:CO_ROWS, :], s["zbig"][:, :]),
                 waits=[tk], sem=t + "cow", dma=True)

    ag_toks = []
    for kch in range(8):
        need_groups = min(NGRP, (CO_CH * (kch + 1) + 511) // 512)
        w = [co_group_done[need_groups - 1]]
        if kch == 7:
            w.append(tk_zs)
        ag_toks.append(P.op("gpsimd", lambda e, kch=kch: e.collective_compute(
            "AllGather", ALU.bypass, replica_groups=RG,
            ins=[s["compact"][kch * CO_CH:(kch + 1) * CO_CH, :].opt()],
            outs=[s["allOut"][kch * 8 * CO_CH:(kch + 1) * 8 * CO_CH, :].opt()]),
            waits=w, sem=SCC))

    hrel = []
    for cn, sn in [("c_emap", "emap"), ("c_eoff3", "eoff3"), ("c_zrowcol", "zrowcol"),
                   ("c_iotaslot", "iotaslot"), ("c_keslot3", "keslot3")]:
        hrel.append(P.op("scalar", lambda e, cn=cn, sn=sn: e.dma_start(s[sn][:, :], s[cn][:, :]),
                         waits=[co_group_done[-1]] if not hrel else [], sem=SD, dma=True))

    def ld_smine(e):
        pid = P.nc.partition_id([mybir.EngineType.SP])
        return e.dma_start(s["smine"][0:8, 0:64], s["Srows"][0:8, bass.ts(pid, 64)])

    def ld_smine2(e):
        pid = P.nc.partition_id([mybir.EngineType.SP])
        return e.dma_start(s["smine"][0:8, 64:128], s["Srows"][0:8, bass.ts(pid + 1, 64)])

    tk_sm = P.op("sync", ld_smine, waits=sD_toks + [co_group_done[-1]], sem=SD, dma=True)
    tk_sm2 = P.op("sync", ld_smine2, waits=[tk_sm], sem=SD, dma=True)

    tk_se = P.op("vector", lambda e: e.tensor_copy(
        s["sEven"][0:8, 0:33],
        s["smine"][0:8, 0:128].rearrange("e (b two) -> e b two", two=2)[:, 0:33, 0:1].rearrange("e b 1 -> e b")),
        waits=[tk_sm, tk_sm2], sem=SC)
    tk_cb = P.op("vector", lambda e: e.tensor_tensor(s["cEven"][0:8, 0:32], s["sEven"][0:8, 1:33],
                                                     s["sEven"][0:8, 0:32], op=ALU.subtract),
                 waits=[tk_se], sem=SC)
    sv_toks = []
    for gc in range(3):
        sv_toks.append(P.op("tensor", lambda e, gc=gc: e.matmul(
            s["pE"][:, gc * 32:(gc + 1) * 32], s["emap"][0:8, gc * 128:(gc + 1) * 128], s["sEven"][0:8, 0:32],
            start=True, stop=True), waits=[tk_se] + hrel, sem=SMM))
        sv_toks.append(P.op("tensor", lambda e, gc=gc: e.matmul(
            s["pF"][:, 128 + gc * 32:128 + (gc + 1) * 32], s["emap"][0:8, gc * 128:(gc + 1) * 128],
            s["cEven"][0:8, 0:32], start=True, stop=True), waits=[tk_cb], sem=SMM))
    tk_sv = P.op("vector", lambda e: e.tensor_copy(s["svals"][:, 0:96], s["pE"][:, 0:96]), waits=sv_toks, sem=SC)
    tk_cvv = P.op("vector", lambda e: e.tensor_copy(s["cvals"][:, 0:96], s["pF"][:, 128:224]), waits=sv_toks, sem=SC)
    kebc = s["keslot3"][:, 0:3].rearrange("p (g one) -> p g one", one=1).to_broadcast([128, 3, 32])
    v96 = lambda ap: ap.rearrange("p (g b) -> p g b", b=32)
    tk = P.op("vector", lambda e: e.tensor_tensor(v96(s["iwork"][:, 0:96]), v96(s["svals"][:, 0:96]), kebc, op=ALU.add),
              waits=[tk_sv] + hrel, sem=SC)
    tkp = P.op("vector", lambda e: e.tensor_tensor(v96(s["ipred"][:, 0:96]), kebc, v96(s["cvals"][:, 0:96]), op=ALU.is_lt),
               waits=[tk_cvv], sem=SC)
    tk = P.op("vector", lambda e: e.tensor_scalar(s["iwork2"][:, 0:96], s["iwork"][:, 0:96], float(ZROW), None, op0=ALU.min),
              waits=[tk], sem=SC)
    tk = P.op("vector", lambda e: e.tensor_tensor(s["iwork"][:, 0:96], s["iwork2"][:, 0:96], s["zrowcol"][:, 0:96],
                                                  op=ALU.subtract), waits=[tk, tkp], sem=SC)
    tk = P.op("vector", lambda e: e.tensor_tensor(s["iwork"][:, 0:96], s["iwork"][:, 0:96], s["ipred"][:, 0:96],
                                                  op=ALU.mult), waits=[tk], sem=SC)
    tk = P.op("vector", lambda e: e.tensor_tensor(s["iwork"][:, 0:96], s["iwork"][:, 0:96], s["zrowcol"][:, 0:96],
                                                  op=ALU.add), waits=[tk], sem=SC)
    tk = emit_mod(P, s, s["iwork2"][:, 0:96], s["iwork"][:, 0:96], CO_CH, s["idxslot"][:, 0:96], [tk], SC)
    tk = P.op("vector", lambda e: e.tensor_tensor(s["iwork3"][:, 0:96], s["iwork"][:, 0:96], s["iwork2"][:, 0:96],
                                                  op=ALU.subtract), waits=[tk], sem=SC)
    tk = P.op("vector", lambda e: e.tensor_scalar(s["iwork3"][:, 0:96], s["iwork3"][:, 0:96], 8.0, None, op0=ALU.mult),
              waits=[tk], sem=SC)
    tk = P.op("vector", lambda e: e.tensor_tensor(s["iwork2"][:, 0:96], s["iwork2"][:, 0:96], s["eoff3"][:, 0:96],
                                                  op=ALU.add), waits=[tk], sem=SC)
    tk = P.op("vector", lambda e: e.tensor_tensor(s["iwork"][:, 0:96], s["iwork3"][:, 0:96], s["iwork2"][:, 0:96],
                                                  op=ALU.add), waits=[tk], sem=SC)
    tk_idx = P.op("vector", lambda e: e.tensor_copy(s["idxslot"][:, 0:96], s["iwork"][:, 0:96]), waits=[tk], sem=SC)

    for blk in range(32):
        db = blk % 2
        g_toks = []
        for gc in range(3):
            g_toks.append(P.op("gpsimd", lambda e, blk=blk, gc=gc, db=db: e.indirect_dma_start(
                s["rslot"][db][:, gc * 1024:(gc + 1) * 1024], None, s["allOut"][:, :],
                IndirectOffsetOnAxis(s["idxslot"][:, gc * 32 + blk:gc * 32 + blk + 1], axis=0)),
                waits=([tk_idx] + ag_toks if blk <= 1 and gc == 0 else []) +
                      [P.tok(t + "rsfree" + str(db))],
                sem=t + "rg" + str(db), dma=True))
        emit_combine_block(P, s, t, blk, db, g_toks, Xin, Xout, tk_thc,
                           [tk_idx] if blk < 2 else [])

    return P.tok(t + "xow")


def emit_combine_block(P, s, t, blk, db, g_toks, Xin, Xout, tk_thc, w0):
    SD, SC, SMM, SEV = t + "d", t + "c", t + "mm", t + "ev"
    pt_toks = []
    half_done = []
    tk_sc8 = None
    for half in range(2):
        a = 2 * blk + half
        ho = half * 8
        tk = P.op("vector", lambda e, a=a, ho=ho: e.tensor_tensor(
            s["selc"][:, ho:ho + 8], s["probs_tm"][:, a * 8:(a + 1) * 8], s["thcols"][:, 0:8], op=ALU.is_gt),
            waits=[tk_thc, P.tok(t + "pfree")] + w0, sem=SC)
        tk_selbf = P.op("vector", lambda e, ho=ho: e.tensor_copy(s["selcb"][:, ho:ho + 8], s["selc"][:, ho:ho + 8]),
                        waits=[tk], sem=SC)
        tk_r = P.op("tensor", lambda e, ho=ho, half=half: e.matmul(
            s["pG"][:, ho:ho + 8], s["lstrictb"][:, :], s["selcb"][:, ho:ho + 8],
            start=True, stop=(half == 0)), waits=[tk_selbf], sem=SMM)
        if half == 0:
            tk_c = P.op("tensor", lambda e: e.matmul(s["pG"][0:1, 16:24], s["onescolb"][:, 0:1], s["selcb"][:, 0:8],
                                                     start=True, stop=True), waits=[tk_selbf], sem=SMM)
            tk_sc8 = P.op("vector", lambda e: e.tensor_copy(s["sc8"][0:1, 0:8], s["pG"][0:1, 16:24]),
                          waits=[tk_c], sem=SC)
        else:
            tk_r = P.op("tensor", lambda e: e.matmul(s["pG"][:, 8:16], s["onesrowb"][0:1, :], s["sc8"][0:1, 0:8],
                                                     start=False, stop=True), waits=[tk_sc8], sem=SMM)
        tk_rv = P.op("vector", lambda e, ho=ho: e.tensor_copy(s["slotof"][:, ho:ho + 8], s["pG"][:, ho:ho + 8]),
                     waits=[tk_r], sem=SC)
        tk = P.op("vector", lambda e, ho=ho: e.tensor_tensor(s["slotof"][:, ho:ho + 8], s["slotof"][:, ho:ho + 8],
                                                             s["selc"][:, ho:ho + 8], op=ALU.mult),
                  waits=[tk_rv], sem=SC)
        tk2 = P.op("vector", lambda e, ho=ho: e.tensor_scalar(s["sel4k"][:, ho:ho + 8], s["selc"][:, ho:ho + 8],
                                                              -4096.0, 4096.0, op0=ALU.mult, op1=ALU.add),
                   waits=[tk], sem=SC)
        tk = P.op("vector", lambda e, ho=ho: e.tensor_tensor(s["slotof"][:, ho:ho + 8], s["slotof"][:, ho:ho + 8],
                                                             s["sel4k"][:, ho:ho + 8], op=ALU.add),
                  waits=[tk2], sem=SC)
        pe_toks = []
        for ee in range(EXP):
            pe_toks.append(P.op("vector", lambda e, ee=ee, a=a, ho=ho: e.tensor_scalar(
                s["Ptile"][:, ee * SLOTC:(ee + 1) * SLOTC], s["iotaslot"][:, :], s["slotof"][:, ho + ee:ho + ee + 1],
                s["probs_tm"][:, a * 8 + ee:a * 8 + ee + 1], op0=ALU.is_equal, op1=ALU.mult),
                waits=[tk] + (pt_toks[-1:] if pe_toks == [] and pt_toks else []), sem=SC))
        for sc in range(3):
            tkt = P.op("tensor", lambda e, sc=sc: e.transpose(
                s["pT"][:, 0:128], s["Ptile"][:, sc * 128:(sc + 1) * 128], s["identbf"][:, :]),
                waits=pe_toks + (pt_toks[-1:] if pt_toks else []), sem=SMM)
            pt_toks.append(P.op("vector", lambda e, half=half, sc=sc: e.tensor_copy(
                s["PT"][half][:, sc * 128:(sc + 1) * 128], s["pT"][:, 0:128]), waits=[tkt], sem=SEV))
        half_done.append(pt_toks[-1])
    P.op("vector", lambda e: e.tensor_copy(s["dmy"][0:1, 0:1], s["dmy"][0:1, 0:1]),
         waits=half_done, sem=t + "pfree")

    tk_x1 = P.op("sync", lambda e, blk=blk, db=db: e.dma_start(s["xcomb"][db][:, :], Xin[blk * 256:blk * 256 + 128, :]),
                 waits=[P.tok(t + "xcfree" + str(db))] + w0, sem=t + f"xc{db}", dma=True)
    tk_x2 = P.op("sync", lambda e, blk=blk, db=db: e.dma_start(s["xcomb2"][db][:, :], Xin[blk * 256 + 128:(blk + 1) * 256, :]),
                 waits=list(w0), sem=t + f"xc{db}", dma=True)
    ev_toks = []
    for half in range(2):
        for dh in range(2):
            j = half * 2 + dh
            pc = s["pC"] if j % 2 == 0 else s["pD"]
            mmw = g_toks + [half_done[half]]
            wk = "Hpsum" + str(j % 2)
            if wk in s["_tmp"]:
                mmw.append(s["_tmp"][wk])
            tkm = None
            for sc in range(3):
                tkm = P.op("tensor", lambda e, half=half, dh=dh, sc=sc, pc=pc, db=db: e.matmul(
                    pc[:, :], s["PT"][half][:, sc * 128:(sc + 1) * 128],
                    s["rslot"][db][:, sc * 1024 + dh * 512: sc * 1024 + (dh + 1) * 512],
                    start=(sc == 0), stop=(sc == 2)),
                    waits=mmw if sc == 0 else [], sem=(SMM if sc == 2 else None))
            xb = s["xcomb"] if half == 0 else s["xcomb2"]
            tke = P.op("vector", lambda e, dh=dh, pc=pc, db=db, xb=xb: e.scalar_tensor_tensor(
                s["ocomb"][:, dh * 512:(dh + 1) * 512], pc[:, :], 0.5, xb[db][:, dh * 512:(dh + 1) * 512],
                op0=ALU.mult, op1=ALU.add),
                waits=[tkm, tk_x1 if half == 0 else tk_x2, P.tok(t + "xow")], sem=SEV)
            s["_tmp"][wk] = tke
            ev_toks.append(tke)
            if dh == 1:
                P.op("scalar", lambda e, blk=blk, half=half: e.dma_start(
                    Xout[blk * 256 + half * 128: blk * 256 + (half + 1) * 128, :], s["ocomb"][:, :]),
                    waits=ev_toks[-2:], sem=t + "xow", dma=True)
    P.op("vector", lambda e: e.tensor_copy(s["dmy"][0:1, 0:1], s["dmy"][0:1, 0:1]),
         waits=[s["_tmp"]["Hpsum0"], s["_tmp"]["Hpsum1"]], sem=t + "rsfree" + str(db))
    P.op("vector", lambda e: e.tensor_copy(s["dmy"][0:1, 0:1], s["dmy"][0:1, 0:1]),
         waits=ev_toks[-2:], sem=t + "xcfree" + str(db))


CH = 128
NCHUNK = 4096 // CH
GC = GAMMA ** CH


def emit_att(P, s, f1_done, dr):
    t = "T_"
    SC, SMM, SEV = t + "c", t + "mm", t + "ev"
    X1 = dr["X1"]
    X2 = dr["X2"]
    myLN = s["myLN"]

    wl = []
    wl.append(P.op("gpsimd", lambda e: e.dma_start(s["wqsb"][:, :], s["wqt"][:, :]), waits=[f1_done], sem=t + "wl", dma=True))
    wl.append(P.op("gpsimd", lambda e: e.dma_start(s["wksb"][:, :], s["wkt"][:, :]), sem=t + "wl", dma=True))
    wl.append(P.op("gpsimd", lambda e: e.dma_start(s["wvsb"][:, :], s["wvt"][:, :]), sem=t + "wl", dma=True))
    wl.append(P.op("gpsimd", lambda e: e.dma_start(s["wosb"][:, :], s["wot"][:, :]), sem=t + "wl", dma=True))

    c = s["cols"]
    for i in range(64):
        b = i % 2
        co = 8 * b
        tk_x = P.op("sync", lambda e, i=i, b=b: e.dma_start(s["xbuf"][b][:, :], X1[i * 128:(i + 1) * 128, :]),
                    waits=[f1_done] + [P.tok(t + f"lnfree{b}")], sem=t + f"ldx{b}", dma=True)
        tk1 = P.op("vector", lambda e, b=b, co=co: e.reduce_sum(c[:, co:co + 1], s["xbuf"][b][:, :], axis=AX.X),
                   waits=[tk_x], sem=SC)
        tk2 = P.op("scalar", lambda e, b=b, co=co: e.activation(s["lnscr"][:, :], s["xbuf"][b][:, :], AF.Square,
                                                                accum_out=c[:, co + 1:co + 2]),
                   waits=[tk_x], sem=SC)
        tk = P.op("vector", lambda e, co=co: e.tensor_scalar(c[:, co + 2:co + 3], c[:, co:co + 1], 1.0 / D, None, op0=ALU.mult),
                  waits=[tk1], sem=SC)
        tk = P.op("vector", lambda e, co=co: e.tensor_scalar(c[:, co + 3:co + 4], c[:, co + 1:co + 2], 1.0 / D, None, op0=ALU.mult),
                  waits=[tk2, tk], sem=SC)
        tk = P.op("vector", lambda e, co=co: e.tensor_tensor(c[:, co + 4:co + 5], c[:, co + 2:co + 3], c[:, co + 2:co + 3], op=ALU.mult),
                  waits=[tk], sem=SC)
        tk = P.op("vector", lambda e, co=co: e.tensor_tensor(c[:, co + 4:co + 5], c[:, co + 3:co + 4], c[:, co + 4:co + 5], op=ALU.subtract),
                  waits=[tk], sem=SC)
        tk = P.op("vector", lambda e, co=co: e.tensor_scalar(c[:, co + 4:co + 5], c[:, co + 4:co + 5], EPS, None, op0=ALU.add),
                  waits=[tk], sem=SC)
        tk = P.op("scalar", lambda e, co=co: e.activation(c[:, co + 4:co + 5], c[:, co + 4:co + 5], AF.Sqrt),
                  waits=[tk], sem=SC)
        tk = P.op("vector", lambda e, co=co: e.reciprocal(c[:, co + 4:co + 5], c[:, co + 4:co + 5]),
                  waits=[tk], sem=SC)
        tk = P.op("vector", lambda e, co=co: e.tensor_tensor(c[:, co + 5:co + 6], c[:, co + 2:co + 3], c[:, co + 4:co + 5], op=ALU.mult),
                  waits=[tk], sem=SC)
        tk = P.op("vector", lambda e, co=co: e.tensor_scalar(c[:, co + 5:co + 6], c[:, co + 5:co + 6], -1.0, None, op0=ALU.mult),
                  waits=[tk], sem=SC)
        tk_ln = P.op("scalar", lambda e, b=b, co=co: e.activation(s["lnbuf"][b][:, :], s["xbuf"][b][:, :], AF.Identity,
                                                                  bias=c[:, co + 5:co + 6], scale=c[:, co + 4:co + 5]),
                     waits=[tk, P.tok(t + "ln")], sem=t + f"lnfree{b}")
        P.op("scalar", lambda e, i=i, b=b: e.dma_start(myLN[i * 128:(i + 1) * 128, :], s["lnbuf"][b][:, :]),
             waits=[tk_ln], sem=t + "ln", dma=True)
    base_ln = P.tok(t + "ln")[1] - 64 * 16

    prev_evac = {}
    prev_sp = {}
    wsbs = {0: "wqsb", 1: "wksb", 2: "wvsb"}
    fmds = {0: "QfmD", 1: "KfmD", 2: "VfmD"}
    for g in range(NGRP):
        pb = g % 2
        lnp = s["lnpA"] if pb == 0 else s["lnpB"]
        wln = [(t + "ln@scalar", base_ln + 16 * 4 * (g + 1))]
        ld_toks = []
        for dc in range(8):
            ld_toks.append(P.op("sync", lambda e, g=g, dc=dc, lnp=lnp: e.dma_start(
                lnp[:, dc * 512:(dc + 1) * 512],
                myLN[g * 512:(g + 1) * 512, dc * 128:(dc + 1) * 128], transpose=True),
                waits=wln + wl + ([P.tok(t + f"prel{pb}")] if g >= 2 else []), sem=t + f"pl{pb}", dma=True))
        mm_last = []
        for j in range(12):
            pj, hdc = j // 4, j % 4
            psn = "pF" if j % 2 == 0 else "pG"
            spb = s["spA"] if j % 2 == 0 else s["spB"]
            mmw = list(ld_toks)
            if (j % 2) in prev_evac:
                mmw.append(prev_evac[j % 2])
            tkm = None
            for dc in range(8):
                tkm = P.op("tensor", lambda e, pj=pj, hdc=hdc, dc=dc, psn=psn, lnp=lnp: e.matmul(
                    s[psn][:, :], s[wsbs[pj]][:, dc * 512 + hdc * 128: dc * 512 + (hdc + 1) * 128],
                    lnp[:, dc * 512:(dc + 1) * 512], start=(dc == 0), stop=(dc == 7)),
                    waits=mmw if dc == 0 else [], sem=(SMM if dc == 7 else None))
            mm_last.append(tkm)
            tke = P.op("vector", lambda e, psn=psn, spb=spb: e.tensor_copy(spb[:, :], s[psn][:, :]),
                       waits=[tkm] + ([prev_sp[j % 2]] if (j % 2) in prev_sp else []), sem=SEV)
            prev_evac[j % 2] = tke
            prev_sp[j % 2] = P.op("scalar", lambda e, pj=pj, hdc=hdc, g=g, spb=spb: e.dma_start(
                s[fmds[pj]][hdc * 128:(hdc + 1) * 128, g * 512:(g + 1) * 512], spb[:, :]),
                waits=[tke], sem=t + "sp", dma=True)
        P.op("vector", lambda e: e.tensor_copy(s["dmy"][0:1, 2:3], s["dmy"][0:1, 0:1]),
             waits=mm_last[-2:], sem=t + f"prel{pb}")
    proj_done = P.tok(t + "sp")
    ml = []
    for cn, sn in [("c_mA4", "mA4"), ("c_mA24", "mA24"), ("c_mB2p4", "mB2p4"), ("c_mX124", "mX124"),
                   ("c_mB2T4", "mB2T4"), ("c_mD", "mD"), ("c_mqd", "mqd"), ("c_mblk", "mblk"),
                   ("c_kgcol", "kgcol")]:
        ml.append(P.op("sync", lambda e, cn=cn, sn=sn: e.dma_start(s[sn][:, :], s[cn][:, :]),
                       waits=[proj_done, P.tok(t + "prel0"), P.tok(t + "prel1")] if not ml else [],
                       sem=t + "ml", dma=True))

    pA, pB, pC, pD, pE, pF2, pG2 = s["pA"], s["pB"], s["pC"], s["pD"], s["pE"], s["pF"], s["pG"]
    v4b = lambda ap: ap.rearrange("p (s i) -> p s i", i=CH)
    m1b = lambda ap: ap.rearrange("p (one i) -> p one i", one=1).to_broadcast([128, 4, CH])
    m1b32 = lambda ap: ap.rearrange("p (one i) -> p one i", one=1).to_broadcast([32, 4, CH])
    T = s["_tmp"]
    for k in ["aQK1", "aQK2", "aP12", "aO", "aPE", "aPF", "aPG", "aOF"]:
        T.pop(k, None)
    ST = {}
    for gp in range(4):
        for sl in (0, 1):
            g8 = 2 * gp + sl
            tkz = P.op("vector", lambda e, g8=g8: e.memset(s["stm"][0:32, g8 * 384:(g8 + 1) * 384], 0.0),
                       waits=[proj_done] + ([T["aSb" + str(sl)]] if ("aSb" + str(sl)) in T else []), sem=SC)
            tkz2 = P.op("vector", lambda e, sl=sl: e.memset(s[f"Sb4{sl}"][0:32, :], 0.0), waits=[tkz], sem=SC)
            tkz2 = P.op("vector", lambda e, sl=sl: e.memset(s[f"Cb4{sl}"][0:32, :], 0.0), waits=[tkz2], sem=SC)
            ST[sl] = P.op("vector", lambda e, sl=sl: e.memset(s[f"Gnb4{sl}"][0:32, :], 0.0), waits=[tkz2], sem=SC)
        for ck in range(NCHUNK):
          for sl in (0, 1):
            g8 = 2 * gp + sl
            seq, hg = g8 // 4, g8 % 4
            db = sl
            tk_st0 = ST[sl]
            tok0 = seq * 4096 + ck * CH
            Sb4_, Cb4_, Gnb4_ = s[f"Sb4{sl}"], s[f"Cb4{sl}"], s[f"Gnb4{sl}"]
            lw = ([proj_done] + ml if gp == 0 and ck == 0 else []) + \
                 ([P.tok(t + f"lr{db}")] if ck >= 1 else [])
            l_toks = []
            for st4 in range(4):
                r0 = 128 * hg + 32 * st4
                l_toks.append(P.op("sync", lambda e, st4=st4, r0=r0, tok0=tok0, db=db: e.dma_start(
                    s[f"qT4{db}"][0:32, st4 * CH:(st4 + 1) * CH], s["QfmD"][r0:r0 + 32, tok0:tok0 + CH]),
                    waits=lw if st4 == 0 else [], sem=t + f"ld{db}", dma=True))
                l_toks.append(P.op("sync", lambda e, st4=st4, r0=r0, tok0=tok0, db=db: e.dma_start(
                    s[f"kT4{db}"][0:32, st4 * CH:(st4 + 1) * CH], s["KfmD"][r0:r0 + 32, tok0:tok0 + CH]),
                    sem=t + f"ld{db}", dma=True))
                l_toks.append(P.op("sync", lambda e, st4=st4, r0=r0, tok0=tok0, db=db: e.dma_start(
                    s[f"q4tm{db}"][:, st4 * 32:(st4 + 1) * 32], s["QfmD"][r0:r0 + 32, tok0:tok0 + CH], transpose=True),
                    sem=t + f"ld{db}", dma=True))
                l_toks.append(P.op("sync", lambda e, st4=st4, r0=r0, tok0=tok0, db=db: e.dma_start(
                    s[f"k4tm{db}"][:, st4 * 32:(st4 + 1) * 32], s["KfmD"][r0:r0 + 32, tok0:tok0 + CH], transpose=True),
                    sem=t + f"ld{db}", dma=True))
                l_toks.append(P.op("sync", lambda e, st4=st4, r0=r0, tok0=tok0, db=db: e.dma_start(
                    s[f"v4tm{db}"][:, st4 * 32:(st4 + 1) * 32], s["VfmD"][r0:r0 + 32, tok0:tok0 + CH], transpose=True),
                    sem=t + f"ld{db}", dma=True))
            ld_tok = l_toks[-1]
            qT4, kT4 = s[f"qT4{db}"], s[f"kT4{db}"]
            q4tm, k4tm, v4tm = s[f"q4tm{db}"], s[f"k4tm{db}"], s[f"v4tm{db}"]

            qk1 = []
            for st4 in range(4):
                qk1.append(P.op("tensor", lambda e, st4=st4, qT4=qT4, kT4=kT4: e.matmul(
                    pA[:, st4 * CH:(st4 + 1) * CH], kT4[0:32, st4 * CH:(st4 + 1) * CH],
                    qT4[0:32, st4 * CH:(st4 + 1) * CH], start=True, stop=True),
                    waits=([ld_tok] if st4 == 0 else []) + ([T["aQK1"]] if st4 == 0 and "aQK1" in T else []) +
                          ([T["aBp"]] if st4 == 0 and "aBp" in T else []),
                    sem=SMM))
            qk2 = []
            for st4 in range(4):
                qk2.append(P.op("tensor", lambda e, st4=st4, qT4=qT4, kT4=kT4: e.matmul(
                    pB[:, st4 * CH:(st4 + 1) * CH], qT4[0:32, st4 * CH:(st4 + 1) * CH],
                    kT4[0:32, st4 * CH:(st4 + 1) * CH], start=True, stop=True),
                    waits=([T["aQK2"]] if st4 == 0 and "aQK2" in T else []), sem=SMM))
            qpp = []
            for st4 in range(4):
                qpp.append(P.op("tensor", lambda e, st4=st4, qT4=qT4, Sb4_=Sb4_: e.matmul(
                    pE[0:32, st4 * CH:(st4 + 1) * CH], Sb4_[0:32, st4 * 32:(st4 + 1) * 32],
                    qT4[0:32, st4 * CH:(st4 + 1) * CH], start=True, stop=True),
                    waits=([tk_st0] if st4 == 0 else []) + ([T["aPE"]] if st4 == 0 and "aPE" in T else []),
                    sem=SMM))

            e_bp = P.op("scalar", lambda e: e.activation(s["Bp4"][:, :], pA[:, :], AF.Copy), waits=qk1, sem=SEV)
            T["aBp"] = e_bp
            tk = None
            for nm, msk in [("Ap4", "mA4"), ("A2n4", "mA24"), ("B2p4", "mB2p4"), ("X12_4", "mX124")]:
                tk = P.op("vector", lambda e, nm=nm, msk=msk: e.tensor_tensor(
                    s[nm][:, :], pA[:, :], s[msk][:, :], op=ALU.mult),
                    waits=qk1 + [e_bp] if tk is None else [], sem=SC)
            e_x12 = tk
            T["aQK1"] = e_x12
            e_b2t = P.op("vector", lambda e: e.tensor_tensor(
                s["B2T4"][:, :], pB[:, :], s["mB2T4"][:, :], op=ALU.mult),
                waits=qk2 + [e_x12], sem=SC)
            T["aQK2"] = e_b2t
            e_qpp = P.op("vector", lambda e: e.tensor_scalar(s["QppT4"][0:32, :], pE[0:32, :], GAMMA, None, op0=ALU.mult),
                         waits=qpp + [e_b2t], sem=SC)

            p12 = []
            for st4 in range(4):
                sl = slice(st4 * CH, (st4 + 1) * CH)
                w0 = [e_bp, e_x12, e_qpp] + ([T["aP12"]] if "aP12" in T else []) if st4 == 0 else []
                P.op("tensor", lambda e, sl=sl: e.matmul(pC[:, sl], s["Bp4"][:, sl], s["Ap4"][:, sl],
                                                         start=True, stop=False), waits=w0, sem=None)
                P.op("tensor", lambda e, sl=sl, qT4=qT4: e.matmul(pC[:, sl], qT4[0:32, sl], s["QppT4"][0:32, sl],
                                                                  start=False, stop=False), sem=None)
                p12.append(P.op("tensor", lambda e, sl=sl: e.matmul(pC[:, sl], s["B2p4"][:, sl], s["A2n4"][:, sl],
                                                                    start=False, stop=True), sem=SMM))
            tk = None
            for st4 in range(4):
                sl = slice(st4 * CH, (st4 + 1) * CH)
                tk = P.op("vector", lambda e, sl=sl: e.tensor_tensor(
                    s["DT4"][:, sl], pC[:, sl], s["mD"][:, :], op=ALU.mult),
                    waits=p12 if tk is None else [], sem=SC)
            e_dt = tk
            T["aP12"] = e_dt

            u_mm = []
            for st4 in range(4):
                u_mm.append(P.op("tensor", lambda e, st4=st4, kT4=kT4, Cb4_=Cb4_: e.matmul(
                    pF2[:, 32 * st4:32 * (st4 + 1)], kT4[0:32, st4 * CH:(st4 + 1) * CH],
                    Cb4_[0:32, st4 * 32:(st4 + 1) * 32], start=True, stop=True),
                    waits=([tk_st0] + ([T["aPF"]] if "aPF" in T else []) if st4 == 0 else []), sem=SMM))
            e_u = P.op("scalar", lambda e: e.activation(s["U4"][:, 0:128], pF2[:, 0:128], AF.Copy),
                       waits=u_mm, sem=SEV)
            z_mm = []
            for st4 in range(4):
                z_mm.append(P.op("tensor", lambda e, st4=st4, Sb4_=Sb4_, Cb4_=Cb4_: e.matmul(
                    pG2[0:32, 32 * st4:32 * (st4 + 1)], Sb4_[0:32, st4 * 32:(st4 + 1) * 32],
                    Cb4_[0:32, st4 * 32:(st4 + 1) * 32], start=True, stop=True),
                    waits=([tk_st0] + ([T["aPG"]] if "aPG" in T else []) if st4 == 0 else []), sem=SMM))
            e_z = P.op("scalar", lambda e: e.activation(s["Z4"][0:32, 0:128], pG2[0:32, 0:128], AF.Copy),
                       waits=z_mm, sem=SEV)
            e_qd1 = P.op("vector", lambda e, qT4=qT4: e.tensor_tensor(
                v4b(s["Qd1T4"][0:32, :]), v4b(qT4[0:32, :]), m1b32(s["mqd"][0:32, 0:CH]), op=ALU.mult),
                waits=[ld_tok] + ([T["aO"]] if "aO" in T else []), sem=SC)
            e_qd2 = P.op("vector", lambda e, qT4=qT4: e.tensor_tensor(
                v4b(s["Qd2T4"][0:32, :]), v4b(qT4[0:32, :]), m1b32(s["mqd"][0:32, CH:2 * CH]), op=ALU.mult),
                waits=[e_qd1], sem=SC)
            T["aQd"] = e_qd2

            o_mm = []
            w0 = [e_dt, e_u, e_z, e_qd2] + ([T["aO"]] if "aO" in T else [])
            P.op("tensor", lambda e, v4tm=v4tm: e.matmul(pD[:, :], v4tm[:, :], s["DT4"][:, :],
                                                         start=True, stop=False), waits=w0, sem=None)
            P.op("tensor", lambda e: e.matmul(pD[:, :], s["U4"][:, 0:128], s["X12_4"][:, :],
                                              start=False, stop=False), sem=None)
            P.op("tensor", lambda e: e.matmul(pD[:, :], s["Z4"][0:32, 0:128], s["Qd2T4"][0:32, :],
                                              start=False, stop=False), sem=None)
            o_mm.append(P.op("tensor", lambda e, Gnb4_=Gnb4_: e.matmul(pD[:, :], Gnb4_[0:32, :], s["Qd1T4"][0:32, :],
                                                                       start=False, stop=True), sem=SMM))
            e_om = P.op("vector", lambda e: e.tensor_tensor(s["Om4"][:, :], pD[:, :], s["mblk"][:, :], op=ALU.mult),
                        waits=o_mm + ([T["aOF"]] if "aOF" in T else []), sem=SC)
            T["aO"] = e_om
            def _fold(e):
                with P.nc.allow_low_precision(reason="band-masked sum: 3 of 4 addends are exactly zero"):
                    return e.reduce_sum(s["ofm"][:, :], s["Om4"][:, :].rearrange("p (s i) -> p i s", s=4), axis=AX.X)
            e_of = P.op("vector", _fold, waits=[e_om] + ([T["aOFD"]] if "aOFD" in T else []), sem=SC)
            T["aOF"] = e_of
            T["aOFD"] = P.op("scalar", lambda e, hg=hg, tok0=tok0: e.dma_start(
                s["OFmD"][128 * hg:128 * (hg + 1), tok0:tok0 + CH], s["ofm"][:, :]),
                waits=[e_of], sem=t + "of", dma=True)

            e_kg = P.op("vector", lambda e, k4tm=k4tm: e.tensor_tensor(
                s["Kg4"][:, :], k4tm[:, :], s["kgcol"][:, 0:1].to_broadcast([128, 128]), op=ALU.mult),
                waits=[ld_tok] + (list(T["aKgR"]) if "aKgR" in T else []), sem=SC)
            e_qg = P.op("vector", lambda e, q4tm=q4tm: e.tensor_tensor(
                s["Qg4"][:, :], q4tm[:, :], s["kgcol"][:, 0:1].to_broadcast([128, 128]), op=ALU.mult),
                waits=[e_kg], sem=SC)
            mm_s = P.op("tensor", lambda e, k4tm=k4tm: e.matmul(pE[:, 0:128], s["Kg4"][:, :], k4tm[:, :],
                                                                start=True, stop=True), waits=[e_qpp, e_kg], sem=SMM)
            mm_c = P.op("tensor", lambda e, v4tm=v4tm: e.matmul(pE[:, 128:256], s["Qg4"][:, :], v4tm[:, :],
                                                                start=True, stop=True), waits=[e_qg], sem=SMM)
            mm_kk = P.op("tensor", lambda e, k4tm=k4tm: e.matmul(pE[:, 256:384], k4tm[:, :], k4tm[:, :],
                                                                 start=True, stop=True), sem=SMM)
            T["aKgR"] = (mm_s, mm_c)
            kkev = []
            for st4 in range(4):
                kkev.append(P.op("vector", lambda e, st4=st4: e.tensor_scalar(
                    s["KTKg4"][0:32, st4 * 32:(st4 + 1) * 32], pE[32 * st4:32 * (st4 + 1), 256 + 32 * st4:256 + 32 * (st4 + 1)],
                    GC, None, op0=ALU.mult), waits=[mm_kk] if st4 == 0 else [], sem=SC))
            b2v_mm = []
            for st4 in range(4):
                b2v_mm.append(P.op("tensor", lambda e, st4=st4, v4tm=v4tm: e.matmul(
                    pG2[:, 128 + 32 * st4:128 + 32 * (st4 + 1)], s["B2T4"][:, st4 * CH:(st4 + 1) * CH],
                    v4tm[:, st4 * 32:(st4 + 1) * 32], start=True, stop=True),
                    waits=[e_z, e_b2t] if st4 == 0 else [], sem=SMM))
            e_b2v = P.op("scalar", lambda e: e.activation(s["B2V4"][:, :], pG2[:, 128:256], AF.Copy),
                         waits=b2v_mm, sem=SEV)
            mm_t2 = P.op("tensor", lambda e, k4tm=k4tm: e.matmul(pF2[:, 128:256], k4tm[:, :], s["B2V4"][:, :],
                                                                 start=True, stop=True), waits=[e_u, e_b2v], sem=SMM)
            t3_mm = []
            for st4 in range(4):
                t3_mm.append(P.op("tensor", lambda e, st4=st4, Cb4_=Cb4_: e.matmul(
                    pF2[0:32, 256 + 32 * st4:256 + 32 * (st4 + 1)], s["KTKg4"][0:32, st4 * 32:(st4 + 1) * 32],
                    Cb4_[0:32, st4 * 32:(st4 + 1) * 32], start=True, stop=True),
                    waits=kkev if st4 == 0 else [], sem=SMM))
            stw = []
            for st4 in range(4):
                mo = (g8 * 4 + st4) * 96
                stw.append(P.op("vector", lambda e, st4=st4, mo=mo: e.scalar_tensor_tensor(
                    s["stm"][0:32, mo:mo + 32], s["stm"][0:32, mo:mo + 32], GC,
                    pE[32 * st4:32 * (st4 + 1), 32 * st4:32 * (st4 + 1)], op0=ALU.mult, op1=ALU.add),
                    waits=[mm_kk] if st4 == 0 else [], sem=SC))
                stw.append(P.op("vector", lambda e, st4=st4, mo=mo: e.scalar_tensor_tensor(
                    s["stm"][0:32, mo + 32:mo + 64], s["stm"][0:32, mo + 32:mo + 64], GC,
                    pE[32 * st4:32 * (st4 + 1), 128 + 32 * st4:128 + 32 * (st4 + 1)], op0=ALU.mult, op1=ALU.add),
                    waits=[], sem=SC))
                stw.append(P.op("vector", lambda e, st4=st4, mo=mo: e.scalar_tensor_tensor(
                    s["stm"][0:32, mo + 64:mo + 96], s["stm"][0:32, mo + 64:mo + 96], GC,
                    pF2[32 * st4:32 * (st4 + 1), 128 + 32 * st4:128 + 32 * (st4 + 1)], op0=ALU.mult, op1=ALU.subtract),
                    waits=[t3_mm[3]] if st4 == 0 else [], sem=SC))
                stw.append(P.op("vector", lambda e, st4=st4, mo=mo: e.tensor_tensor(
                    s["stm"][0:32, mo + 64:mo + 96], s["stm"][0:32, mo + 64:mo + 96],
                    pF2[0:32, 256 + 32 * st4:256 + 32 * (st4 + 1)], op=ALU.subtract),
                    waits=[t3_mm[st4]], sem=SC))
            T["aPE"] = stw[-3]
            T["aPF"] = stw[-1]
            T["aPG"] = e_b2v
            cpw = stw + [e_om]
            for st4 in range(4):
                mo = (g8 * 4 + st4) * 96
                tkc = P.op("vector", lambda e, st4=st4, mo=mo, Sb4_=Sb4_: e.tensor_copy(
                    Sb4_[0:32, st4 * 32:(st4 + 1) * 32], s["stm"][0:32, mo:mo + 32]),
                    waits=cpw if st4 == 0 else [], sem=SC)
                tkc = P.op("vector", lambda e, st4=st4, mo=mo, Cb4_=Cb4_: e.tensor_copy(
                    Cb4_[0:32, st4 * 32:(st4 + 1) * 32], s["stm"][0:32, mo + 32:mo + 64]), sem=SC)
                tkc = P.op("vector", lambda e, st4=st4, mo=mo, Gnb4_=Gnb4_: e.tensor_copy(
                    Gnb4_[0:32, st4 * 32:(st4 + 1) * 32], s["stm"][0:32, mo + 64:mo + 96]), sem=SC)
            ST[sl] = tkc
            T["aSb" + str(sl)] = tkc
            P.op("vector", lambda e: e.tensor_copy(s["dmy"][0:1, 3:4], s["dmy"][0:1, 0:1]),
                 waits=[e_om, stw[-1], e_qd2, e_qg], sem=t + f"lr{db}")

    scan_done = P.tok(t + "of")

    prev_ev = {}
    for i in range(64):
        db = i % 2
        tk_x = P.op("sync", lambda e, i=i, db=db: e.dma_start(s["xbuf"][db][:, :], X1[i * 128:(i + 1) * 128, :]),
                    waits=[scan_done] + ([P.tok(t + f"cr{db}")] if i >= 2 else []), sem=t + f"xl{db}", dma=True)
        o_toks = []
        for hgg in range(4):
            o_toks.append(P.op("sync", lambda e, i=i, hgg=hgg, db=db: e.dma_start(
                s[f"of4{db}"][:, hgg * 128:(hgg + 1) * 128], s["OFmD"][128 * hgg:128 * (hgg + 1), i * 128:(i + 1) * 128]),
                sem=t + f"xl{db}", dma=True))
        ev2 = []
        for dh in range(2):
            psn = "pF" if dh == 0 else "pG"
            tkm = None
            for hgg in range(4):
                w0 = []
                if hgg == 0:
                    w0 = o_toks + wl + ([prev_ev[dh]] if dh in prev_ev else [])
                    if i == 0:
                        w0 += [T[k] for k in ("aPF", "aPG") if k in T]
                tkm = P.op("tensor", lambda e, hgg=hgg, dh=dh, psn=psn, db=db: e.matmul(
                    s[psn][:, :], s[f"of4{db}"][:, hgg * 128:(hgg + 1) * 128],
                    s["wosb"][:, hgg * 1024 + dh * 512: hgg * 1024 + (dh + 1) * 512],
                    start=(hgg == 0), stop=(hgg == 3)),
                    waits=w0, sem=(SMM if hgg == 3 else None))
            tke = P.op("vector", lambda e, dh=dh, psn=psn, db=db: e.tensor_tensor(
                s["ocomb"][:, dh * 512:(dh + 1) * 512], s[psn][:, :], s["xbuf"][db][:, dh * 512:(dh + 1) * 512],
                op=ALU.add), waits=[tkm, tk_x, P.tok(t + "xw")], sem=SEV)
            prev_ev[dh] = tke
            ev2.append(tke)
        P.op("scalar", lambda e, i=i: e.dma_start(X2[i * 128:(i + 1) * 128, :], s["ocomb"][:, :]),
             waits=ev2, sem=t + "xw", dma=True)
        P.op("vector", lambda e: e.tensor_copy(s["dmy"][0:1, 3:4], s["dmy"][0:1, 0:1]),
             waits=ev2, sem=t + f"cr{db}")
    return P.tok(t + "xw")


def build(nc):
    import contextlib
    stack = contextlib.ExitStack()

    x0 = nc.declare_dram_parameter("x0", [NT, D], F32, isOutput=False)
    w1t = nc.declare_dram_parameter("w1t", [128, 32768], F32, isOutput=False)
    w2t = nc.declare_dram_parameter("w2t", [128, 32768], F32, isOutput=False)
    b1t = nc.declare_dram_parameter("b1t", [128, 32], F32, isOutput=False)
    b2r = nc.declare_dram_parameter("b2r", [1, D], F32, isOutput=False)
    wst = nc.declare_dram_parameter("wst", [128, 64], F32, isOutput=False)
    bsI = nc.declare_dram_parameter("bsI", [8, 1], F32, isOutput=False)
    ehotI = nc.declare_dram_parameter("ehotI", [128, 8], F32, isOutput=False)
    xout = nc.declare_dram_parameter("xout", [NT, D], F32, isOutput=True)
    wqt = nc.declare_dram_parameter("wqt", [128, 4096], F32, isOutput=False)
    wkt = nc.declare_dram_parameter("wkt", [128, 4096], F32, isOutput=False)
    wvt = nc.declare_dram_parameter("wvt", [128, 4096], F32, isOutput=False)
    wot = nc.declare_dram_parameter("wot", [128, 4096], F32, isOutput=False)
    cdefs = [
        ("c_mA4", [128, 512], F32), ("c_mA24", [128, 512], F32), ("c_mB2p4", [128, 512], F32),
        ("c_mX124", [128, 512], F32), ("c_mB2T4", [128, 512], F32), ("c_mD", [128, 128], F32),
        ("c_mqd", [32, 256], BF16), ("c_mblk", [128, 512], F32), ("c_kgcol", [128, 1], BF16),
        ("c_lstrictb", [128, 128], BF16), ("c_lstrictf", [128, 128], F32),
        ("c_identbf", [128, 128], BF16), ("c_onescolb", [128, 1], BF16),
        ("c_onescolf", [128, 1], F32), ("c_onesrowb", [1, 128], BF16), ("c_onesrowf", [1, 128], F32),
        ("c_onesff", [128, 128], F32),
        ("c_iotaslot", [128, SLOTC], F32), ("c_gid", [128, 512], F32),
        ("c_piog", [128, 64], F32),
        ("c_keslot3", [128, 3], F32), ("c_eoff3", [128, 96], F32), ("c_zrowcol", [128, 96], F32),
        ("c_emap", [8, 384], F32), ("c_zrow", [1, 512], F32),
    ]
    consts = {}
    for name, shape, dt in cdefs:
        consts[name] = nc.declare_dram_parameter(name, shape, dt, isOutput=False)

    dr = {}
    dr["X1"] = nc.dram_tensor("X1", [NT, D], F32)
    dr["X2"] = nc.dram_tensor("X2", [NT, D], F32)
    dr["myLN"] = nc.dram_tensor("myLN", [NT, D], BF16)
    dr["LNall"] = nc.dram_tensor("LNall", [N, D], BF16)
    dr["probsMine"] = nc.dram_tensor("probsMine", [NT, 8], F32)
    dr["probsAll"] = nc.dram_tensor("probsAll", [N, 8], F32)
    dr["thMine"] = nc.dram_tensor("thMine", [1, 1], F32)
    dr["thAll"] = nc.dram_tensor("thAll", [8, 1], F32)
    dr["Srows"] = nc.dram_tensor("Srows", [8, 640], F32)
    dr["pagesKB"] = nc.dram_tensor("pagesKB", [KCAP * 512, 1], F32)
    dr["piAdjD"] = nc.dram_tensor("piAdjD", [128 * 65, 1], F32)
    dr["rowD"] = nc.dram_tensor("rowD", [3, 512], F32)
    dr["CX"] = nc.dram_tensor("CX", [NT, D], BF16)
    dr["compact"] = nc.dram_tensor("compact", [CO_ROWS, D], BF16)
    dr["allOut"] = nc.dram_tensor("allOut", [8 * CO_ROWS, D], BF16)
    dr["QfmD"] = nc.dram_tensor("QfmD", [512, NT], BF16)
    dr["KfmD"] = nc.dram_tensor("KfmD", [512, NT], BF16)
    dr["VfmD"] = nc.dram_tensor("VfmD", [512, NT], BF16)
    dr["OFmD"] = nc.dram_tensor("OFmD", [512, NT], BF16)

    sb = {}

    def S(name, shape, dt):
        sb[name] = stack.enter_context(nc.sbuf_tensor(name, shape, dt))

    S("w1sb", [128, 32768], BF16)
    S("w2sb", [128, 32768], BF16)
    hsb_off = nc.sbuf_base
    S("hsb", [128, 16384], BF16)
    A = lambda name, shape, dt, off: sb.__setitem__(name, nc.alloc_sbuf_tensor_at(name, shape, dt, offset=off))
    A("pAll", [128, 4096], F32, hsb_off)
    A("Ptile", [128, NSLOT], BF16, hsb_off + 0)
    A("PT0", [128, NSLOT], BF16, hsb_off + 768)
    A("PT1", [128, NSLOT], BF16, hsb_off + 1536)
    A("selc", [128, 16], F32, hsb_off + 2304)
    A("selcb", [128, 16], BF16, hsb_off + 2368)
    A("slotof", [128, 16], F32, hsb_off + 2400)
    A("sel4k", [128, 16], F32, hsb_off + 2464)
    A("sc8", [1, 8], BF16, hsb_off + 2528)
    A("iwork2", [128, 96], F32, hsb_off + 2560)
    A("iwork3", [128, 96], F32, hsb_off + 2944)
    A("ipred", [128, 96], F32, hsb_off + 3328)
    A("idxslot", [128, 96], I32, hsb_off + 3712)
    A("smine", [8, 128], F32, hsb_off + 4096)
    A("sEven", [8, 34], F32, hsb_off + 4608)
    A("cEven", [8, 32], F32, hsb_off + 4768)
    A("emap", [8, 384], F32, hsb_off + 4896)
    A("eoff3", [128, 96], F32, hsb_off + 6432)
    A("zrowcol", [128, 96], F32, hsb_off + 6816)
    A("iotaslot", [128, SLOTC], F32, hsb_off + 7200)
    A("keslot3", [128, 3], F32, hsb_off + 7392)
    A("gbuf0", [128, 1024], BF16, hsb_off + 8192)
    A("gbuf1", [128, 1024], BF16, hsb_off + 10240)
    A("xcomb21", [128, 1024], F32, hsb_off + 8192)
    A("cmpA", [128, 512], F32, hsb_off + 16384)
    A("cmpB", [128, 512], F32, hsb_off + 18432)
    A("rmine", [128, 512], F32, hsb_off + 20480)
    A("selbf", [128, 512], BF16, hsb_off + 22528)
    A("row1", [1, 512], F32, hsb_off + 23552)
    A("row2", [1, 512], F32, hsb_off + 25600)
    A("row3", [1, 512], F32, hsb_off + 27648)
    A("row4", [1, 512], F32, hsb_off + 29696)
    xfm_off = nc.sbuf_base
    S("xfm0", [128, 4096], BF16)
    A("srow", [1, 640], F32, xfm_off)
    A("cntrow", [1, 512], F32, xfm_off + 2560)
    A("sown", [1, 513], F32, xfm_off + 4608)
    A("svals", [128, 96], F32, xfm_off + 6688)
    A("cvals", [128, 96], F32, xfm_off + 7072)
    A("iwork", [128, 96], F32, xfm_off + 7456)
    rslot_off = nc.sbuf_base
    S("rslot0", [128, 3072], BF16)
    S("rslot1", [128, 3072], BF16)
    A("gid", [128, 512], F32, rslot_off)
    A("zrow", [1, 512], F32, rslot_off + 2048)
    A("row5", [1, 512], F32, rslot_off + 4096)
    A("ztile", [128, 65], F32, rslot_off + 6144)
    A("atile", [128, 65], F32, rslot_off + 6432)
    A("A65", [128, 65], F32, rslot_off + 6720)
    A("B65", [128, 64], F32, rslot_off + 7008)
    A("C65", [128, 64], F32, rslot_off + 7264)
    A("pii", [128, 64], I32, rslot_off + 7520)
    A("idvf", [128, 64], F32, rslot_off + 7776)
    A("w64a", [128, 64], F32, rslot_off + 8032)
    A("w64b", [128, 64], F32, rslot_off + 8288)
    A("w64c", [128, 64], F32, rslot_off + 8544)
    A("rowsi", [128, 64], I32, rslot_off + 8800)
    A("piog", [128, 64], F32, rslot_off + 9056)
    A("logit_bf", [8, 512], BF16, rslot_off + 9312)
    A("lnfm0", [128, 512], BF16, rslot_off + 10336)
    A("ltm", [128, 64], F32, rslot_off + 11360)
    A("red1", [128, 8], F32, rslot_off + 11616)
    xbuf_off = nc.sbuf_base
    S("xbuf0", [128, 1024], F32)
    S("xbuf1", [128, 1024], F32)
    A("xfm1", [128, 4096], BF16, xbuf_off)
    A("lnfm1", [128, 512], BF16, hsb_off + 14336)
    A("lnpA", [128, 4096], BF16, hsb_off + 0)
    A("lnpB", [128, 4096], BF16, rslot_off + 0)
    A("wqsb", [128, 4096], BF16, hsb_off + 8192)
    A("wksb", [128, 4096], BF16, hsb_off + 16384)
    A("wvsb", [128, 4096], BF16, hsb_off + 24576)
    A("wosb", [128, 4096], BF16, xfm_off)
    A("Bp4", [128, 512], BF16, hsb_off + 0)
    A("Ap4", [128, 512], BF16, hsb_off + 1024)
    A("A2n4", [128, 512], BF16, hsb_off + 2048)
    A("B2p4", [128, 512], BF16, hsb_off + 3072)
    A("X12_4", [128, 512], BF16, hsb_off + 4096)
    A("B2T4", [128, 512], BF16, hsb_off + 5120)
    A("QppT4", [32, 512], BF16, hsb_off + 6144)
    A("DT4", [128, 512], BF16, hsb_off + 7168)
    A("U4", [128, 128], BF16, hsb_off + 8192)
    A("Z4", [32, 128], BF16, hsb_off + 8448)
    A("Qd1T4", [32, 512], BF16, hsb_off + 8704)
    A("Qd2T4", [32, 512], BF16, hsb_off + 9728)
    A("Kg4", [128, 128], BF16, hsb_off + 10752)
    A("Qg4", [128, 128], BF16, hsb_off + 11008)
    A("B2V4", [128, 128], BF16, hsb_off + 11264)
    A("KTKg4", [32, 128], BF16, hsb_off + 11520)
    A("Sb40", [32, 128], BF16, hsb_off + 11776)
    A("Cb40", [32, 128], BF16, hsb_off + 12032)
    A("Gnb40", [32, 128], BF16, hsb_off + 12288)
    A("Sb41", [32, 128], BF16, hsb_off + 28672)
    A("Cb41", [32, 128], BF16, hsb_off + 28928)
    A("Gnb41", [32, 128], BF16, hsb_off + 29184)
    A("Om4", [128, 512], BF16, hsb_off + 12544)
    A("ofm", [128, 128], BF16, hsb_off + 13568)
    A("qT40", [32, 512], BF16, hsb_off + 13824)
    A("qT41", [32, 512], BF16, hsb_off + 14848)
    A("kT40", [32, 512], BF16, hsb_off + 15872)
    A("kT41", [32, 512], BF16, hsb_off + 16896)
    A("q4tm0", [128, 128], BF16, hsb_off + 17920)
    A("q4tm1", [128, 128], BF16, hsb_off + 18176)
    A("k4tm0", [128, 128], BF16, hsb_off + 18432)
    A("k4tm1", [128, 128], BF16, hsb_off + 18688)
    A("v4tm0", [128, 128], BF16, hsb_off + 18944)
    A("v4tm1", [128, 128], BF16, hsb_off + 19200)
    A("of40", [128, 512], BF16, hsb_off + 0)
    A("of41", [128, 512], BF16, hsb_off + 1024)
    A("stm", [32, 3072], F32, rslot_off + 0)
    A("mA4", [128, 512], F32, hsb_off + 22528)
    A("mA24", [128, 512], F32, hsb_off + 24576)
    A("mB2p4", [128, 512], F32, hsb_off + 26624)
    A("mX124", [128, 512], F32, hsb_off + 29440)
    A("mqd", [32, 256], BF16, hsb_off + 19456)
    A("kgcol", [128, 1], BF16, hsb_off + 19968)
    A("mblk", [128, 512], F32, hsb_off + 20480)
    S("lnbuf0", [128, 1024], BF16)
    lnscr_off = nc.sbuf_base
    S("lnscr", [128, 1024], F32)
    ocomb_off = nc.sbuf_base
    S("ocomb", [128, 1024], F32)
    A("obufb", [128, 1024], BF16, ocomb_off)
    A("obufb2", [128, 1024], BF16, ocomb_off + 2048)
    A("spA", [128, 512], BF16, ocomb_off + 0)
    A("spB", [128, 512], BF16, ocomb_off + 1024)
    A("mB2T4", [128, 512], F32, lnscr_off + 0)
    A("mD", [128, 128], F32, lnscr_off + 2048)
    S("probs_tm", [128, 512], F32)
    S("cols", [128, 24], F32)
    S("bis", [128, 8], F32)
    S("jtile", [128, 4], F32)
    S("postile", [128, 4], F32)
    S("positile", [128, 4], I32)
    S("stile", [128, 4], F32)
    S("mtile", [128, 4], F32)
    S("dmy", [1, 4], F32)
    S("lstrictb", [128, 128], BF16)
    S("lstrictf", [128, 128], F32)
    S("identbf", [128, 128], BF16)
    S("onescolb", [128, 1], BF16)
    S("onescolf", [128, 1], F32)
    S("onesrowb", [1, 128], BF16)
    S("onesrowf", [1, 128], F32)
    S("onesff", [128, 128], F32)
    S("b2sb", [1, 1024], BF16)
    S("wsb", [128, 64], BF16)
    S("b1sb", [128, 32], F32)
    S("bscol", [8, 1], F32)
    S("thcols", [128, 8], F32)
    S("throw", [1, 8], F32)
    S("ehot", [128, 8], F32)

    ps = {}
    for nm in ["pA", "pB", "pC", "pD", "pE", "pF", "pG"]:
        ps[nm] = stack.enter_context(nc.psum_tensor(nm, [128, 512], F32))
    ps["pT"] = stack.enter_context(nc.psum_tensor("pT", [128, 256], BF16))

    s = {}
    s.update(sb)
    s.update(ps)
    s.update({k: v for k, v in dr.items()})
    s["_tmp"] = {}
    s["xbuf"] = [sb["xbuf0"], sb["xbuf1"]]
    s["xcomb"] = s["xbuf"]
    s["plane"] = sb["lnscr"][:, 0:512]
    s["gidsel"] = sb["ocomb"][:, 0:512]
    s["obuf"] = sb["ocomb"]
    s["zbig"] = sb["lnbuf0"]
    s["lnbuf"] = [sb["lnbuf0"], sb["lnbuf0"]]
    s["gbuf"] = [sb["gbuf0"], sb["gbuf1"]]
    s["xfm"] = [sb["xfm0"], sb["xfm1"]]
    s["lnfm"] = [sb["lnfm0"], sb["lnfm1"]]
    s["rslot"] = [sb["rslot0"], sb["rslot1"]]
    s["xcomb2"] = [sb["lnscr"], sb["xcomb21"]]
    s["PT"] = [sb["PT0"], sb["PT1"]]

    P = Planner(nc)

    P.op("gpsimd", lambda e: e.dma_start(s["w1sb"][:, :], w1t[:, :]), sem="initg", dma=True)
    P.op("gpsimd", lambda e: e.dma_start(s["w2sb"][:, :], w2t[:, :]), sem="initg", dma=True)
    P.op("gpsimd", lambda e: e.dma_start(s["wsb"][:, :], wst[:, :]), sem="initg", dma=True)
    P.op("gpsimd", lambda e: e.dma_start(s["b2sb"][:, :], b2r[:, :]), sem="initg", dma=True)
    P.op("sync", lambda e: e.dma_start(s["b1sb"][:, :], b1t[:, :]), sem="init", dma=True)
    P.op("sync", lambda e: e.dma_start(s["ehot"][:, :], ehotI[:, :]), sem="init", dma=True)
    P.op("sync", lambda e: e.dma_start(s["bscol"][:, :], bsI[:, :]), sem="init", dma=True)
    cmap = [("c_lstrictb", "lstrictb"), ("c_lstrictf", "lstrictf"), ("c_identbf", "identbf"),
            ("c_onescolb", "onescolb"), ("c_onescolf", "onescolf"), ("c_onesrowb", "onesrowb"),
            ("c_onesrowf", "onesrowf"), ("c_onesff", "onesff")]
    for cname, sbname in cmap:
        P.op("sync", lambda e, cname=cname, sbname=sbname: e.dma_start(s[sbname][:, :], consts[cname][:, :]),
             sem="init", dma=True)
    for cn in ["c_gid", "c_zrow", "c_piog", "c_iotaslot", "c_keslot3", "c_eoff3", "c_zrowcol", "c_emap",
               "c_mA4", "c_mA24", "c_mB2p4", "c_mX124", "c_mB2T4", "c_mD", "c_mqd", "c_mblk", "c_kgcol"]:
        s[cn] = consts[cn]
    s["wqt"], s["wkt"], s["wvt"], s["wot"] = wqt, wkt, wvt, wot
    P.gate_engines(P.tok("init"))
    P.gate_engines(P.tok("initg"))

    emit_ffn(P, s, x0.ap(), dr["X1"].ap())
    f1_done = P.tok("F_xow")

    att_done = emit_att(P, s, f1_done, dr)
    P.gate_engines(att_done)

    emit_ffn(P, s, dr["X2"].ap(), xout.ap())

    P.op("vector", lambda e: e.tensor_copy(s["dmy"][0:1, 1:2], s["dmy"][0:1, 0:1]),
         waits=[P.tok("F_xow")], sem="fin")
    P.finalize_gates()
    return P, stack


def host_constants():
    c = {}
    jj = np.arange(CH, dtype=np.float64)
    J, I = np.meshgrid(jj, jj, indexing="ij")
    le = (J <= I).astype(np.float64)
    lt = (I < J).astype(np.float64)
    g = float(GAMMA)
    c["c_mA4"] = np.tile((g ** (-J) * le).astype(np.float32), (1, 4))
    c["c_mA24"] = np.tile((-(g ** (-I)) * le).astype(np.float32), (1, 4))
    c["c_mB2p4"] = np.tile(lt.astype(np.float32), (1, 4))
    c["c_mX124"] = np.tile(((g ** (2 * I + 1 - J) - g ** (I + 1)) * le).astype(np.float32), (1, 4))
    U2, J2 = np.meshgrid(jj, jj, indexing="ij")
    c["c_mB2T4"] = np.tile((g ** (CH - 1 - U2) * (U2 < J2)).astype(np.float32), (1, 4))
    c["c_mD"] = (g ** (2 * I - J) * le).astype(np.float32)
    mqd = np.zeros((32, 2 * CH), np.float64)
    mqd[:, 0:CH] = g ** (jj + 1)
    mqd[:, CH:2 * CH] = g ** (2 * jj + 2)
    c["c_mqd"] = mqd.astype(np.float32).astype(bfdt)
    pp = np.arange(128)[:, None] // 32
    cc2 = np.arange(512)[None, :] // 128
    c["c_mblk"] = (pp == cc2).astype(np.float32)
    c["c_kgcol"] = (g ** (CH - 1 - jj)).reshape(128, 1).astype(np.float32).astype(bfdt)
    c["c_lstrictb"] = np.triu(np.ones((128, 128), np.float32), 1).astype(bfdt)
    c["c_lstrictf"] = np.triu(np.ones((128, 128), np.float32), 1)
    c["c_identbf"] = np.eye(128, dtype=np.float32).astype(bfdt)
    c["c_onescolb"] = np.ones((128, 1), np.float32).astype(bfdt)
    c["c_onescolf"] = np.ones((128, 1), np.float32)
    c["c_onesrowb"] = np.ones((1, 128), np.float32).astype(bfdt)
    c["c_onesrowf"] = np.ones((1, 128), np.float32)
    c["c_onesff"] = np.ones((128, 128), np.float32)
    c["c_iotaslot"] = np.broadcast_to(np.arange(SLOTC, dtype=np.float32), (128, SLOTC)).copy()
    p = np.arange(128, dtype=np.float32)[:, None]
    a = np.arange(512, dtype=np.float32)[None, :]
    c["c_gid"] = (a * 128 + p).astype(np.float32)
    g64 = np.arange(64, dtype=np.float32)[None, :]
    c["c_piog"] = (p * 64 + g64).astype(np.float32)
    sl = np.arange(128)[:, None] + np.arange(3)[None, :] * 128
    c["c_keslot3"] = (sl % SLOTC).astype(np.float32)
    es = (sl // SLOTC).astype(np.float32)
    c["c_eoff3"] = np.repeat((es * CO_CH)[:, :, None], 32, axis=2).reshape(128, 96).astype(np.float32)
    c["c_zrowcol"] = np.full((128, 96), float(ZROW), np.float32)
    em = np.zeros((8, 384), np.float32)
    for sgl in range(384):
        em[sgl // SLOTC, sgl] = 1.0
    c["c_emap"] = em
    c["c_zrow"] = np.zeros((1, 512), np.float32)
    return c


_PROG = {}


def get_prog():
    if "nc" not in _PROG:
        nc = bass.Bass("TRN2", target_bir_lowering=False, num_devices=NCORE)
        P, stack = build(nc)
        import contextlib
        semstack = contextlib.ExitStack()
        handles = {}
        for name in P.cnt.keys():
            handles[name] = semstack.enter_context(nc.semaphore(name))
        with nc.Block() as block:
            P.replay(block, handles)
        semstack.close()
        stack.close()
        _PROG["nc"] = nc
    return _PROG["nc"]


def kernel(**inputs):
    x = np.ascontiguousarray(np.asarray(inputs["L0_attn_x"], np.float32).reshape(N, D))
    W1 = np.asarray(inputs["L0_ffn_W1"], np.float32)
    W2 = np.asarray(inputs["L0_ffn_W2"], np.float32)
    b1 = np.asarray(inputs["L0_ffn_b1"], np.float32)
    b2 = np.asarray(inputs["L0_ffn_b2"], np.float32)
    Ws = np.ascontiguousarray(np.asarray(inputs["L0_ffn_Ws"], np.float32))
    bs = np.asarray(inputs["L0_ffn_bs"], np.float32)

    Wq = np.asarray(inputs["L0_attn_Wq"], np.float32)
    Wk = np.asarray(inputs["L0_attn_Wk"], np.float32)
    Wv = np.asarray(inputs["L0_attn_Wv"], np.float32)
    Wo = np.asarray(inputs["L0_attn_Wo"], np.float32)

    nc = get_prog()
    consts = host_constants()
    wst = Ws.reshape(8, 128, 8).transpose(1, 0, 2).reshape(128, 64)
    wqt = np.ascontiguousarray(Wq.reshape(8, 128, 512).transpose(1, 0, 2).reshape(128, 4096))
    wkt = np.ascontiguousarray(Wk.reshape(8, 128, 512).transpose(1, 0, 2).reshape(128, 4096))
    wvt = np.ascontiguousarray(Wv.reshape(8, 128, 512).transpose(1, 0, 2).reshape(128, 4096))
    wot_ = np.ascontiguousarray(Wo.reshape(4, 128, 1024).transpose(1, 0, 2).reshape(128, 4096))

    in_maps = []
    for c in range(NCORE):
        m = {
            "x0": np.ascontiguousarray(x[c * NT:(c + 1) * NT]),
            "w1t": np.ascontiguousarray(W1[c].reshape(8, 128, FF).transpose(1, 0, 2).reshape(128, 8 * FF)),
            "w2t": np.ascontiguousarray(W2[c].reshape(32, 128, D).transpose(1, 0, 2).reshape(128, 32 * D)),
            "b1t": np.ascontiguousarray(b1[c].reshape(32, 128).T),
            "b2r": np.ascontiguousarray(b2[c].reshape(1, D)),
            "wst": np.ascontiguousarray(wst),
            "bsI": np.ascontiguousarray(bs.reshape(8, 1)),
            "ehotI": np.broadcast_to(np.eye(8, dtype=np.float32)[c], (128, 8)).copy(),
            "wqt": wqt, "wkt": wkt, "wvt": wvt, "wot": wot_,
        }
        m.update(consts)
        in_maps.append(m)

    res = run_bass_kernel_spmd(nc, in_maps, core_ids=list(range(NCORE)))
    out = np.concatenate([res.results[c]["xout"] for c in range(NCORE)], axis=0)
    return out.reshape(B_SHAPE)


B_SHAPE = (16, 4096, 1024)
